# version 01
# baseline (speedup 1.0000x reference)
import numpy as np
import ml_dtypes
import concourse.bass as bass
import concourse.mybir as mybir
from concourse.bass_utils import run_bass_kernel_spmd

F32 = mybir.dt.float32
BF16 = mybir.dt.bfloat16
F16 = mybir.dt.float16
I32 = mybir.dt.int32
AF = mybir.ActivationFunctionType
ALU = mybir.AluOpType
AX = mybir.AxisListType
BF = ml_dtypes.bfloat16

B_SEQ, T_SEQ, DM = 4, 2048, 4096
H0, HD0 = 32, 128
NKC = DM // 128
P_SPLIT = 1152
EPS = 1e-5

PARAMS = {
    0: dict(kvW=1280, q0=0,    qW=1280, own0=0,      ownW=1152),
    1: dict(kvW=2048, q0=1024, qW=1024, own0=1152,   ownW=896),
}

def tiles_of(width, step=512):
    out = []
    c = 0
    while c < width:
        w = min(step, width - c)
        out.append((c, w))
        c += w
    return out


def pack_lhsT(wT):
    IN, OUT = wT.shape
    return np.ascontiguousarray(
        wT.reshape(IN // 128, 128, OUT // 128, 128).transpose(2, 1, 0, 3).reshape(OUT // 128, 128, IN))


def host_prep(inp):
    g = {}
    x = np.ascontiguousarray(np.asarray(inp["L0_attn_x"]))
    w_attn = np.asarray(inp["L0_attn_w_attn"])
    g["wqkv_img"] = pack_lhsT(np.ascontiguousarray(w_attn.T))
    wte = np.asarray(inp["L0_attn_adapter_wte"])
    g["wte_img"] = np.ascontiguousarray(wte.reshape(10, NKC, 128).transpose(2, 1, 0).reshape(128, NKC * 10))
    g["gating"] = np.asarray(inp["L0_attn_gating"]).reshape(1, 1)

    theta = 1.0 / 10000.0 ** (np.arange(0, HD0, 2) / HD0)
    idx = np.outer(np.arange(T_SEQ), theta)
    idx2 = np.concatenate([idx, idx], axis=-1)
    cos2 = np.cos(idx2).astype(np.float32)
    sin2 = np.sin(idx2).astype(np.float32)
    ss = sin2.copy()
    ss[:, :64] *= -1.0
    g["cos_fm"] = np.ascontiguousarray(cos2.T)
    ssT = ss.T
    g["ssin_fm"] = np.ascontiguousarray(np.concatenate([ssT[64:], ssT[:64]], axis=0))

    g["w_proj_img"] = pack_lhsT(np.ascontiguousarray(np.asarray(inp["L0_attn_w_proj"]).T))
    qw = np.asarray(inp["L0_ffn_qweight"])
    g["qw_img"] = pack_lhsT(np.ascontiguousarray(qw.T))
    scales = np.asarray(inp["L0_ffn_scales"]).reshape(DM, 16)
    zeros = np.asarray(inp["L0_ffn_zeros"]).reshape(DM, 16)
    oi = np.arange(DM)
    kc_i = oi // 128
    o_i = oi % 128
    s_img = np.empty((32, DM), np.float32)
    z_img = np.empty((32, DM), np.float32)
    for oc in range(32):
        s_img[oc] = scales[oc * 128 + o_i, kc_i // 2]
        z_img[oc] = zeros[oc * 128 + o_i, kc_i // 2]
    g["s_img"] = s_img
    g["z_img"] = z_img
    g["z16"] = np.ascontiguousarray(zeros.T)
    g["s16"] = np.ascontiguousarray(scales.T)
    gim = np.zeros((128, NKC * 16), np.float32)
    for kc in range(NKC):
        gim[:, kc * 16 + kc // 2] = 1.0
    g["gimg"] = gim
    g["s_rep"] = np.ascontiguousarray(np.broadcast_to(s_img[:, None, :], (32, 128, DM)))
    g["z_rep_row"] = z_img
    g["bias0"] = np.asarray(inp["L0_ffn_bias"]).reshape(32, 128).T.copy()

    cw = np.asarray(inp["L1_attn_conv_w"])
    g["conv_img"] = np.ascontiguousarray(
        cw.reshape(3, NKC, 128, 24, 128).transpose(3, 2, 0, 1, 4).reshape(24, 128, 3 * NKC * 128))
    g["conv_b"] = np.asarray(inp["L1_attn_conv_b"]).reshape(24, 128).T.copy()
    wq = np.asarray(inp["L1_attn_wq"]).reshape(1024, 1024)
    wk = np.asarray(inp["L1_attn_wk"]).reshape(1024, 1024)
    wv = np.asarray(inp["L1_attn_wv"]).reshape(1024, 1024)
    g["wq_img"] = pack_lhsT(wq)
    g["wk_img"] = pack_lhsT(wk)
    g["wv_img"] = pack_lhsT(wv)
    g["bq"] = np.asarray(inp["L1_attn_bq"]).reshape(1024).reshape(8, 128).T.copy()
    g["bk"] = np.asarray(inp["L1_attn_bk"]).reshape(1024).reshape(8, 128).T.copy()
    g["bv_col"] = np.asarray(inp["L1_attn_bv"]).reshape(1024).reshape(8, 128).T.copy()
    g["wo_img"] = pack_lhsT(np.asarray(inp["L1_attn_wo"]).reshape(1024, 1024))
    g["bo"] = np.asarray(inp["L1_attn_bo"]).reshape(8, 128).T.copy()
    g["projw_img"] = pack_lhsT(np.asarray(inp["L1_attn_proj_w"]))
    g["projb"] = np.asarray(inp["L1_attn_proj_b"]).reshape(32, 128).T.copy()

    g["wbase_img"] = pack_lhsT(np.ascontiguousarray(np.asarray(inp["L1_ffn_W_base"]).T))
    g["bbase"] = np.asarray(inp["L1_ffn_b_base"]).reshape(32, 128).T.copy()
    wr = np.asarray(inp["L1_ffn_W_router"])
    g["wr_img"] = np.ascontiguousarray(wr.reshape(8, NKC, 128).transpose(2, 1, 0).reshape(128, NKC * 8))
    g["br"] = np.asarray(inp["L1_ffn_b_router"]).reshape(8, 1)
    la = np.asarray(inp["L1_ffn_lora_A"])
    A_all = np.ascontiguousarray(la.transpose(1, 0, 2).reshape(DM, 128))
    g["A_img"] = pack_lhsT(A_all)[0]
    g["B_sb"] = np.ascontiguousarray(np.asarray(inp["L1_ffn_lora_B"]).reshape(128, DM))
    rep = np.zeros((8, 128), np.float32)
    for e in range(8):
        rep[e, e * 16:(e + 1) * 16] = 1.0
    g["rep"] = rep

    cores = []
    for c in range(8):
        par = c % 2
        P = PARAMS[par]
        seq = c // 2
        xs = np.ascontiguousarray(x[seq].T)
        d = dict(parity=par, seq=seq)
        d["xfm_kv"] = np.ascontiguousarray(xs[:, :P["kvW"]])
        d["xfm_q"] = np.ascontiguousarray(xs[:, P["q0"]:P["q0"] + P["qW"]])
        d["cos"] = np.ascontiguousarray(g["cos_fm"][:, :P["kvW"]])
        d["ssin"] = np.ascontiguousarray(g["ssin_fm"][:, :P["kvW"]])
        cores.append(d)
    return g, cores


ENGINES = ("tensor", "vector", "scalar", "gpsimd", "sync")


class Prog:

    def __init__(self, nc):
        self.nc = nc
        self.ops = {e: [] for e in ENGINES}
        self.sems = {}
        self.counts = {}
        self.hwm = {e: {} for e in ENGINES}

    def make_sems(self):
        for e in ENGINES:
            self._mk(e)

    def _mk(self, key):
        if key not in self.sems:
            self.sems[key] = self.nc.semaphore(f"s_{key}").__enter__()
            self.counts[key] = 0
        return key

    def emit(self, eng, fn, waits=(), inc=False, dma=None):
        key = self._mk("dma_" + dma) if dma else eng
        tok = None
        if inc:
            self.counts[key] += 16 if dma else 1
            tok = (key, self.counts[key])
        elif dma:
            raise AssertionError("every DMA needs inc=True (sync info required)")
        wl = []
        for t in waits:
            if t is None:
                continue
            (sk, v) = t
            if not sk.startswith("dma_") and sk == eng:
                continue
            h = self.hwm[eng].get(sk, 0)
            if v > h:
                wl.append((sk, v))
                self.hwm[eng][sk] = v
        self.ops[eng].append((fn, wl, tok))
        return tok

    def chan_total(self, ch):
        return ("dma_" + ch, self.counts.get("dma_" + ch, 0))

    def finalize(self, block, final_waits=()):
        for ename in ENGINES:
            oplist = self.ops[ename]
            sems = self.sems

            def mk(ename, oplist):
                def body(eng):
                    for (fn, wl, tok) in oplist:
                        for (sk, v) in wl:
                            eng.wait_ge(sems[sk], v)
                        ins = fn(eng)
                        if tok is not None:
                            ins.then_inc(sems[tok[0]], 16 if tok[0].startswith("dma_") else 1)
                    for t in final_waits:
                        if t is not None:
                            eng.wait_ge(sems[t[0]], t[1])
                return body
            getattr(block, ename)(mk(ename, oplist))


class PsumPool:

    def __init__(self, prog, banks):
        self.prog = prog
        self.banks = banks
        self.tokens = [None] * len(banks)
        self.ptr = 0

    def acquire(self, n):
        out = []
        for _ in range(n):
            i = self.ptr % len(self.banks)
            self.ptr += 1
            out.append((i, self.banks[i], self.tokens[i]))
        return out

    def release(self, i, token):
        self.tokens[i] = token



def emit_ln(p, pool, xq, xq_cols, kvW, a_m, b_m, sq_scr, rowbuf, ones_col, ones_row, load_tok, col0=0):
    nc = p.nc
    toks = []
    for (t0, tw) in tiles_of(kvW):
        bk = pool.acquire(4)
        (i_sx, ps_sx, w1), (i_sq, ps_sq, w2), (i_a, ps_a, w3), (i_b, ps_b, w4) = bk
        mm_sx = None
        mmsq_toks = {}
        for kc in range(NKC):
            xsl = xq[:, kc * xq_cols + col0 + t0: kc * xq_cols + col0 + t0 + tw]
            half = (kc % 2) * 512
            sqv = sq_scr[:, half:half + tw]
            sq_tok = p.emit("scalar", lambda s, sqv=sqv, xsl=xsl: s.activation(sqv, xsl, AF.Square),
                            waits=[load_tok, mmsq_toks.get(kc - 2)], inc=True)
            mm_sx = p.emit("tensor", lambda t, xsl=xsl, ps=ps_sx, tw=tw, kc=kc:
                           t.matmul(ps[0:1, 0:tw], ones_col[:, 0:1], xsl, start=(kc == 0), stop=(kc == NKC - 1)),
                           waits=[load_tok, w1 if kc == 0 else None], inc=(kc == NKC - 1))
            mmsq_toks[kc] = p.emit("tensor", lambda t, sqv=sqv, ps=ps_sq, tw=tw, kc=kc:
                           t.matmul(ps[0:1, 0:tw], ones_col[:, 0:1], sqv, start=(kc == 0), stop=(kc == NKC - 1)),
                           waits=[sq_tok, w2 if kc == 0 else None], inc=True)
        mm_sq = mmsq_toks[NKC - 1]
        Rs = [rowbuf[0:1, j * 512: j * 512 + tw] for j in range(6)]
        t_mean = p.emit("vector", lambda v, o=Rs[0], i=ps_sx[0:1, 0:tw]: v.tensor_scalar_mul(o, i, 1.0 / DM),
                        waits=[mm_sx], inc=True)
        t_var = p.emit("vector", lambda v, o=Rs[1], i=ps_sq[0:1, 0:tw]: v.tensor_scalar_mul(o, i, 1.0 / DM),
                       waits=[mm_sq], inc=True)
        p.emit("vector", lambda v, o=Rs[4], i=Rs[0]: v.tensor_mul(o, i, i), waits=[t_mean])
        p.emit("vector", lambda v, o=Rs[1], i=Rs[4]: v.tensor_sub(o, o, i), waits=[t_var])
        t_ve = p.emit("vector", lambda v, o=Rs[1]: v.tensor_scalar_add(o, o, EPS), inc=True)
        t_sd = p.emit("scalar", lambda s, o=Rs[5], i=Rs[1]: s.activation(o, i, AF.Sqrt), inc=True,
                      waits=[t_ve])
        t_a = p.emit("vector", lambda v, o=Rs[2], i=Rs[5]: v.reciprocal(o, i), waits=[t_sd], inc=True)
        t_b = p.emit("vector", lambda v, o=Rs[3], i=Rs[0], j=Rs[2]: v.tensor_mul(o, i, j), waits=[t_a], inc=True)
        bca = p.emit("tensor", lambda t, o=ps_a[:, 0:tw], r=Rs[2]: t.matmul(o, ones_row[0:1, 0:128], r,
                                                  start=True, stop=True), waits=[t_a, w3], inc=True)
        bcb = p.emit("tensor", lambda t, o=ps_b[:, 0:tw], r=Rs[3]: t.matmul(o, ones_row[0:1, 0:128], r,
                                                  start=True, stop=True), waits=[t_b, w4], inc=True)
        ea = p.emit("vector", lambda v, o=a_m[:, t0:t0 + tw], i=ps_a[:, 0:tw]: v.tensor_copy(o, i),
                    waits=[bca], inc=True)
        eb = p.emit("vector", lambda v, o=b_m[:, t0:t0 + tw], i=ps_b[:, 0:tw]: v.tensor_copy(o, i),
                    waits=[bcb], inc=True)
        pool.release(i_sx, t_mean)
        pool.release(i_sq, t_var)
        pool.release(i_a, ea)
        pool.release(i_b, eb)
    for kc in range(NKC):
        xsl = xq[:, kc * xq_cols + col0: kc * xq_cols + col0 + kvW]
        p.emit("vector", lambda v, xsl=xsl: v.tensor_mul(xsl, xsl, a_m[:, 0:kvW]), waits=[ea])
        tk = p.emit("vector", lambda v, xsl=xsl: v.tensor_sub(xsl, xsl, b_m[:, 0:kvW]), inc=True)
        toks.append(tk)
    return toks


def build_s1(parity, qkv_enable=True):
    P = PARAMS[parity]
    kvW, q0, qW = P["kvW"], P["q0"], P["qW"]
    xq_cols = kvW + 16
    nc = bass.Bass("TRN2", target_bir_lowering=False)

    xfm = nc.dram_tensor("xfm_kv", [DM, kvW], F32, kind="ExternalInput").ap()
    wqkv = nc.dram_tensor("wqkv_img", [96, 128, DM], F32, kind="ExternalInput").ap()
    wte = nc.dram_tensor("wte_img", [128, NKC * 10], F32, kind="ExternalInput").ap()
    cosd = nc.dram_tensor("cos", [128, kvW], F32, kind="ExternalInput").ap()
    ssind = nc.dram_tensor("ssin", [128, kvW], F32, kind="ExternalInput").ap()
    q_out = nc.dram_tensor("q_out", [32, 128, qW], BF16, kind="ExternalOutput").ap()
    k_out = nc.dram_tensor("k_out", [32, 128, kvW + 10], BF16, kind="ExternalOutput").ap()
    v_out = nc.dram_tensor("v_out", [32, 128, kvW + 10], BF16, kind="ExternalOutput").ap()
    xn_out = nc.dram_tensor("xn_own", [NKC, 128, qW], BF16, kind="ExternalOutput").ap()

    from contextlib import ExitStack
    with ExitStack() as stk:
        sb = lambda name, shape, dt: stk.enter_context(nc.sbuf_tensor(name, shape, dt))
        xq = sb("xq", [128, NKC * xq_cols], BF16)
        wb0 = sb("wb0", [128, DM], BF16)
        wb1 = sb("wb1", [128, DM], BF16)
        wteb = sb("wteb", [128, NKC * 10], BF16)
        cosb = sb("cosb", [128, kvW], BF16)
        ssinb = sb("ssinb", [128, kvW], BF16)
        am = sb("am", [128, kvW], BF16)
        bm = sb("bm", [128, kvW], BF16)
        sq_scr = sb("sq", [128, 1024], BF16)
        rowbuf = sb("rows", [1, 6 * 512], F32)
        ones_col = sb("onec", [128, 1], BF16)
        ones_row = sb("oner", [1, 128], F32)
        ropet = sb("ropet", [128, kvW], BF16)
        st0 = sb("st0", [128, kvW + 16], BF16)
        st1 = sb("st1", [128, kvW + 16], BF16)
        sr0 = sb("sr0", [128, kvW + 16], BF16)
        sr1 = sb("sr1", [128, kvW + 16], BF16)
        ppool = [stk.enter_context(nc.psum_tensor(f"pp{i}", [128, 512], F32)) for i in range(8)]
        block = stk.enter_context(nc.Block())
        p = Prog(nc)
        p.make_sems()
        pool = PsumPool(p, ppool)

        p.emit("vector", lambda v: v.memset(ones_col[:, :], 1.0))
        c_ones = p.emit("vector", lambda v: v.memset(ones_row[:, :], 1.0), inc=True)

        xtoks = []
        for kc in range(NKC):
            tk = p.emit("gpsimd", lambda g, kc=kc: g.dma_start(
                out=xq[:, kc * xq_cols: kc * xq_cols + kvW], in_=xfm[kc * 128:(kc + 1) * 128, :]),
                inc=True, dma="xin")
            xtoks.append(tk)
        t_wte = p.emit("gpsimd", lambda g: g.dma_start(out=wteb[:, :], in_=wte), inc=True, dma="xin")
        t_cos = p.emit("gpsimd", lambda g: g.dma_start(out=cosb[:, :], in_=cosd), inc=True, dma="xin")
        t_ssin = p.emit("gpsimd", lambda g: g.dma_start(out=ssinb[:, :], in_=ssind), inc=True, dma="xin")
        xin_all = p.chan_total("xin")
        t_wte = t_cos = t_ssin = xin_all
        wte_toks = []
        for kc in range(NKC):
            tk = p.emit("vector", lambda v, kc=kc: v.tensor_copy(
                xq[:, kc * xq_cols + kvW: kc * xq_cols + kvW + 10],
                wteb[:, kc * 10:(kc + 1) * 10]), waits=[t_wte], inc=(kc == NKC - 1))
            wte_toks.append(tk)

        ln_toks = emit_ln(p, pool, xq, xq_cols, kvW, am, bm, sq_scr, rowbuf,
                          ones_col, ones_row, xin_all)
        ln_done = ln_toks[-1]
        wte_done = wte_toks[-1]

        out_toks = []
        for kc in range(NKC):
            tk = p.emit("sync", lambda s, kc=kc: s.dma_start(
                out=xn_out[kc], in_=xq[:, kc * xq_cols + q0: kc * xq_cols + q0 + qW]),
                waits=[ln_toks[kc]], inc=True, dma="xout")
            out_toks.append(tk)

        wbufs = [wb0, wb1]
        stages = [(st0, sr0), (st1, sr1)]
        wload_tok = {}
        mm_last = {}
        raw_free = [None, None]
        rope_free = [None, None]
        for oc in (range(96) if qkv_enable else []):
            kind = "q" if oc < 32 else ("k" if oc < 64 else "v")
            head = oc % 32
            wbuf = wbufs[oc % 2]
            wload_tok[oc] = p.emit("gpsimd", lambda g, oc=oc, wbuf=wbuf: g.dma_start(
                out=wbuf[:, :], in_=wqkv[oc]), waits=[mm_last.get(oc - 2)], inc=True, dma="w%d" % (oc % 2))

            if kind == "q":
                col0, width = q0, qW
            else:
                col0, width = 0, kvW + 10
            tl = tiles_of(width)
            if len(tl) >= 2 and tl[-1][1] + tl[-2][1] <= 512:
                t0a, w0a = tl[-2]
                tl = tl[:-2] + [(t0a, w0a + tl[-1][1])]
            banks = pool.acquire(len(tl))
            stg_raw, stg_rope = stages[oc % 2]
            sfree = raw_free[oc % 2]
            last = None
            evac_done = None
            for ti, (t0, tw) in enumerate(tl):
                bi, ps, wtok = banks[ti]
                for kc in range(NKC):
                    rhs = xq[:, kc * xq_cols + col0 + t0: kc * xq_cols + col0 + t0 + tw]
                    w_first = [wload_tok[oc], ln_done, wte_done, wtok] if kc == 0 else ()
                    last = p.emit("tensor", lambda t, ps=ps, wbuf=wbuf, kc=kc, rhs=rhs, tw=tw:
                                  t.matmul(ps[:, 0:tw], wbuf[:, kc * 128:(kc + 1) * 128], rhs,
                                           start=(kc == 0), stop=(kc == NKC - 1)),
                                  waits=w_first, inc=(kc == NKC - 1))
                etok = p.emit("scalar", lambda s, ps=ps, t0=t0, tw=tw, stg_raw=stg_raw:
                              s.copy(stg_raw[:, t0:t0 + tw], ps[:, 0:tw]),
                              waits=[last, sfree], inc=True)
                pool.release(bi, etok)
                evac_done = etok
            mm_last[oc] = last

            if kind == "v":
                dtok = p.emit("sync", lambda s, head=head, stg_raw=stg_raw, width=width: s.dma_start(
                    out=v_out[head], in_=stg_raw[:, 0:width]), waits=[evac_done], inc=True, dma="st%d" % (oc % 2))
                raw_free[oc % 2] = dtok
                out_toks.append(dtok)
            else:
                wW = qW if kind == "q" else kvW
                cs0 = q0 if kind == "q" else 0
                coss = cosb[:, cs0:cs0 + wW]
                ssins = ssinb[:, cs0:cs0 + wW]
                rfree = rope_free[oc % 2]
                p.emit("vector", lambda v, sr=stg_rope, st=stg_raw, coss=coss, wW=wW:
                       v.tensor_mul(sr[:, 0:wW], st[:, 0:wW], coss),
                       waits=[evac_done, t_cos, t_ssin, rfree])
                p.emit("vector", lambda v, st=stg_raw, ssins=ssins, wW=wW:
                       v.tensor_mul(ropet[0:64, 0:wW], st[64:128, 0:wW], ssins[64:128, 0:wW]))
                p.emit("vector", lambda v, st=stg_raw, ssins=ssins, wW=wW:
                       v.tensor_mul(ropet[64:128, 0:wW], st[0:64, 0:wW], ssins[0:64, 0:wW]))
                if kind == "k":
                    p.emit("vector", lambda v, sr=stg_rope, st=stg_raw:
                           v.tensor_copy(sr[:, kvW:kvW + 10], st[:, kvW:kvW + 10]))
                rtok = p.emit("vector", lambda v, sr=stg_rope, wW=wW:
                              v.tensor_add(sr[:, 0:wW], sr[:, 0:wW], ropet[:, 0:wW]), inc=True)
                raw_free[oc % 2] = rtok
                dwidth = qW if kind == "q" else kvW + 10
                dst = q_out[head] if kind == "q" else k_out[head]
                dtok = p.emit("sync", lambda s, dst=dst, sr=stg_rope, dwidth=dwidth: s.dma_start(
                    out=dst, in_=sr[:, 0:dwidth]), waits=[rtok], inc=True, dma="st%d" % (oc % 2))
                rope_free[oc % 2] = dtok
                out_toks.append(dtok)

        p.finalize(block, final_waits=[p.chan_total("xout"), p.chan_total("st0"), p.chan_total("st1")])
    return nc


def build_s2(parity):
    P = PARAMS[parity]
    kvW, q0, qW = P["kvW"], P["q0"], P["qW"]
    KT = kvW // 128
    nc = bass.Bass("TRN2", target_bir_lowering=False)

    q_in = nc.dram_tensor("q_out", [32, 128, qW], BF16, kind="ExternalInput").ap()
    k_in = nc.dram_tensor("k_out", [32, 128, kvW + 10], BF16, kind="ExternalInput").ap()
    v_in = nc.dram_tensor("v_out", [32, 128, kvW + 10], BF16, kind="ExternalInput").ap()
    gat = nc.dram_tensor("gating", [1, 1], F32, kind="ExternalInput").ap()
    ident = nc.dram_tensor("ident", [128, 128], BF16, kind="ExternalInput").ap()
    y_out = nc.dram_tensor("y_out", [32, 128, qW], BF16, kind="ExternalOutput").ap()

    from contextlib import ExitStack
    with ExitStack() as stk:
        sb = lambda name, shape, dt: stk.enter_context(nc.sbuf_tensor(name, shape, dt))
        kb = [sb("kb0", [128, kvW + 10], BF16), sb("kb1", [128, kvW + 10], BF16)]
        vb = [sb("vb0", [128, kvW + 10], BF16), sb("vb1", [128, kvW + 10], BF16)]
        qb = [sb("qb0", [128, qW], BF16), sb("qb1", [128, qW], BF16)]
        vtm = [sb("vtm0", [128, kvW], BF16), sb("vtm1", [128, kvW], BF16)]
        avtm = [sb("avtm0", [10, 128], BF16), sb("avtm1", [10, 128], BF16)]
        expt = sb("expt", [128, KT * 512], BF16)
        expa = sb("expa", [10, 512], BF16)
        rowbuf = sb("rows", [1, 1536], F32)
        smv = sb("smv", [10, 512], F32)
        rmv = sb("rmv", [128, 512], BF16)
        ystage = [sb("yst0", [128, qW], BF16), sb("yst1", [128, qW], BF16)]
        onec = sb("onec", [128, 1], BF16)
        oner = sb("oner", [1, 128], F32)
        gatb = sb("gatb", [1, 1], F32)
        idb = sb("idb", [128, 128], BF16)
        pool_rot = [stk.enter_context(nc.psum_tensor(f"pr{i}", [128, 512], F32)) for i in range(2)]
        pool_y = [stk.enter_context(nc.psum_tensor(f"py{i}", [128, 512], F32)) for i in range(2)]
        pool_sm = [stk.enter_context(nc.psum_tensor(f"pm{i}", [128, 512], F32)) for i in range(2)]
        ptrb = [stk.enter_context(nc.psum_tensor(f"pt{i}", [128, 1024], BF16)) for i in range(2)]
        block = stk.enter_context(nc.Block())

        p = Prog(nc)
        p.make_sems()
        rot = PsumPool(p, pool_rot)

        p.emit("vector", lambda v: v.memset(onec[:, :], 1.0))
        p.emit("vector", lambda v: v.memset(oner[:, :], 1.0), inc=True)
        p.emit("gpsimd", lambda g: g.dma_start(out=gatb[:, :], in_=gat), inc=True, dma="xin")
        p.emit("gpsimd", lambda g: g.dma_start(out=idb[:, :], in_=ident), inc=True, dma="xin")
        xin_all = p.chan_total("xin")

        SCALE_M = 1.0 / float(np.sqrt(128.0))
        SCALE_A = 1.0 / 64.0
        head_done = {}
        tr_tok = [None, None]
        avp_tok = [None, None]
        ybank_tok = [None, None]
        sm_tok = [None, None]
        ystage_free = [None, None]
        prev_qc_t = None
        prev_qc_g = None
        for h in range(32):
            par = h % 2
            hd2 = head_done.get(h - 2)
            ld_k = p.emit("sync", lambda s, h=h, b=kb[par]: s.dma_start(out=b[:, :], in_=k_in[h]),
                          waits=[hd2], inc=True, dma="kld%d" % par)
            ld_v = p.emit("sync", lambda s, h=h, b=vb[par]: s.dma_start(out=b[:, :], in_=v_in[h]),
                          waits=[hd2], inc=True, dma="vld%d" % par)
            ld_q = p.emit("sync", lambda s, h=h, b=qb[par]: s.dma_start(out=b[:, :], in_=q_in[h]),
                          waits=[hd2], inc=True, dma="qld%d" % par)
            for kt in range(KT):
                bk = kt % 2
                ttok = p.emit("tensor", lambda t, b=vb[par], kt=kt, o=ptrb[bk][:, 0:128]:
                              t.transpose(o, b[:, kt * 128:(kt + 1) * 128], idb[:, :]),
                              waits=[ld_v, xin_all, tr_tok[bk], hd2], inc=True)
                tr_tok[bk] = p.emit("vector", lambda v, o=vtm[par], kt=kt, i=ptrb[bk][:, 0:128]:
                                    v.tensor_copy(o[:, kt * 128:(kt + 1) * 128], i), waits=[ttok], inc=True)
            bk = KT % 2
            ttok = p.emit("tensor", lambda t, b=vb[par], o=ptrb[bk][0:10, 0:128]:
                          t.transpose(o, b[:, kvW:kvW + 10], idb[:, :]),
                          waits=[ld_v, tr_tok[bk], avp_tok[bk]], inc=True)
            av_ev = p.emit("vector", lambda v, o=avtm[par], i=ptrb[bk][0:10, 0:128]:
                           v.tensor_copy(o[:, :], i), waits=[ttok], inc=True)
            avp_tok[bk] = av_ev
            vt_done = av_ev

            last_pva = None
            yev = None
            for (qc0, qcW) in tiles_of(qW):
                qa0 = q0 + qc0
                qcp = (qc0 // 512) % 2
                kt_hi = (qa0 + qcW - 1) // 128 + 1
                smb = pool_sm[qcp]
                ybank = pool_y[qcp]
                sum_last = None
                for kt in range(kt_hi):
                    bi, ps, wtok = rot.acquire(1)[0]
                    mm = p.emit("tensor", lambda t, ps=ps, b=kb[par], kt=kt, q=qb[par], qc0=qc0, qcW=qcW:
                                t.matmul(ps[:, 0:qcW], b[:, kt * 128:(kt + 1) * 128], q[:, qc0:qc0 + qcW],
                                         start=True, stop=True),
                                waits=[ld_k, ld_q, wtok], inc=True)
                    etok = p.emit("scalar", lambda s, ps=ps, kt=kt, qcW=qcW:
                                  s.activation(expt[:, kt * 512: kt * 512 + qcW], ps[:, 0:qcW], AF.Exp,
                                               scale=SCALE_M),
                                  waits=[mm, prev_qc_t, prev_qc_g], inc=True)
                    rot.release(bi, etok)
                    if kt * 128 + 127 >= qa0:
                        etok = p.emit("gpsimd", lambda g, kt=kt, qcW=qcW, qa0=qa0:
                                      g.affine_select(expt[:, kt * 512: kt * 512 + qcW],
                                                      expt[:, kt * 512: kt * 512 + qcW],
                                                      [[1, qcW]], ALU.is_ge, 0.0,
                                                      base=qa0 - kt * 128, channel_multiplier=-1),
                                      waits=[etok], inc=True)
                        prev_qc_g = etok
                    sum_last = p.emit("tensor", lambda t, ps=smb, kt=kt, qcW=qcW, first=(kt == 0), last=(kt == kt_hi - 1):
                                      t.matmul(ps[0:1, 0:qcW], onec[:, 0:1], expt[:, kt * 512: kt * 512 + qcW],
                                               start=first, stop=last),
                                      waits=[etok, sm_tok[qcp] if kt == 0 else None], inc=(kt == kt_hi - 1))
                    p.emit("tensor", lambda t, ps=ybank, kt=kt, qcW=qcW, par=par, first=(kt == 0):
                           t.matmul(ps[:, 0:qcW], vtm[par][:, kt * 128:(kt + 1) * 128],
                                    expt[:, kt * 512: kt * 512 + qcW], start=first, stop=False),
                           waits=[vt_done, etok, ybank_tok[qcp] if kt == 0 else None])
                bi, psa, wtok = rot.acquire(1)[0]
                mma = p.emit("tensor", lambda t, ps=psa, b=kb[par], q=qb[par], qc0=qc0, qcW=qcW:
                             t.matmul(ps[0:10, 0:qcW], b[:, kvW:kvW + 10], q[:, qc0:qc0 + qcW],
                                      start=True, stop=True), waits=[wtok], inc=True)
                eta = p.emit("scalar", lambda s, ps=psa, qcW=qcW:
                             s.activation(expa[0:10, 0:qcW], ps[0:10, 0:qcW], AF.Exp, scale=SCALE_A),
                             waits=[mma, prev_qc_t], inc=True)
                rot.release(bi, eta)
                bi2, psas, wtok2 = rot.acquire(1)[0]
                asum = p.emit("tensor", lambda t, ps=psas, qcW=qcW:
                              t.matmul(ps[0:1, 0:qcW], onec[0:10, 0:1], expa[0:10, 0:qcW],
                                       start=True, stop=True), waits=[eta, wtok2], inc=True)
                r_as = p.emit("vector", lambda v, i=psas, qcW=qcW: v.reciprocal(rowbuf[0:1, 0:qcW], i[0:1, 0:qcW]),
                              waits=[asum, prev_qc_t], inc=True)
                rot.release(bi2, r_as)
                p.emit("vector", lambda v, i=smb, qcW=qcW:
                       v.tensor_mul(rowbuf[0:1, 512:512 + qcW], i[0:1, 0:qcW], rowbuf[0:1, 0:qcW]),
                       waits=[sum_last, xin_all])
                rS = p.emit("vector", lambda v, qcW=qcW:
                            v.tensor_scalar_mul(rowbuf[0:1, 512:512 + qcW], rowbuf[0:1, 512:512 + qcW], gatb[0:1, 0:1]),
                            inc=True)
                rR = p.emit("vector", lambda v, i=smb, qcW=qcW:
                            v.reciprocal(rowbuf[0:1, 1024:1024 + qcW], i[0:1, 0:qcW]),
                            waits=[sum_last], inc=True)
                sm_tok[qcp] = rR
                bi3, psm, wtok3 = rot.acquire(1)[0]
                mS = p.emit("tensor", lambda t, ps=psm, qcW=qcW:
                            t.matmul(ps[0:10, 0:qcW], oner[0:1, 0:10], rowbuf[0:1, 512:512 + qcW],
                                     start=True, stop=True), waits=[rS, wtok3], inc=True)
                eS = p.emit("vector", lambda v, i=psm, qcW=qcW: v.tensor_copy(smv[0:10, 0:qcW], i[0:10, 0:qcW]),
                            waits=[mS], inc=True)
                rot.release(bi3, eS)
                sc = p.emit("vector", lambda v, qcW=qcW:
                            v.tensor_mul(expa[0:10, 0:qcW], expa[0:10, 0:qcW], smv[0:10, 0:qcW]), inc=True)
                last_pva = p.emit("tensor", lambda t, ps=ybank, par=par, qcW=qcW:
                                  t.matmul(ps[:, 0:qcW], avtm[par][:, :], expa[0:10, 0:qcW],
                                           start=False, stop=True), waits=[av_ev, sc], inc=True)
                bi4, psr, wtok4 = rot.acquire(1)[0]
                mR = p.emit("tensor", lambda t, ps=psr, qcW=qcW:
                            t.matmul(ps[:, 0:qcW], oner[0:1, 0:128], rowbuf[0:1, 1024:1024 + qcW],
                                     start=True, stop=True), waits=[rR, wtok4], inc=True)
                eR = p.emit("vector", lambda v, i=psr, qcW=qcW: v.tensor_copy(rmv[:, 0:qcW], i[:, 0:qcW]),
                            waits=[mR], inc=True)
                rot.release(bi4, eR)
                yev = p.emit("vector", lambda v, ps=ybank, par=par, qc0=qc0, qcW=qcW:
                             v.tensor_mul(ystage[par][:, qc0:qc0 + qcW], ps[:, 0:qcW], rmv[:, 0:qcW]),
                             waits=[last_pva, eR, ystage_free[par]], inc=True)
                ybank_tok[qcp] = yev
                prev_qc_t = last_pva
            dtok = p.emit("sync", lambda s, h=h, par=par: s.dma_start(out=y_out[h], in_=ystage[par][:, :]),
                          waits=[yev], inc=True, dma="yo%d" % par)
            ystage_free[par] = dtok
            head_done[h] = last_pva
        p.finalize(block, final_waits=[p.chan_total("yo0"), p.chan_total("yo1")])
    return nc


def build_s3(parity):
    P = PARAMS[parity]
    q0, qW = P["q0"], P["qW"]
    nc = bass.Bass("TRN2", target_bir_lowering=False)

    y_in = nc.dram_tensor("y_out", [32, 128, qW], BF16, kind="ExternalInput").ap()
    xn_in = nc.dram_tensor("xn_own", [NKC, 128, qW], BF16, kind="ExternalInput").ap()
    wp_in = nc.dram_tensor("w_proj_img", [32, 128, DM], F32, kind="ExternalInput").ap()
    qw_in = nc.dram_tensor("qw_img", [32, 128, DM], I32, kind="ExternalInput").ap()
    sr_in = nc.dram_tensor("s_rep", [32, 128, DM], F32, kind="ExternalInput").ap()
    z16_in = nc.dram_tensor("z16", [16, DM], F32, kind="ExternalInput").ap()
    s16_in = nc.dram_tensor("s16", [16, DM], F32, kind="ExternalInput").ap()
    gim_in = nc.dram_tensor("gimg", [128, NKC * 16], F32, kind="ExternalInput").ap()
    b0_in = nc.dram_tensor("bias0", [128, 32], F32, kind="ExternalInput").ap()
    xq_in = nc.dram_tensor("xfm_q", [DM, qW], F32, kind="ExternalInput").ap()
    x1_out = nc.dram_tensor("x1", [NKC, 128, qW], F32, kind="ExternalOutput").ap()

    passes = [(0, 640), (640, qW - 640)]

    from contextlib import ExitStack
    with ExitStack() as stk:
        sb = lambda name, shape, dt: stk.enter_context(nc.sbuf_tensor(name, shape, dt))
        pW = max(w for (_, w) in passes)
        ysb = sb("ysb", [128, NKC * pW], BF16)
        xnsb = sb("xnsb", [128, NKC * pW], BF16)
        wpb = [sb("wpb0", [128, DM], BF16), sb("wpb1", [128, DM], BF16)]
        qwb = [sb("qwb0", [128, DM], F16), sb("qwb1", [128, DM], F16)]
        srb = [sb("srb0", [128, DM], F16), sb("srb1", [128, DM], F16)]
        wdq = [sb("wdq0", [128, DM], BF16), sb("wdq1", [128, DM], BF16)]
        z16b = sb("z16b", [16, DM], BF16)
        s16b = sb("s16b", [16, DM], BF16)
        zsneg = sb("zsneg", [16, DM], BF16)
        gimb = sb("gimb", [128, NKC * 16], BF16)
        gsum = sb("gsum", [16, qW], BF16)
        b0b = sb("b0b", [128, 32], F32)
        xts = [sb(f"xt{i}", [128, 512], F32) for i in range(4)]
        x1s = [sb(f"x1s{i}", [128, 512], F32) for i in range(4)]
        ppool = [stk.enter_context(nc.psum_tensor(f"pp{i}", [128, 512], F32)) for i in range(8)]
        block = stk.enter_context(nc.Block())

        p = Prog(nc)
        p.make_sems()
        pool = PsumPool(p, ppool)

        p.emit("gpsimd", lambda g: g.dma_start(out=z16b[:, :], in_=z16_in), inc=True, dma="tab")
        p.emit("gpsimd", lambda g: g.dma_start(out=s16b[:, :], in_=s16_in), inc=True, dma="tab")
        p.emit("gpsimd", lambda g: g.dma_start(out=gimb[:, :], in_=gim_in), inc=True, dma="tab")
        p.emit("gpsimd", lambda g: g.dma_start(out=b0b[:, :], in_=b0_in), inc=True, dma="tab")
        tab_all = p.chan_total("tab")
        p.emit("vector", lambda v: v.tensor_mul(zsneg[:, :], z16b[:, :], s16b[:, :]), waits=[tab_all])
        zs_tok = p.emit("vector", lambda v: v.tensor_scalar_mul(zsneg[:, :], zsneg[:, :], -1.0), inc=True)

        wl_tok = {}
        mm_last = {}
        dq_tok = {}
        xt_tok = [None] * 4
        x1_tok = [None] * 4
        slot = 0
        for pi, (p0, pw) in enumerate(passes):
            ych = "yin%d" % pi
            for kc in range(NKC):
                p.emit("sync", lambda s, kc=kc, p0=p0, pw=pw: s.dma_start(
                    out=ysb[:, kc * pW: kc * pW + pw], in_=y_in[kc][:, p0:p0 + pw]),
                    waits=[mm_last.get(("pass_end", pi - 1))], inc=True, dma=ych)
                p.emit("sync", lambda s, kc=kc, p0=p0, pw=pw: s.dma_start(
                    out=xnsb[:, kc * pW: kc * pW + pw], in_=xn_in[kc][:, p0:p0 + pw]),
                    inc=True, dma=ych)
            act_all = p.chan_total(ych)

            gs_toks = []
            for (t0, tw) in tiles_of(pw):
                bi, ps, wtok = pool.acquire(1)[0]
                last = None
                for kc in range(NKC):
                    last = p.emit("tensor", lambda t, ps=ps, kc=kc, t0=t0, tw=tw:
                                  t.matmul(ps[0:16, 0:tw], gimb[:, kc * 16:(kc + 1) * 16],
                                           xnsb[:, kc * pW + t0: kc * pW + t0 + tw],
                                           start=(kc == 0), stop=(kc == NKC - 1)),
                                  waits=[act_all, tab_all, wtok] if kc == 0 else (), inc=(kc == NKC - 1))
                ev = p.emit("vector", lambda v, ps=ps, t0=t0, tw=tw, p0=p0:
                            v.tensor_copy(gsum[0:16, p0 + t0: p0 + t0 + tw], ps[0:16, 0:tw]),
                            waits=[last], inc=True)
                pool.release(bi, ev)
                gs_toks.append(ev)
            gs_done = gs_toks[-1]

            for oci in range(32):
                oc = oci
                gi = pi * 32 + oci
                wb = wpb[gi % 2]
                qb_ = qwb[gi % 2]
                sr_ = srb[gi % 2]
                dq_ = wdq[gi % 2]
                prev = mm_last.get(gi - 2)
                wl = p.emit("gpsimd", lambda g, oc=oc, wb=wb: g.dma_start(out=wb[:, :], in_=wp_in[oc]),
                            waits=[prev], inc=True, dma="wp%d" % (gi % 2))
                ql = p.emit("gpsimd", lambda g, oc=oc, qb_=qb_: g.dma_start(out=qb_[:, :], in_=qw_in[oc]),
                            waits=[prev], inc=True, dma="qw%d" % (gi % 2))
                sl = p.emit("gpsimd", lambda g, oc=oc, sr_=sr_: g.dma_start(out=sr_[:, :], in_=sr_in[oc]),
                            waits=[prev], inc=True, dma="sr%d" % (gi % 2))
                deq_eng = "vector" if oci % 2 == 0 else "gpsimd"
                dq_tok[gi] = p.emit(deq_eng, lambda e, qb_=qb_, sr_=sr_, dq_=dq_:
                                    e.tensor_mul(dq_[:, :], qb_[:, :], sr_[:, :]),
                                    waits=[ql, sl, mm_last.get(gi - 2)], inc=True)
                last = None
                for (t0, tw) in tiles_of(pw):
                    bi, ps, wtok = pool.acquire(1)[0]
                    sl_x = slot % 4
                    xt = p.emit("sync", lambda s, oc=oc, t0=t0, tw=tw, b=xts[sl_x], p0=p0:
                                s.dma_start(out=b[:, 0:tw], in_=xq_in[oc * 128:(oc + 1) * 128, p0 + t0:p0 + t0 + tw]),
                                waits=[xt_tok[sl_x]], inc=True, dma="xt%d" % sl_x)
                    last = p.emit("tensor", lambda t, ps=ps, oc=oc, t0=t0, tw=tw, p0=p0:
                                  t.matmul(ps[:, 0:tw], zsneg[:, oc * 128:(oc + 1) * 128],
                                           gsum[0:16, p0 + t0: p0 + t0 + tw], start=True, stop=False),
                                  waits=[zs_tok, gs_done, wtok], inc=False)
                    for kc in range(NKC):
                        p.emit("tensor", lambda t, ps=ps, kc=kc, t0=t0, tw=tw, dq_=dq_:
                               t.matmul(ps[:, 0:tw], dq_[:, kc * 128:(kc + 1) * 128],
                                        xnsb[:, kc * pW + t0: kc * pW + t0 + tw], start=False, stop=False),
                               waits=[dq_tok[gi]] if kc == 0 else ())
                    for kc in range(NKC):
                        last = p.emit("tensor", lambda t, ps=ps, kc=kc, t0=t0, tw=tw, wb=wb:
                                      t.matmul(ps[:, 0:tw], wb[:, kc * 128:(kc + 1) * 128],
                                               ysb[:, kc * pW + t0: kc * pW + t0 + tw],
                                               start=False, stop=(kc == NKC - 1)),
                                      waits=[wl] if kc == 0 else (), inc=(kc == NKC - 1))
                    ev = p.emit("vector", lambda v, ps=ps, oc=oc, tw=tw, b=xts[sl_x], o=x1s[sl_x]:
                                v.scalar_tensor_tensor(o[:, 0:tw], ps[:, 0:tw], b0b[:, oc:oc + 1],
                                                       b[:, 0:tw], op0=ALU.add, op1=ALU.add),
                                waits=[last, xt, x1_tok[sl_x]], inc=True)
                    pool.release(bi, ev)
                    xt_tok[sl_x] = ev
                    dt = p.emit("sync", lambda s, oc=oc, t0=t0, tw=tw, o=x1s[sl_x], p0=p0:
                                s.dma_start(out=x1_out[oc][:, p0 + t0: p0 + t0 + tw], in_=o[:, 0:tw]),
                                waits=[ev], inc=True, dma="x1o%d" % sl_x)
                    x1_tok[sl_x] = dt
                    slot += 1
                mm_last[gi] = last
            mm_last[("pass_end", pi)] = mm_last[pi * 32 + 31]
        p.finalize(block, final_waits=[p.chan_total("x1o%d" % i) for i in range(4)])
    return nc


def build_s4a(parity):
    P = PARAMS[parity]
    q0, qW, own0, ownW = P["q0"], P["qW"], P["own0"], P["ownW"]
    stride = ownW + 2
    lo = 1 if own0 == 0 else 0
    ln_cnt = ownW + 1
    x1_lo = own0 - q0 - 1 + lo
    nc = bass.Bass("TRN2", target_bir_lowering=False)

    x1_in = nc.dram_tensor("x1", [NKC, 128, qW], F32, kind="ExternalInput").ap()
    cw_in = nc.dram_tensor("conv_img", [24, 128, 3 * DM], F32, kind="ExternalInput").ap()
    cb_in = nc.dram_tensor("conv_b", [128, 24], F32, kind="ExternalInput").ap()
    qkv1_out = nc.dram_tensor("qkv1", [24, 128, ownW], BF16, kind="ExternalOutput").ap()
    xn1_out = nc.dram_tensor("xn1_own", [NKC, 128, ownW], BF16, kind="ExternalOutput").ap()

    from contextlib import ExitStack
    with ExitStack() as stk:
        sb = lambda name, shape, dt: stk.enter_context(nc.sbuf_tensor(name, shape, dt))
        xn1 = sb("xn1", [128, NKC * stride], BF16)
        cwb = [sb("cwb0", [128, 3 * DM], BF16), sb("cwb1", [128, 3 * DM], BF16)]
        cbb = sb("cbb", [128, 24], F32)
        am = sb("am", [128, ln_cnt], BF16)
        bm = sb("bm", [128, ln_cnt], BF16)
        sq_scr = sb("sq", [128, 1024], BF16)
        rowbuf = sb("rows", [1, 6 * 512], F32)
        onec = sb("onec", [128, 1], BF16)
        oner = sb("oner", [1, 128], F32)
        qst = [sb(f"qst{i}", [128, 512], BF16) for i in range(4)]
        ppool = [stk.enter_context(nc.psum_tensor(f"pp{i}", [128, 512], F32)) for i in range(8)]
        block = stk.enter_context(nc.Block())

        p = Prog(nc)
        p.make_sems()
        pool = PsumPool(p, ppool)

        p.emit("vector", lambda v: v.memset(onec[:, :], 1.0))
        p.emit("vector", lambda v: v.memset(oner[:, :], 1.0), inc=True)
        padc = 0 if own0 == 0 else ownW + 1
        zp = []
        for kc in range(NKC):
            zp.append(p.emit("vector", lambda v, kc=kc: v.memset(
                xn1[:, kc * stride + padc: kc * stride + padc + 1], 0.0), inc=(kc == NKC - 1)))
        pad_tok = zp[-1]
        p.emit("gpsimd", lambda g: g.dma_start(out=cbb[:, :], in_=cb_in), inc=True, dma="tab")
        tab_all = p.chan_total("tab")
        for kc in range(NKC):
            p.emit("gpsimd", lambda g, kc=kc: g.dma_start(
                out=xn1[:, kc * stride + lo: kc * stride + lo + ln_cnt],
                in_=x1_in[kc][:, x1_lo: x1_lo + ln_cnt]), inc=True, dma="xin")
        xin_all = p.chan_total("xin")

        ln_toks = emit_ln(p, pool, xn1, stride, ln_cnt, am, bm, sq_scr, rowbuf,
                          onec, oner, xin_all, col0=lo)
        ln_done = ln_toks[-1]

        for kc in range(NKC):
            p.emit("sync", lambda s, kc=kc: s.dma_start(
                out=xn1_out[kc], in_=xn1[:, kc * stride + 1: kc * stride + 1 + ownW]),
                waits=[ln_toks[kc]], inc=True, dma="xno")

        wl_tok = {}
        mm_last = {}
        st_tok = [None] * 4
        slot = 0
        for oc in range(24):
            wb = cwb[oc % 2]
            wl = p.emit("gpsimd", lambda g, oc=oc, wb=wb: g.dma_start(out=wb[:, :], in_=cw_in[oc]),
                        waits=[mm_last.get(oc - 2)], inc=True, dma="cw%d" % (oc % 2))
            last = None
            for (t0, tw) in tiles_of(ownW):
                bi, ps, wtok = pool.acquire(1)[0]
                for w in range(3):
                    for kc in range(NKC):
                        first = (w == 0 and kc == 0)
                        last = p.emit("tensor", lambda t, ps=ps, w=w, kc=kc, t0=t0, tw=tw, wb=wb:
                                      t.matmul(ps[:, 0:tw], wb[:, (w * NKC + kc) * 128:(w * NKC + kc + 1) * 128],
                                               xn1[:, kc * stride + t0 + w: kc * stride + t0 + w + tw],
                                               start=(w == 0 and kc == 0), stop=(w == 2 and kc == NKC - 1)),
                                      waits=[wl, ln_done, pad_tok, wtok] if first else (),
                                      inc=(w == 2 and kc == NKC - 1))
                sl = slot % 4
                ev = p.emit("scalar", lambda s, ps=ps, oc=oc, tw=tw, o=qst[sl]:
                            s.add(o[:, 0:tw], ps[:, 0:tw], cbb[:, oc:oc + 1]),
                            waits=[last, tab_all, st_tok[sl]], inc=True)
                pool.release(bi, ev)
                dt = p.emit("sync", lambda s, oc=oc, t0=t0, tw=tw, o=qst[sl]:
                            s.dma_start(out=qkv1_out[oc][:, t0:t0 + tw], in_=o[:, 0:tw]),
                            waits=[ev], inc=True, dma="qo%d" % sl)
                st_tok[sl] = dt
                slot += 1
            mm_last[oc] = last
        p.finalize(block, final_waits=[p.chan_total("xno")] + [p.chan_total("qo%d" % i) for i in range(4)])
    return nc


def build_s4b(parity, stop_after=None, skip=()):
    P = PARAMS[parity]
    ownW = P["ownW"]
    NB = ownW // 128
    nc = bass.Bass("TRN2", target_bir_lowering=False)

    qkv1_in = nc.dram_tensor("qkv1", [24, 128, ownW], BF16, kind="ExternalInput").ap()
    wq_in = nc.dram_tensor("wq_img", [8, 128, 1024], F32, kind="ExternalInput").ap()
    wk_in = nc.dram_tensor("wk_img", [8, 128, 1024], F32, kind="ExternalInput").ap()
    wv_in = nc.dram_tensor("wv_img", [8, 128, 1024], F32, kind="ExternalInput").ap()
    wo_in = nc.dram_tensor("wo_img", [8, 128, 1024], F32, kind="ExternalInput").ap()
    bq_in = nc.dram_tensor("bq", [128, 8], F32, kind="ExternalInput").ap()
    bk_in = nc.dram_tensor("bk", [128, 8], F32, kind="ExternalInput").ap()
    bv_in = nc.dram_tensor("bv_col", [128, 8], F32, kind="ExternalInput").ap()
    bo_in = nc.dram_tensor("bo", [128, 8], F32, kind="ExternalInput").ap()
    id_in = nc.dram_tensor("ident", [128, 128], BF16, kind="ExternalInput").ap()
    o2_out = nc.dram_tensor("o2", [8, 128, ownW], BF16, kind="ExternalOutput").ap()

    from contextlib import ExitStack
    with ExitStack() as stk:
        sb = lambda name, shape, dt: stk.enter_context(nc.sbuf_tensor(name, shape, dt))
        qkv1 = sb("qkv1b", [128, 24 * ownW], BF16)
        whb = [sb("whb0", [128, 1024], BF16), sb("whb1", [128, 1024], BF16)]
        wob = sb("wob", [128, 8 * 1024], BF16)
        qh = sb("qh", [128, 8 * ownW], BF16)
        kh = sb("kh", [128, 8 * ownW], BF16)
        vh = sb("vh", [128, 8 * ownW], BF16)
        ofm = sb("ofm", [128, 8 * ownW], BF16)
        vtmb = sb("vtmb", [128, 1024], BF16)
        expq = sb("expq", [128, 512], BF16)
        r4 = sb("r4", [1, 512], F32)
        rmq = sb("rmq", [64, 512], BF16)
        wobias = sb("wobias", [128, 8], F32)
        bqb = sb("bqb", [128, 8], F32)
        bkb = sb("bkb", [128, 8], F32)
        bvb = sb("bvb", [128, 8], BF16)
        bob = sb("bob", [128, 8], F32)
        idb = sb("idb", [128, 128], BF16)
        onec = sb("onec", [128, 1], BF16)
        oner = sb("oner", [1, 128], F32)
        ost = [sb(f"ost{i}", [128, 512], BF16) for i in range(4)]
        pool6 = [stk.enter_context(nc.psum_tensor(f"pp{i}", [128, 512], F32)) for i in range(6)]
        ptrb = [stk.enter_context(nc.psum_tensor(f"pt{i}", [128, 1024], BF16)) for i in range(2)]
        block = stk.enter_context(nc.Block())

        p = Prog(nc)
        p.make_sems()
        pool = PsumPool(p, pool6)

        p.emit("vector", lambda v: v.memset(onec[:, :], 1.0))
        p.emit("vector", lambda v: v.memset(oner[:, :], 1.0), inc=True)
        for (dst, src) in [(bqb, bq_in), (bkb, bk_in), (bvb, bv_in), (bob, bo_in), (idb, id_in)]:
            p.emit("gpsimd", lambda g, dst=dst, src=src: g.dma_start(out=dst[:, :], in_=src),
                   inc=True, dma="tab")
        for oc in range(8):
            p.emit("gpsimd", lambda g, oc=oc: g.dma_start(out=wob[:, oc * 1024:(oc + 1) * 1024], in_=wo_in[oc]),
                   inc=True, dma="tab")
        tab_all = p.chan_total("tab")
        for ch in range(24):
            p.emit("sync", lambda s, ch=ch: s.dma_start(out=qkv1[:, ch * ownW:(ch + 1) * ownW], in_=qkv1_in[ch]),
                   inc=True, dma="qin")
        qin_all = p.chan_total("qin")

        wb_toks = []
        for oc in range(8):
            bi, ps, wtok = pool.acquire(1)[0]
            last = None
            for kc in range(8):
                last = p.emit("tensor", lambda t, ps=ps, oc=oc, kc=kc:
                              t.matmul(ps[:, 0:1], wob[:, oc * 1024 + kc * 128: oc * 1024 + (kc + 1) * 128],
                                       bvb[:, kc:kc + 1], start=(kc == 0), stop=(kc == 7)),
                              waits=[tab_all, wtok] if kc == 0 else (), inc=(kc == 7))
            ev = p.emit("vector", lambda v, ps=ps, oc=oc: v.tensor_copy(wobias[:, oc:oc + 1], ps[:, 0:1]),
                        waits=[last], inc=True)
            pool.release(bi, ev)
            wb_toks.append(ev)
        wob_tok = p.emit("vector", lambda v: v.tensor_add(wobias[:, :], wobias[:, :], bob[:, :]),
                         waits=[tab_all], inc=True)

        st_tok = [None] * 4
        proj_done_s = None
        proj_done_v = None
        gi = 0
        wl_prev = {}
        for (kind, img, dstbuf, bias) in [(0, wq_in, qh, bqb), (1, wk_in, kh, bkb), (2, wv_in, vh, None)]:
            for oc in range(8):
                wb = whb[gi % 2]
                wl = p.emit("gpsimd", lambda g, img=img, oc=oc, wb=wb: g.dma_start(out=wb[:, :], in_=img[oc]),
                            waits=[wl_prev.get(gi - 2)], inc=True, dma="wh%d" % (gi % 2))
                last = None
                for (t0, tw) in tiles_of(ownW):
                    bi, ps, wtok = pool.acquire(1)[0]
                    for kc in range(8):
                        last = p.emit("tensor", lambda t, ps=ps, kc=kc, t0=t0, tw=tw, wb=wb, kind=kind:
                                      t.matmul(ps[:, 0:tw], wb[:, kc * 128:(kc + 1) * 128],
                                               qkv1[:, (kind * 8 + kc) * ownW + t0:(kind * 8 + kc) * ownW + t0 + tw],
                                               start=(kc == 0), stop=(kc == 7)),
                                      waits=[wl, qin_all, wtok] if kc == 0 else (), inc=(kc == 7))
                    for hh in range(2):
                        h = 2 * oc + hh
                        rb = 64 * (h // 8)
                        cb = (h % 8) * ownW
                        if bias is not None:
                            ev = p.emit("scalar", lambda s, ps=ps, hh=hh, oc=oc, t0=t0, tw=tw, dstbuf=dstbuf,
                                        bias=bias, rb=rb, cb=cb:
                                        s.add(dstbuf[rb:rb + 64, cb + t0: cb + t0 + tw],
                                              ps[64 * hh:64 * hh + 64, 0:tw], bias[64 * hh:64 * hh + 64, oc:oc + 1]),
                                        waits=[last, tab_all], inc=True)
                            proj_done_s = ev
                        else:
                            ev = p.emit("vector", lambda v, ps=ps, hh=hh, oc=oc, t0=t0, tw=tw, dstbuf=dstbuf,
                                        rb=rb, cb=cb:
                                        v.tensor_copy(dstbuf[rb:rb + 64, cb + t0: cb + t0 + tw],
                                                      ps[64 * hh:64 * hh + 64, 0:tw]),
                                        waits=[last], inc=True)
                            proj_done_v = ev
                    pool.release(bi, ev)
                wl_prev[gi] = last
                gi += 1
        if stop_after == "proj":
            dbg = p.emit("sync", lambda s: s.dma_start(out=o2_out[0], in_=qh[:, 0:ownW]), waits=[proj_done_s, proj_done_v], inc=True, dma="oo0")
            p.finalize(block, final_waits=[p.chan_total("oo0")])
            return nc
        tr_tok = [None, None]
        prev_quad_t = None
        prev_quad_g = None
        prev_b_pv = None
        prev_rml = None
        for b in range(NB):
            vt_evs = []
            for h in range(16):
                bk_ = h % 2
                rb = 64 * (h // 8)
                cb = (h % 8) * ownW
                ttok = p.emit("tensor", lambda t, h=h, b=b, rb=rb, cb=cb, o=ptrb[bk_][:, 0:64]:
                              t.transpose(o, vh[rb:rb + 64, cb + b * 128: cb + b * 128 + 128], idb[rb:rb + 64, rb:rb + 64]),
                              waits=[proj_done_v, tab_all, tr_tok[bk_], prev_b_pv], inc=True)
                ev = p.emit("vector", lambda v, h=h, i=ptrb[bk_][:, 0:64]:
                            v.tensor_copy(vtmb[:, h * 64:(h + 1) * 64], i), waits=[ttok], inc=True)
                tr_tok[bk_] = ev
                vt_evs.append(ev)
            vt_done = vt_evs[-1]
            if stop_after == "trans" and b == 0:
                dbg = p.emit("sync", lambda s: s.dma_start(out=o2_out[0][:, 0:1024], in_=vtmb[:, :]), waits=[vt_done], inc=True, dma="oo0")
                p.finalize(block, final_waits=[p.chan_total("oo0")])
                return nc
            for qd in range(4):
                bi, psc, wtok = pool.acquire(1)[0]
                mms = None
                rbq = 64 * (qd // 2)
                for j in range(4):
                    h = 4 * qd + j
                    cb = (h % 8) * ownW
                    mms = p.emit("tensor", lambda t, ps=psc, j=j, cb=cb, rbq=rbq, b=b:
                                 t.matmul(ps[:, j * 128:(j + 1) * 128],
                                          kh[rbq:rbq + 64, cb + b * 128: cb + b * 128 + 128],
                                          qh[rbq:rbq + 64, cb + b * 128: cb + b * 128 + 128],
                                          start=True, stop=True),
                                 waits=[proj_done_s, proj_done_v, wtok] if j == 0 else (), inc=(j == 3))
                etok = p.emit("scalar", lambda s, ps=psc: s.activation(expq[:, :], ps[:, 0:512], AF.Exp, scale=0.125),
                              waits=[mms, prev_quad_t, prev_quad_g], inc=True)
                rot_rel_sc = etok
                if "mask" not in skip:
                    mtok = p.emit("gpsimd", lambda g: g.affine_select(expq[:, :], expq[:, :],
                                                                      [[0, 4], [1, 128]], ALU.is_ge, 0.0,
                                                                      base=0, channel_multiplier=-1),
                                  waits=[etok], inc=True)
                    prev_quad_g = mtok
                else:
                    mtok = etok
                bi2, pss, wtok2 = pool.acquire(1)[0]
                smm = p.emit("tensor", lambda t, ps=pss: t.matmul(ps[0:1, 0:512], onec[:, 0:1], expq[:, :],
                                                                  start=True, stop=True),
                             waits=[mtok, wtok2], inc=True)
                rtok = p.emit("vector", lambda v, ps=pss: v.reciprocal(r4[0:1, :], ps[0:1, 0:512]),
                              waits=[smm, prev_rml], inc=True)
                pool.release(bi2, rtok)
                bi3, pso, wtok3 = pool.acquire(1)[0]
                pvl = None
                for j in range(4):
                    h = 4 * qd + j
                    pvl = p.emit("tensor", lambda t, ps=pso, j=j, h=h:
                                 t.matmul(ps[0:64, j * 128:(j + 1) * 128],
                                          vtmb[:, h * 64: h * 64 + 64],
                                          expq[:, j * 128:(j + 1) * 128], start=True, stop=True),
                                 waits=[vt_done, mtok, wtok3] if j == 0 else (), inc=(j == 3))
                pool.release(bi, etok)
                bi4, psr, wtok4 = pool.acquire(1)[0]
                rml = None
                for j in ([] if "rm" in skip else range(4)):
                    rml = p.emit("tensor", lambda t, ps=psr, j=j:
                                 t.matmul(ps[0:64, j * 128:(j + 1) * 128],
                                          oner[0:1, 0:64], r4[0:1, j * 128:(j + 1) * 128],
                                          start=True, stop=True),
                                 waits=[rtok, wtok4] if j == 0 else (), inc=(j == 3))
                if "rm" in skip:
                    rml = rtok
                    erm = p.emit("vector", lambda v: v.memset(rmq[:, :], 1.0), waits=[rtok], inc=True)
                else:
                    erm = p.emit("vector", lambda v, ps=psr: v.tensor_copy(rmq[0:64, :], ps[0:64, 0:512]), waits=[rml], inc=True)
                prev_rml = rml
                pool.release(bi4, erm)
                ev2 = None
                for j in range(4):
                    h = 4 * qd + j
                    rb = 64 * (h % 2)
                    cb = (h // 2) * ownW
                    ev2 = p.emit("vector", lambda v, ps=pso, j=j, rb=rb, cb=cb, b=b:
                                 v.tensor_mul(ofm[rb:rb + 64, cb + b * 128: cb + b * 128 + 128],
                                              ps[0:64, j * 128:(j + 1) * 128], rmq[0:64, j * 128:(j + 1) * 128]),
                                 waits=[pvl, erm], inc=(j == 3))
                pool.release(bi3, ev2)
                prev_quad_t = pvl
                prev_b_pv = pvl
                ofm_tok = ev2
        if stop_after == "attn":
            dbg = p.emit("sync", lambda s: s.dma_start(out=o2_out[0], in_=ofm[:, 0:ownW]), waits=[ofm_tok], inc=True, dma="oo0")
            p.finalize(block, final_waits=[p.chan_total("oo0")])
            return nc
        st_tok2 = [None] * 4
        slot = 0
        out_chans = []
        for oc in range(8):
            last = None
            for (t0, tw) in tiles_of(ownW):
                bi, ps, wtok = pool.acquire(1)[0]
                for kc in range(8):
                    last = p.emit("tensor", lambda t, ps=ps, oc=oc, kc=kc, t0=t0, tw=tw:
                                  t.matmul(ps[:, 0:tw], wob[:, oc * 1024 + kc * 128: oc * 1024 + (kc + 1) * 128],
                                           ofm[:, kc * ownW + t0: kc * ownW + t0 + tw],
                                           start=(kc == 0), stop=(kc == 7)),
                                  waits=[ofm_tok, wob_tok, wtok] if kc == 0 else (), inc=(kc == 7))
                sl = slot % 4
                ev = p.emit("scalar", lambda s, ps=ps, oc=oc, tw=tw, o=ost[sl]:
                            s.add(o[:, 0:tw], ps[:, 0:tw], wobias[:, oc:oc + 1]),
                            waits=[last, st_tok2[sl], wob_tok], inc=True)
                pool.release(bi, ev)
                dt = p.emit("sync", lambda s, oc=oc, t0=t0, tw=tw, o=ost[sl]:
                            s.dma_start(out=o2_out[oc][:, t0:t0 + tw], in_=o[:, 0:tw]),
                            waits=[ev], inc=True, dma="oo%d" % sl)
                st_tok2[sl] = dt
                slot += 1
        p.finalize(block, final_waits=[p.chan_total("oo%d" % i) for i in range(4)])
    return nc


SORT8_NET = [(0, 1), (2, 3), (4, 5), (6, 7),
             (0, 2), (1, 3), (4, 6), (5, 7),
             (1, 2), (5, 6), (0, 4), (3, 7),
             (1, 5), (2, 6), (1, 4), (3, 6),
             (2, 4), (3, 5), (3, 4)]


def build_s5(parity):
    P = PARAMS[parity]
    q0, qW, own0, ownW = P["q0"], P["qW"], P["own0"], P["ownW"]
    NT = ownW // 128
    x1_own = own0 - q0
    nc = bass.Bass("TRN2", target_bir_lowering=False)

    x1_in = nc.dram_tensor("x1", [NKC, 128, qW], F32, kind="ExternalInput").ap()
    xn_in = nc.dram_tensor("xn1_own", [NKC, 128, ownW], BF16, kind="ExternalInput").ap()
    o2_in = nc.dram_tensor("o2", [8, 128, ownW], BF16, kind="ExternalInput").ap()
    wb_in = nc.dram_tensor("wbase_img", [32, 128, DM], F32, kind="ExternalInput").ap()
    pj_in = nc.dram_tensor("projw_img", [32, 128, 1024], F32, kind="ExternalInput").ap()
    wr_in = nc.dram_tensor("wr_img", [128, NKC * 8], F32, kind="ExternalInput").ap()
    br_in = nc.dram_tensor("br", [8, 1], F32, kind="ExternalInput").ap()
    A_in = nc.dram_tensor("A_img", [128, DM], F32, kind="ExternalInput").ap()
    B_in = nc.dram_tensor("B_sb", [128, DM], F32, kind="ExternalInput").ap()
    rep_in = nc.dram_tensor("rep", [8, 128], F32, kind="ExternalInput").ap()
    bb_in = nc.dram_tensor("bbase", [128, 32], F32, kind="ExternalInput").ap()
    pb_in = nc.dram_tensor("projb", [128, 32], F32, kind="ExternalInput").ap()
    idf_in = nc.dram_tensor("identf", [128, 128], F32, kind="ExternalInput").ap()
    out_d = nc.dram_tensor("outp", [NKC, 128, ownW], F32, kind="ExternalOutput").ap()

    from contextlib import ExitStack
    with ExitStack() as stk:
        sb = lambda name, shape, dt: stk.enter_context(nc.sbuf_tensor(name, shape, dt))
        xnb = sb("xnb", [128, NKC * ownW], BF16)
        o2b = sb("o2b", [128, 8 * ownW], BF16)
        wbb = [sb("wbb0", [128, DM], BF16), sb("wbb1", [128, DM], BF16)]
        pjb = [sb("pjb0", [128, 1024], BF16), sb("pjb1", [128, 1024], BF16)]
        wrb = sb("wrb", [128, NKC * 8], BF16)
        brb = sb("brb", [8, 1], F32)
        ab = sb("ab", [128, DM], BF16)
        bb = sb("bb", [128, DM], BF16)
        repb = sb("repb", [8, 128], BF16)
        btot = sb("btot", [128, 32], F32)
        pbb = sb("pbb", [128, 32], F32)
        idf = sb("idf", [128, 128], F32)
        logits = sb("logits", [8, 2048], F32)
        ztm = sb("ztm", [128, NT * 8], F32)
        zs = sb("zs", [128, NT * 8], F32)
        zc = sb("zc", [128, NT * 8], F32)
        tmp = sb("tmpv", [128, NT], F32)
        tau = sb("tau", [128, NT], F32)
        wsp = sb("wsp", [128, NT * 8], F32)
        wem = sb("wem", [8, ownW], BF16)
        wrep = sb("wrep", [128, ownW], BF16)
        hsb = sb("hsb", [128, ownW], BF16)
        xts = [sb(f"xt{i}", [128, 512], F32) for i in range(4)]
        x1s = [sb(f"x1s{i}", [128, 512], F32) for i in range(4)]
        ppool = [stk.enter_context(nc.psum_tensor(f"pp{i}", [128, 512], F32)) for i in range(8)]
        block = stk.enter_context(nc.Block())

        p = Prog(nc)
        p.make_sems()
        pool = PsumPool(p, ppool)

        for (dst, src) in [(wrb, wr_in), (brb, br_in), (ab, A_in), (bb, B_in), (repb, rep_in),
                           (btot, bb_in), (pbb, pb_in), (idf, idf_in)]:
            p.emit("gpsimd", lambda g, dst=dst, src=src: g.dma_start(out=dst[:, :], in_=src),
                   inc=True, dma="tab")
        tab_all = p.chan_total("tab")
        bt_tok = p.emit("vector", lambda v: v.tensor_add(btot[:, :], btot[:, :], pbb[:, :]),
                        waits=[tab_all], inc=True)
        for kc in range(NKC):
            p.emit("sync", lambda s, kc=kc: s.dma_start(out=xnb[:, kc * ownW:(kc + 1) * ownW], in_=xn_in[kc]),
                   inc=True, dma="xin")
        for ch in range(8):
            p.emit("sync", lambda s, ch=ch: s.dma_start(out=o2b[:, ch * ownW:(ch + 1) * ownW], in_=o2_in[ch]),
                   inc=True, dma="xin")
        xin_all = p.chan_total("xin")

        lg_toks = []
        for (t0, tw) in tiles_of(ownW):
            bi, ps, wtok = pool.acquire(1)[0]
            last = None
            for kc in range(NKC):
                last = p.emit("tensor", lambda t, ps=ps, kc=kc, t0=t0, tw=tw:
                              t.matmul(ps[0:8, 0:tw], wrb[:, kc * 8:(kc + 1) * 8],
                                       xnb[:, kc * ownW + t0: kc * ownW + t0 + tw],
                                       start=(kc == 0), stop=(kc == NKC - 1)),
                              waits=[tab_all, xin_all, wtok] if kc == 0 else (), inc=(kc == NKC - 1))
            ev = p.emit("scalar", lambda s, ps=ps, t0=t0, tw=tw:
                        s.add(logits[0:8, t0:t0 + tw], ps[0:8, 0:tw], brb[0:8, 0:1]),
                        waits=[last], inc=True)
            pool.release(bi, ev)
            lg_toks.append(ev)
        lg_done = lg_toks[-1]

        tr_evs = []
        for tt in range(NT):
            bi, ps, wtok = pool.acquire(1)[0]
            ttok = p.emit("tensor", lambda t, ps=ps, tt=tt:
                          t.transpose(ps[0:128, 0:8], logits[0:8, tt * 128:(tt + 1) * 128], idf[0:8, 0:8]),
                          waits=[lg_done, tab_all, wtok], inc=True)
            ev = p.emit("vector", lambda v, ps=ps, tt=tt:
                        v.tensor_copy(ztm[:, tt * 8:(tt + 1) * 8], ps[0:128, 0:8]), waits=[ttok], inc=True)
            pool.release(bi, ev)
            tr_evs.append(ev)
        def esl(buf, e):
            return bass.AP(buf[:, :].tensor, buf[:, :].offset + e, [[buf[:, :].ap[0][0], 128], [8, NT]])
        cp = p.emit("vector", lambda v: v.tensor_copy(zs[:, :], ztm[:, :]), waits=[tr_evs[-1]])
        for (i, j) in SORT8_NET:
            p.emit("vector", lambda v, i=i, j=j: v.tensor_tensor(tmp[:, :], esl(zs, i), esl(zs, j), ALU.min))
            p.emit("vector", lambda v, i=i, j=j: v.tensor_tensor(esl(zs, i), esl(zs, i), esl(zs, j), ALU.max))
            p.emit("vector", lambda v, j=j: v.tensor_copy(esl(zs, j), tmp[:, :]))
        p.emit("vector", lambda v: v.tensor_copy(esl(zc, 0), esl(zs, 0)))
        for k in range(1, 8):
            p.emit("vector", lambda v, k=k: v.tensor_add(esl(zc, k), esl(zc, k - 1), esl(zs, k)))
        for k in range(8):
            p.emit("vector", lambda v, k=k: v.tensor_scalar(esl(wsp, k), esl(zc, k),
                                                            1.0 / (k + 1), -1.0 / (k + 1),
                                                            op0=ALU.mult, op1=ALU.add))
        p.emit("vector", lambda v: v.tensor_tensor(tau[:, :], esl(wsp, 0), esl(wsp, 1), ALU.max))
        for k in range(2, 8):
            p.emit("vector", lambda v, k=k: v.tensor_tensor(tau[:, :], tau[:, :], esl(wsp, k), ALU.max))
        for e in range(8):
            p.emit("vector", lambda v, e=e: v.tensor_tensor(esl(wsp, e), esl(ztm, e), tau[:, :], ALU.subtract))
        sp_done = p.emit("vector", lambda v: v.tensor_scalar_max(wsp[:, :], wsp[:, :], 0.0), inc=True)

        wt_evs = []
        for tt in range(NT):
            bi, ps, wtok = pool.acquire(1)[0]
            ttok = p.emit("tensor", lambda t, ps=ps, tt=tt:
                          t.transpose(ps[0:8, 0:128], wsp[:, tt * 8:(tt + 1) * 8], idf[:, :]),
                          waits=[sp_done, wtok], inc=True)
            ev = p.emit("vector", lambda v, ps=ps, tt=tt:
                        v.tensor_copy(wem[0:8, tt * 128:(tt + 1) * 128], ps[0:8, 0:128]), waits=[ttok], inc=True)
            pool.release(bi, ev)
            wt_evs.append(ev)
        wem_done = wt_evs[-1]

        h_toks = []
        for (t0, tw) in tiles_of(ownW):
            bi, ps, wtok = pool.acquire(1)[0]
            mr = p.emit("tensor", lambda t, ps=ps, t0=t0, tw=tw:
                        t.matmul(ps[:, 0:tw], repb[0:8, :], wem[0:8, t0:t0 + tw], start=True, stop=True),
                        waits=[wem_done, tab_all, wtok], inc=True)
            er = p.emit("vector", lambda v, ps=ps, t0=t0, tw=tw:
                        v.tensor_copy(wrep[:, t0:t0 + tw], ps[:, 0:tw]), waits=[mr], inc=True)
            pool.release(bi, er)
            bi2, ps2, wtok2 = pool.acquire(1)[0]
            last = None
            for kc in range(NKC):
                last = p.emit("tensor", lambda t, ps=ps2, kc=kc, t0=t0, tw=tw:
                              t.matmul(ps[:, 0:tw], ab[:, kc * 128:(kc + 1) * 128],
                                       xnb[:, kc * ownW + t0: kc * ownW + t0 + tw],
                                       start=(kc == 0), stop=(kc == NKC - 1)),
                              waits=[tab_all, wtok2] if kc == 0 else (), inc=(kc == NKC - 1))
            eh = p.emit("vector", lambda v, ps=ps2, t0=t0, tw=tw:
                        v.tensor_copy(hsb[:, t0:t0 + tw], ps[:, 0:tw]), waits=[last], inc=True)
            pool.release(bi2, eh)
            ht = p.emit("vector", lambda v, t0=t0, tw=tw:
                        v.tensor_mul(hsb[:, t0:t0 + tw], hsb[:, t0:t0 + tw], wrep[:, t0:t0 + tw]),
                        waits=[er], inc=True)
            h_toks.append(ht)
        h_done = h_toks[-1]

        mm_last = {}
        xt_tok = [None] * 4
        x1_tok = [None] * 4
        slot = 0
        for oc in range(32):
            wb = wbb[oc % 2]
            pj = pjb[oc % 2]
            prev = mm_last.get(oc - 2)
            wl = p.emit("gpsimd", lambda g, oc=oc, wb=wb: g.dma_start(out=wb[:, :], in_=wb_in[oc]),
                        waits=[prev], inc=True, dma="wb%d" % (oc % 2))
            pl = p.emit("gpsimd", lambda g, oc=oc, pj=pj: g.dma_start(out=pj[:, :], in_=pj_in[oc]),
                        waits=[prev], inc=True, dma="pj%d" % (oc % 2))
            last = None
            for (t0, tw) in tiles_of(ownW):
                bi, ps, wtok = pool.acquire(1)[0]
                sl_x = slot % 4
                xt = p.emit("sync", lambda s, oc=oc, t0=t0, tw=tw, b=xts[sl_x]:
                            s.dma_start(out=b[:, 0:tw], in_=x1_in[oc][:, x1_own + t0: x1_own + t0 + tw]),
                            waits=[xt_tok[sl_x]], inc=True, dma="xt%d" % sl_x)
                for kc in range(NKC):
                    last = p.emit("tensor", lambda t, ps=ps, kc=kc, t0=t0, tw=tw, wb=wb:
                                  t.matmul(ps[:, 0:tw], wb[:, kc * 128:(kc + 1) * 128],
                                           xnb[:, kc * ownW + t0: kc * ownW + t0 + tw],
                                           start=(kc == 0), stop=False),
                                  waits=[wl, wtok] if kc == 0 else ())
                p.emit("tensor", lambda t, ps=ps, oc=oc, t0=t0, tw=tw:
                       t.matmul(ps[:, 0:tw], bb[:, oc * 128:(oc + 1) * 128],
                                hsb[:, t0:t0 + tw], start=False, stop=False),
                       waits=[h_done, tab_all])
                for kc in range(8):
                    last = p.emit("tensor", lambda t, ps=ps, kc=kc, t0=t0, tw=tw, pj=pj:
                                  t.matmul(ps[:, 0:tw], pj[:, kc * 128:(kc + 1) * 128],
                                           o2b[:, kc * ownW + t0: kc * ownW + t0 + tw],
                                           start=False, stop=(kc == 7)),
                                  waits=[pl] if kc == 0 else (), inc=(kc == 7))
                ev = p.emit("vector", lambda v, ps=ps, oc=oc, tw=tw, b=xts[sl_x], o=x1s[sl_x]:
                            v.scalar_tensor_tensor(o[:, 0:tw], ps[:, 0:tw], btot[:, oc:oc + 1],
                                                   b[:, 0:tw], op0=ALU.add, op1=ALU.add),
                            waits=[last, xt, x1_tok[sl_x], bt_tok], inc=True)
                pool.release(bi, ev)
                xt_tok[sl_x] = ev
                dt = p.emit("sync", lambda s, oc=oc, t0=t0, tw=tw, o=x1s[sl_x]:
                            s.dma_start(out=out_d[oc][:, t0:t0 + tw], in_=o[:, 0:tw]),
                            waits=[ev], inc=True, dma="oo%d" % sl_x)
                x1_tok[sl_x] = dt
                slot += 1
            mm_last[oc] = last
        p.finalize(block, final_waits=[p.chan_total("oo%d" % i) for i in range(4)])
    return nc



def kernel(**inputs):
    g, cores = host_prep(inputs)
    ident_bf = np.eye(128, dtype=np.float32).astype(BF)
    identf = np.eye(128, dtype=np.float32)
    res = {c: {} for c in range(8)}

    def launch(builder, par, make_in):
        ids = [c for c in range(8) if c % 2 == par]
        nc = builder(par)
        in_maps = [make_in(c) for c in ids]
        r = run_bass_kernel_spmd(nc, in_maps, core_ids=ids)
        for i, c in enumerate(ids):
            res[c].update(r.results[i])

    for par in (0, 1):
        launch(build_s1, par, lambda c: dict(
            xfm_kv=cores[c]["xfm_kv"], wqkv_img=g["wqkv_img"], wte_img=g["wte_img"],
            cos=cores[c]["cos"], ssin=cores[c]["ssin"]))
    for par in (0, 1):
        launch(build_s2, par, lambda c: dict(
            q_out=res[c]["q_out"], k_out=res[c]["k_out"], v_out=res[c]["v_out"],
            gating=g["gating"], ident=ident_bf))
    for par in (0, 1):
        launch(build_s3, par, lambda c: dict(
            y_out=res[c]["y_out"], xn_own=res[c]["xn_own"], w_proj_img=g["w_proj_img"],
            qw_img=g["qw_img"], s_rep=g["s_rep"], z16=g["z16"], s16=g["s16"],
            gimg=g["gimg"], bias0=g["bias0"], xfm_q=cores[c]["xfm_q"]))
    for par in (0, 1):
        launch(build_s4a, par, lambda c: dict(
            x1=res[c]["x1"], conv_img=g["conv_img"], conv_b=g["conv_b"]))
    for par in (0, 1):
        launch(build_s4b, par, lambda c: dict(
            qkv1=res[c]["qkv1"], wq_img=g["wq_img"], wk_img=g["wk_img"], wv_img=g["wv_img"],
            wo_img=g["wo_img"], bq=g["bq"], bk=g["bk"], bv_col=g["bv_col"], bo=g["bo"],
            ident=ident_bf))
    for par in (0, 1):
        launch(build_s5, par, lambda c: dict(
            x1=res[c]["x1"], xn1_own=res[c]["xn1_own"], o2=res[c]["o2"],
            wbase_img=g["wbase_img"], projw_img=g["projw_img"], wr_img=g["wr_img"], br=g["br"],
            A_img=g["A_img"], B_sb=g["B_sb"], rep=g["rep"], bbase=g["bbase"], projb=g["projb"],
            identf=identf))

    out = np.empty((B_SEQ, T_SEQ, DM), np.float32)
    for c in range(8):
        P = PARAMS[c % 2]
        seq = c // 2
        o = np.asarray(res[c]["outp"], np.float32).reshape(DM, P["ownW"])
        out[seq, P["own0"]:P["own0"] + P["ownW"], :] = o.T
    return out


# version 04
# speedup vs baseline: 1.8004x; 1.8004x over previous
import numpy as np
import ml_dtypes
import concourse.bass as bass
import concourse.mybir as mybir
from concourse.bass_utils import run_bass_kernel_spmd

F32 = mybir.dt.float32
BF16 = mybir.dt.bfloat16
F16 = mybir.dt.float16
I32 = mybir.dt.int32
AF = mybir.ActivationFunctionType
ALU = mybir.AluOpType
AX = mybir.AxisListType
BF = ml_dtypes.bfloat16

B_SEQ, T_SEQ, DM = 4, 2048, 4096
H0, HD0 = 32, 128
NKC = DM // 128
P_SPLIT = 1152
EPS = 1e-5

PARAMS = {
    0: dict(kvW=1280, q0=0,    qW=1280, own0=0,      ownW=1152),
    1: dict(kvW=2048, q0=1024, qW=1024, own0=1152,   ownW=896),
}

def tiles_of(width, step=512):
    out = []
    c = 0
    while c < width:
        w = min(step, width - c)
        out.append((c, w))
        c += w
    return out


def pack_lhsT(wT):
    IN, OUT = wT.shape
    return np.ascontiguousarray(
        wT.reshape(IN // 128, 128, OUT // 128, 128).transpose(2, 1, 0, 3).reshape(OUT // 128, 128, IN))


def host_prep(inp):
    g = {}
    x = np.ascontiguousarray(np.asarray(inp["L0_attn_x"]))
    w_attn = np.asarray(inp["L0_attn_w_attn"])
    g["wqkv_img"] = pack_lhsT(np.ascontiguousarray(w_attn.T))
    wte = np.asarray(inp["L0_attn_adapter_wte"])
    g["wte_img"] = np.ascontiguousarray(wte.reshape(10, NKC, 128).transpose(2, 1, 0).reshape(128, NKC * 10))
    g["gating"] = np.asarray(inp["L0_attn_gating"]).reshape(1, 1)

    theta = 1.0 / 10000.0 ** (np.arange(0, HD0, 2) / HD0)
    idx = np.outer(np.arange(T_SEQ), theta)
    idx2 = np.concatenate([idx, idx], axis=-1)
    cos2 = np.cos(idx2).astype(np.float32)
    sin2 = np.sin(idx2).astype(np.float32)
    ss = sin2.copy()
    ss[:, :64] *= -1.0
    g["cos_fm"] = np.ascontiguousarray(cos2.T)
    ssT = ss.T
    g["ssin_fm"] = np.ascontiguousarray(np.concatenate([ssT[64:], ssT[:64]], axis=0))

    g["w_proj_img"] = pack_lhsT(np.ascontiguousarray(np.asarray(inp["L0_attn_w_proj"]).T))
    qw = np.asarray(inp["L0_ffn_qweight"])
    g["qw_img"] = pack_lhsT(np.ascontiguousarray(qw.T))
    scales = np.asarray(inp["L0_ffn_scales"]).reshape(DM, 16)
    zeros = np.asarray(inp["L0_ffn_zeros"]).reshape(DM, 16)
    oi = np.arange(DM)
    kc_i = oi // 128
    o_i = oi % 128
    s_img = np.empty((32, DM), np.float32)
    z_img = np.empty((32, DM), np.float32)
    for oc in range(32):
        s_img[oc] = scales[oc * 128 + o_i, kc_i // 2]
        z_img[oc] = zeros[oc * 128 + o_i, kc_i // 2]
    g["s_img"] = s_img
    g["z_img"] = z_img
    g["z16"] = np.ascontiguousarray(zeros.T)
    g["s16"] = np.ascontiguousarray(scales.T)
    gim = np.zeros((128, NKC * 16), np.float32)
    for kc in range(NKC):
        gim[:, kc * 16 + kc // 2] = 1.0
    g["gimg"] = gim
    g["s_rep"] = np.ascontiguousarray(np.broadcast_to(s_img[:, None, :], (32, 128, DM)))
    g["z_rep_row"] = z_img
    g["bias0"] = np.asarray(inp["L0_ffn_bias"]).reshape(32, 128).T.copy()

    cw = np.asarray(inp["L1_attn_conv_w"])
    g["conv_img"] = np.ascontiguousarray(
        cw.reshape(3, NKC, 128, 24, 128).transpose(3, 2, 0, 1, 4).reshape(24, 128, 3 * NKC * 128))
    g["conv_b"] = np.asarray(inp["L1_attn_conv_b"]).reshape(24, 128).T.copy()
    wq = np.asarray(inp["L1_attn_wq"]).reshape(1024, 1024)
    wk = np.asarray(inp["L1_attn_wk"]).reshape(1024, 1024)
    wv = np.asarray(inp["L1_attn_wv"]).reshape(1024, 1024)
    g["wq_img"] = pack_lhsT(wq)
    g["wk_img"] = pack_lhsT(wk)
    g["wv_img"] = pack_lhsT(wv)
    g["bq"] = np.asarray(inp["L1_attn_bq"]).reshape(1024).reshape(8, 128).T.copy()
    g["bk"] = np.asarray(inp["L1_attn_bk"]).reshape(1024).reshape(8, 128).T.copy()
    g["bv_col"] = np.asarray(inp["L1_attn_bv"]).reshape(1024).reshape(8, 128).T.copy()
    g["wo_img"] = pack_lhsT(np.asarray(inp["L1_attn_wo"]).reshape(1024, 1024))
    g["bo"] = np.asarray(inp["L1_attn_bo"]).reshape(8, 128).T.copy()
    g["projw_img"] = pack_lhsT(np.asarray(inp["L1_attn_proj_w"]))
    g["projb"] = np.asarray(inp["L1_attn_proj_b"]).reshape(32, 128).T.copy()

    g["wbase_img"] = pack_lhsT(np.ascontiguousarray(np.asarray(inp["L1_ffn_W_base"]).T))
    g["bbase"] = np.asarray(inp["L1_ffn_b_base"]).reshape(32, 128).T.copy()
    wr = np.asarray(inp["L1_ffn_W_router"])
    g["wr_img"] = np.ascontiguousarray(wr.reshape(8, NKC, 128).transpose(2, 1, 0).reshape(128, NKC * 8))
    g["br"] = np.asarray(inp["L1_ffn_b_router"]).reshape(8, 1)
    la = np.asarray(inp["L1_ffn_lora_A"])
    A_all = np.ascontiguousarray(la.transpose(1, 0, 2).reshape(DM, 128))
    g["A_img"] = pack_lhsT(A_all)[0]
    g["B_sb"] = np.ascontiguousarray(np.asarray(inp["L1_ffn_lora_B"]).reshape(128, DM))
    rep = np.zeros((8, 128), np.float32)
    for e in range(8):
        rep[e, e * 16:(e + 1) * 16] = 1.0
    g["rep"] = rep

    cores = []
    for c in range(8):
        par = c % 2
        P = PARAMS[par]
        seq = c // 2
        xs = np.ascontiguousarray(x[seq].T)
        d = dict(parity=par, seq=seq)
        d["xfm_kv"] = np.ascontiguousarray(xs[:, :P["kvW"]])
        d["xfm_q"] = np.ascontiguousarray(xs[:, P["q0"]:P["q0"] + P["qW"]])
        d["cos"] = np.ascontiguousarray(g["cos_fm"][:, :P["kvW"]])
        d["ssin"] = np.ascontiguousarray(g["ssin_fm"][:, :P["kvW"]])
        cores.append(d)
    return g, cores


ENGINES = ("tensor", "vector", "scalar", "gpsimd", "sync")


class Prog:

    def __init__(self, nc, tag="", shared_sems=None):
        self.nc = nc
        self.tag = tag
        self.ops = {e: [] for e in ENGINES}
        self.sems = shared_sems if shared_sems is not None else {}
        self.counts = {}
        self.hwm = {e: {} for e in ENGINES}
        self.pending = {e: [] for e in ENGINES}
        self.finals = []

    def make_sems(self):
        for e in ENGINES:
            self._mk(e)

    def _mk(self, key):
        if key not in self.sems:
            self.sems[key] = self.nc.semaphore(f"s{self.tag}_{key}").__enter__()
        if key not in self.counts:
            self.counts[key] = 0
        return key

    def barrier(self):
        snap = [(k, v) for (k, v) in self.counts.items() if v > 0]
        for e in ENGINES:
            self.pending[e].extend(snap)

    def emit(self, eng, fn, waits=(), inc=False, dma=None):
        key = self._mk("dma_" + dma) if dma else eng
        tok = None
        if inc:
            self.counts[key] += 16 if dma else 1
            tok = (key, self.counts[key])
        elif dma:
            raise AssertionError("every DMA needs inc=True (sync info required)")
        wl = []
        if self.pending[eng]:
            waits = list(self.pending[eng]) + list(waits)
            self.pending[eng] = []
        for t in waits:
            if t is None:
                continue
            (sk, v) = t
            if not sk.startswith("dma_") and sk == eng:
                continue
            h = self.hwm[eng].get(sk, 0)
            if v > h:
                wl.append((sk, v))
                self.hwm[eng][sk] = v
        self.ops[eng].append((fn, wl, tok))
        return tok

    def chan_total(self, ch):
        return ("dma_" + ch, self.counts.get("dma_" + ch, 0))

    def finalize(self, block, final_waits=()):
        for ename in ENGINES:
            oplist = self.ops[ename]
            sems = self.sems

            def mk(ename, oplist):
                def body(eng):
                    for (fn, wl, tok) in oplist:
                        for (sk, v) in wl:
                            eng.wait_ge(sems[sk], v)
                        ins = fn(eng)
                        if tok is not None:
                            ins.then_inc(sems[tok[0]], 16 if tok[0].startswith("dma_") else 1)
                    for t in final_waits:
                        if t is not None:
                            eng.wait_ge(sems[t[0]], t[1])
                return body
            getattr(block, ename)(mk(ename, oplist))


class PsumPool:

    def __init__(self, prog, banks):
        self.prog = prog
        self.banks = banks
        self.tokens = [None] * len(banks)
        self.ptr = 0

    def acquire(self, n):
        out = []
        for _ in range(n):
            i = self.ptr % len(self.banks)
            self.ptr += 1
            out.append((i, self.banks[i], self.tokens[i]))
        return out

    def release(self, i, token):
        self.tokens[i] = token



def emit_ln(p, pool, xq, xq_cols, kvW, a_m, b_m, sq_scr, rowbuf, ones_col, ones_row, load_tok, col0=0):
    nc = p.nc
    toks = []
    for (t0, tw) in tiles_of(kvW):
        bk = pool.acquire(4)
        (i_sx, ps_sx, w1), (i_sq, ps_sq, w2), (i_a, ps_a, w3), (i_b, ps_b, w4) = bk
        mm_sx = None
        mmsq_toks = {}
        for kc in range(NKC):
            xsl = xq[:, kc * xq_cols + col0 + t0: kc * xq_cols + col0 + t0 + tw]
            half = (kc % 2) * 512
            sqv = sq_scr[:, half:half + tw]
            sq_tok = p.emit("scalar", lambda s, sqv=sqv, xsl=xsl: s.activation(sqv, xsl, AF.Square),
                            waits=[load_tok, mmsq_toks.get(kc - 2)], inc=True)
            mm_sx = p.emit("tensor", lambda t, xsl=xsl, ps=ps_sx, tw=tw, kc=kc:
                           t.matmul(ps[0:1, 0:tw], ones_col[:, 0:1], xsl, start=(kc == 0), stop=(kc == NKC - 1)),
                           waits=[load_tok, w1 if kc == 0 else None], inc=(kc == NKC - 1))
            mmsq_toks[kc] = p.emit("tensor", lambda t, sqv=sqv, ps=ps_sq, tw=tw, kc=kc:
                           t.matmul(ps[0:1, 0:tw], ones_col[:, 0:1], sqv, start=(kc == 0), stop=(kc == NKC - 1)),
                           waits=[sq_tok, w2 if kc == 0 else None], inc=True)
        mm_sq = mmsq_toks[NKC - 1]
        Rs = [rowbuf[0:1, j * 512: j * 512 + tw] for j in range(6)]
        t_mean = p.emit("vector", lambda v, o=Rs[0], i=ps_sx[0:1, 0:tw]: v.tensor_scalar_mul(o, i, 1.0 / DM),
                        waits=[mm_sx], inc=True)
        t_var = p.emit("vector", lambda v, o=Rs[1], i=ps_sq[0:1, 0:tw]: v.tensor_scalar_mul(o, i, 1.0 / DM),
                       waits=[mm_sq], inc=True)
        p.emit("vector", lambda v, o=Rs[4], i=Rs[0]: v.tensor_mul(o, i, i), waits=[t_mean])
        p.emit("vector", lambda v, o=Rs[1], i=Rs[4]: v.tensor_sub(o, o, i), waits=[t_var])
        t_ve = p.emit("vector", lambda v, o=Rs[1]: v.tensor_scalar_add(o, o, EPS), inc=True)
        t_sd = p.emit("scalar", lambda s, o=Rs[5], i=Rs[1]: s.activation(o, i, AF.Sqrt), inc=True,
                      waits=[t_ve])
        t_a = p.emit("vector", lambda v, o=Rs[2], i=Rs[5]: v.reciprocal(o, i), waits=[t_sd], inc=True)
        t_b = p.emit("vector", lambda v, o=Rs[3], i=Rs[0], j=Rs[2]: v.tensor_mul(o, i, j), waits=[t_a], inc=True)
        bca = p.emit("tensor", lambda t, o=ps_a[:, 0:tw], r=Rs[2]: t.matmul(o, ones_row[0:1, 0:128], r,
                                                  start=True, stop=True), waits=[t_a, w3], inc=True)
        bcb = p.emit("tensor", lambda t, o=ps_b[:, 0:tw], r=Rs[3]: t.matmul(o, ones_row[0:1, 0:128], r,
                                                  start=True, stop=True), waits=[t_b, w4], inc=True)
        ea = p.emit("vector", lambda v, o=a_m[:, t0:t0 + tw], i=ps_a[:, 0:tw]: v.tensor_copy(o, i),
                    waits=[bca], inc=True)
        eb = p.emit("vector", lambda v, o=b_m[:, t0:t0 + tw], i=ps_b[:, 0:tw]: v.tensor_copy(o, i),
                    waits=[bcb], inc=True)
        pool.release(i_sx, t_mean)
        pool.release(i_sq, t_var)
        pool.release(i_a, ea)
        pool.release(i_b, eb)
    for kc in range(NKC):
        xsl = xq[:, kc * xq_cols + col0: kc * xq_cols + col0 + kvW]
        p.emit("vector", lambda v, xsl=xsl: v.tensor_mul(xsl, xsl, a_m[:, 0:kvW]), waits=[ea])
        tk = p.emit("vector", lambda v, xsl=xsl: v.tensor_sub(xsl, xsl, b_m[:, 0:kvW]), inc=True)
        toks.append(tk)
    return toks


def build_s1(parity, qkv_enable=True, ctx=None):
    P = PARAMS[parity]
    kvW, q0, qW = P["kvW"], P["q0"], P["qW"]
    xq_cols = kvW + 16
    if ctx is not None:
        nc = ctx["nc"]
        _D = lambda name, shape, dt, kind: ctx["d"][name]
    else:
        nc = bass.Bass("TRN2", target_bir_lowering=False)
        _D = lambda name, shape, dt, kind: nc.dram_tensor(name, shape, dt, kind=kind).ap()

    xfm = _D("xfm_kv", [DM, kvW], F32, "ExternalInput")
    wqkv = _D("wqkv_img", [96, 128, DM], F32, "ExternalInput")
    wte = _D("wte_img", [128, NKC * 10], F32, "ExternalInput")
    cosd = _D("cos", [128, kvW], F32, "ExternalInput")
    ssind = _D("ssin", [128, kvW], F32, "ExternalInput")
    q_out = _D("q_out", [32, 128, qW], BF16, "ExternalOutput")
    k_out = _D("k_out", [32, 128, kvW + 10], BF16, "ExternalOutput")
    v_out = _D("v_out", [32, 128, kvW + 10], BF16, "ExternalOutput")
    xn_out = _D("xn_own", [NKC, 128, qW], BF16, "ExternalOutput")

    ctxsfx = "_s1"
    from contextlib import ExitStack
    with ExitStack() as stk:
        if ctx is not None:
            def sb(name, shape, dt):
                if name not in ctx["bufs"]:
                    ctx["bufs"][name] = ctx["stk"].enter_context(nc.sbuf_tensor(name + ctxsfx, shape, dt))
                return ctx["bufs"][name]
        else:
            sb = lambda name, shape, dt: stk.enter_context(nc.sbuf_tensor(name, shape, dt))
        xq = sb("xq", [128, NKC * xq_cols], BF16)
        wb0 = sb("wb0", [128, DM], BF16)
        wb1 = sb("wb1", [128, DM], BF16)
        wteb = sb("wteb", [128, NKC * 10], BF16)
        cosb = sb("cosb", [128, kvW], BF16)
        ssinb = sb("ssinb", [128, kvW], BF16)
        am = sb("am", [128, kvW], BF16)
        bm = sb("bm", [128, kvW], BF16)
        sq_scr = sb("sq", [128, 1024], BF16)
        rowbuf = sb("rows", [1, 6 * 512], F32)
        ones_col = sb("onec", [128, 1], BF16)
        ones_row = sb("oner", [1, 128], F32)
        ropet = sb("ropet", [128, kvW], BF16)
        st0 = sb("st0", [128, kvW + 16], BF16)
        st1 = sb("st1", [128, kvW + 16], BF16)
        sr0 = sb("sr0", [128, kvW + 16], BF16)
        sr1 = sb("sr1", [128, kvW + 16], BF16)
        if ctx is not None:
            ppool = ctx["pf32"]
            p = ctx["p"]
        else:
            ppool = [stk.enter_context(nc.psum_tensor(f"pp{i}", [128, 512], F32)) for i in range(8)]
            block = stk.enter_context(nc.Block())
            p = Prog(nc)
            p.make_sems()
        pool = PsumPool(p, ppool)

        p.emit("vector", lambda v: v.memset(ones_col[:, :], 1.0))
        c_ones = p.emit("vector", lambda v: v.memset(ones_row[:, :], 1.0), inc=True)

        xtoks = []
        for kc in range(NKC):
            tk = p.emit("gpsimd", lambda g, kc=kc: g.dma_start(
                out=xq[:, kc * xq_cols: kc * xq_cols + kvW], in_=xfm[kc * 128:(kc + 1) * 128, 0:kvW]),
                inc=True, dma="xin")
            xtoks.append(tk)
        t_wte = p.emit("gpsimd", lambda g: g.dma_start(out=wteb[:, :], in_=wte), inc=True, dma="xin")
        t_cos = p.emit("gpsimd", lambda g: g.dma_start(out=cosb[:, 0:kvW], in_=cosd[:, 0:kvW]), inc=True, dma="xin")
        t_ssin = p.emit("gpsimd", lambda g: g.dma_start(out=ssinb[:, 0:kvW], in_=ssind[:, 0:kvW]), inc=True, dma="xin")
        xin_all = p.chan_total("xin")
        t_wte = t_cos = t_ssin = xin_all
        wte_toks = []
        for kc in range(NKC):
            tk = p.emit("vector", lambda v, kc=kc: v.tensor_copy(
                xq[:, kc * xq_cols + kvW: kc * xq_cols + kvW + 10],
                wteb[:, kc * 10:(kc + 1) * 10]), waits=[t_wte], inc=(kc == NKC - 1))
            wte_toks.append(tk)

        ln_toks = emit_ln(p, pool, xq, xq_cols, kvW, am, bm, sq_scr, rowbuf,
                          ones_col, ones_row, xin_all)
        ln_done = ln_toks[-1]
        wte_done = wte_toks[-1]

        out_toks = []
        for kc in range(NKC):
            tk = p.emit("sync", lambda s, kc=kc: s.dma_start(
                out=xn_out[kc][:, 0:qW], in_=xq[:, kc * xq_cols + q0: kc * xq_cols + q0 + qW]),
                waits=[ln_toks[kc]], inc=True, dma="xout")
            out_toks.append(tk)

        wbufs = [wb0, wb1]
        stages = [(st0, sr0), (st1, sr1)]
        wload_tok = {}
        mm_last = {}
        raw_free = [None, None]
        rope_free = [None, None]
        for oc in (range(96) if qkv_enable else []):
            kind = "q" if oc < 32 else ("k" if oc < 64 else "v")
            head = oc % 32
            wbuf = wbufs[oc % 2]
            wload_tok[oc] = p.emit("gpsimd", lambda g, oc=oc, wbuf=wbuf: g.dma_start(
                out=wbuf[:, :], in_=wqkv[oc]), waits=[mm_last.get(oc - 2)], inc=True, dma="w%d" % (oc % 2))

            if kind == "q":
                col0, width = q0, qW
            else:
                col0, width = 0, kvW + 10
            tl = tiles_of(width)
            if len(tl) >= 2 and tl[-1][1] + tl[-2][1] <= 512:
                t0a, w0a = tl[-2]
                tl = tl[:-2] + [(t0a, w0a + tl[-1][1])]
            banks = pool.acquire(len(tl))
            stg_raw, stg_rope = stages[oc % 2]
            sfree = raw_free[oc % 2]
            last = None
            evac_done = None
            for ti, (t0, tw) in enumerate(tl):
                bi, ps, wtok = banks[ti]
                for kc in range(NKC):
                    rhs = xq[:, kc * xq_cols + col0 + t0: kc * xq_cols + col0 + t0 + tw]
                    w_first = [wload_tok[oc], ln_done, wte_done, wtok] if kc == 0 else ()
                    last = p.emit("tensor", lambda t, ps=ps, wbuf=wbuf, kc=kc, rhs=rhs, tw=tw:
                                  t.matmul(ps[:, 0:tw], wbuf[:, kc * 128:(kc + 1) * 128], rhs,
                                           start=(kc == 0), stop=(kc == NKC - 1)),
                                  waits=w_first, inc=(kc == NKC - 1))
                etok = p.emit("scalar", lambda s, ps=ps, t0=t0, tw=tw, stg_raw=stg_raw:
                              s.copy(stg_raw[:, t0:t0 + tw], ps[:, 0:tw]),
                              waits=[last, sfree], inc=True)
                pool.release(bi, etok)
                evac_done = etok
            mm_last[oc] = last

            if kind == "v":
                dtok = p.emit("sync", lambda s, head=head, stg_raw=stg_raw, width=width: s.dma_start(
                    out=v_out[head][:, 0:width], in_=stg_raw[:, 0:width]), waits=[evac_done], inc=True, dma="st%d" % (oc % 2))
                raw_free[oc % 2] = dtok
                out_toks.append(dtok)
            else:
                wW = qW if kind == "q" else kvW
                cs0 = q0 if kind == "q" else 0
                coss = cosb[:, cs0:cs0 + wW]
                ssins = ssinb[:, cs0:cs0 + wW]
                rfree = rope_free[oc % 2]
                p.emit("vector", lambda v, sr=stg_rope, st=stg_raw, coss=coss, wW=wW:
                       v.tensor_mul(sr[:, 0:wW], st[:, 0:wW], coss),
                       waits=[evac_done, t_cos, t_ssin, rfree])
                p.emit("vector", lambda v, st=stg_raw, ssins=ssins, wW=wW:
                       v.tensor_mul(ropet[0:64, 0:wW], st[64:128, 0:wW], ssins[64:128, 0:wW]))
                p.emit("vector", lambda v, st=stg_raw, ssins=ssins, wW=wW:
                       v.tensor_mul(ropet[64:128, 0:wW], st[0:64, 0:wW], ssins[0:64, 0:wW]))
                if kind == "k":
                    p.emit("vector", lambda v, sr=stg_rope, st=stg_raw:
                           v.tensor_copy(sr[:, kvW:kvW + 10], st[:, kvW:kvW + 10]))
                rtok = p.emit("vector", lambda v, sr=stg_rope, wW=wW:
                              v.tensor_add(sr[:, 0:wW], sr[:, 0:wW], ropet[:, 0:wW]), inc=True)
                raw_free[oc % 2] = rtok
                dwidth = qW if kind == "q" else kvW + 10
                dst = q_out[head] if kind == "q" else k_out[head]
                dtok = p.emit("sync", lambda s, dst=dst, sr=stg_rope, dwidth=dwidth: s.dma_start(
                    out=dst[:, 0:dwidth], in_=sr[:, 0:dwidth]), waits=[rtok], inc=True, dma="st%d" % (oc % 2))
                rope_free[oc % 2] = dtok
                out_toks.append(dtok)

        if ctx is None:
            p.finalize(block, final_waits=[p.chan_total("xout"), p.chan_total("st0"), p.chan_total("st1")])
    return nc


def build_s2(parity, ctx=None):
    P = PARAMS[parity]
    kvW, q0, qW = P["kvW"], P["q0"], P["qW"]
    KT = kvW // 128
    if ctx is not None:
        nc = ctx["nc"]
        _D = lambda name, shape, dt, kind: ctx["d"][name]
    else:
        nc = bass.Bass("TRN2", target_bir_lowering=False)
        _D = lambda name, shape, dt, kind: nc.dram_tensor(name, shape, dt, kind=kind).ap()

    q_in = _D("q_out", [32, 128, qW], BF16, "ExternalInput")
    k_in = _D("k_out", [32, 128, kvW + 10], BF16, "ExternalInput")
    v_in = _D("v_out", [32, 128, kvW + 10], BF16, "ExternalInput")
    gat = _D("gating", [1, 1], F32, "ExternalInput")
    ident = _D("ident", [128, 128], BF16, "ExternalInput")
    y_out = _D("y_out", [32, 128, qW], BF16, "ExternalOutput")

    ctxsfx = "_s2"
    from contextlib import ExitStack
    with ExitStack() as stk:
        if ctx is not None:
            def sb(name, shape, dt):
                if name not in ctx["bufs"]:
                    ctx["bufs"][name] = ctx["stk"].enter_context(nc.sbuf_tensor(name + ctxsfx, shape, dt))
                return ctx["bufs"][name]
        else:
            sb = lambda name, shape, dt: stk.enter_context(nc.sbuf_tensor(name, shape, dt))
        kb = [sb("kb0", [128, kvW + 10], BF16), sb("kb1", [128, kvW + 10], BF16)]
        vb = [sb("vb0", [128, kvW + 10], BF16), sb("vb1", [128, kvW + 10], BF16)]
        qb = [sb("qb0", [128, 1280], BF16), sb("qb1", [128, 1280], BF16)]
        vtm = [sb("vtm0", [128, kvW], BF16), sb("vtm1", [128, kvW], BF16)]
        avtm = [sb("avtm0", [10, 128], BF16), sb("avtm1", [10, 128], BF16)]
        expt = sb("expt", [128, 16 * 512 * 2], BF16)
        expa = sb("expa", [10, 1024], BF16)
        rowbuf = sb("rows", [1, 3072], F32)
        smv = sb("smv", [10, 1024], F32)
        rmv = sb("rmv", [128, 1024], BF16)
        ystage = [sb("yst0", [128, 1280], BF16), sb("yst1", [128, 1280], BF16)]
        onec = sb("onec", [128, 1], BF16)
        oner = sb("oner", [1, 128], F32)
        gatb = sb("gatb", [1, 1], F32)
        idb = sb("idb", [128, 128], BF16)
        if ctx is not None:
            pool_rot = ctx["pf32"][0:2]
            pool_y = ctx["pf32"][2:4]
            pool_sm = ctx["pf32"][4:6]
            ptrb = ctx["pbf"]
            p = ctx["p"]
        else:
            pool_rot = [stk.enter_context(nc.psum_tensor(f"pr{i}", [128, 512], F32)) for i in range(2)]
            pool_y = [stk.enter_context(nc.psum_tensor(f"py{i}", [128, 512], F32)) for i in range(2)]
            pool_sm = [stk.enter_context(nc.psum_tensor(f"pm{i}", [128, 512], F32)) for i in range(2)]
            ptrb = [stk.enter_context(nc.psum_tensor(f"pt{i}", [128, 1024], BF16)) for i in range(2)]
            block = stk.enter_context(nc.Block())
            p = Prog(nc)
            p.make_sems()
        rot = PsumPool(p, pool_rot)

        p.emit("vector", lambda v: v.memset(onec[:, :], 1.0))
        p.emit("vector", lambda v: v.memset(oner[:, :], 1.0), inc=True)
        p.emit("gpsimd", lambda g: g.dma_start(out=gatb[:, :], in_=gat), inc=True, dma="xin")
        p.emit("gpsimd", lambda g: g.dma_start(out=idb[:, :], in_=ident), inc=True, dma="xin")
        xin_all = p.chan_total("xin")

        SCALE_M = 1.0 / float(np.sqrt(128.0))
        SCALE_A = 1.0 / 64.0
        head_done = {}
        tr_tok = [None, None]
        avp_tok = [None, None]
        ybank_tok = [None, None]
        sm_tok = [None, None]
        ystage_free = [None, None]
        pv_tok = [None, None]
        g_tok = [None, None]
        rb_tok = [None, None]
        for h in range(32):
            par = h % 2
            hd2 = head_done.get(h - 2)
            ld_k = p.emit("sync", lambda s, h=h, b=kb[par]: s.dma_start(out=b[:, 0:kvW + 10], in_=k_in[h][:, 0:kvW + 10]),
                          waits=[hd2], inc=True, dma="kld%d" % par)
            ld_v = p.emit("sync", lambda s, h=h, b=vb[par]: s.dma_start(out=b[:, 0:kvW + 10], in_=v_in[h][:, 0:kvW + 10]),
                          waits=[hd2], inc=True, dma="vld%d" % par)
            ld_q = p.emit("sync", lambda s, h=h, b=qb[par]: s.dma_start(out=b[:, 0:qW], in_=q_in[h][:, 0:qW]),
                          waits=[hd2], inc=True, dma="qld%d" % par)
            for kt in range(KT):
                bk = kt % 2
                ttok = p.emit("tensor", lambda t, b=vb[par], kt=kt, o=ptrb[bk][:, 0:128]:
                              t.transpose(o, b[:, kt * 128:(kt + 1) * 128], idb[:, :]),
                              waits=[ld_v, xin_all, tr_tok[bk], hd2], inc=True)
                tr_tok[bk] = p.emit("vector", lambda v, o=vtm[par], kt=kt, i=ptrb[bk][:, 0:128]:
                                    v.tensor_copy(o[:, kt * 128:(kt + 1) * 128], i), waits=[ttok], inc=True)
            bk = KT % 2
            ttok = p.emit("tensor", lambda t, b=vb[par], o=ptrb[bk][0:10, 0:128]:
                          t.transpose(o, b[:, kvW:kvW + 10], idb[:, :]),
                          waits=[ld_v, tr_tok[bk], avp_tok[bk]], inc=True)
            av_ev = p.emit("vector", lambda v, o=avtm[par], i=ptrb[bk][0:10, 0:128]:
                           v.tensor_copy(o[:, :], i), waits=[ttok], inc=True)
            avp_tok[bk] = av_ev
            vt_done = av_ev

            last_pva = None
            yev = None
            for (qc0, qcW) in tiles_of(qW):
                qa0 = q0 + qc0
                qcp = (qc0 // 512) % 2
                kt_hi = (qa0 + qcW - 1) // 128 + 1
                smb = pool_sm[qcp]
                ybank = pool_y[qcp]
                EB = qcp * 16 * 512
                AB = qcp * 512
                RB = qcp * 1536
                sum_last = None
                for kt in range(kt_hi):
                    bi, ps, wtok = rot.acquire(1)[0]
                    mm = p.emit("tensor", lambda t, ps=ps, b=kb[par], kt=kt, q=qb[par], qc0=qc0, qcW=qcW:
                                t.matmul(ps[:, 0:qcW], b[:, kt * 128:(kt + 1) * 128], q[:, qc0:qc0 + qcW],
                                         start=True, stop=True),
                                waits=[ld_k, ld_q, wtok], inc=True)
                    etok = p.emit("scalar", lambda s, ps=ps, kt=kt, qcW=qcW:
                                  s.activation(expt[:, EB + kt * 512: EB + kt * 512 + qcW], ps[:, 0:qcW], AF.Exp,
                                               scale=SCALE_M),
                                  waits=[mm, pv_tok[qcp], g_tok[qcp]], inc=True)
                    rot.release(bi, etok)
                    if kt * 128 + 127 >= qa0:
                        etok = p.emit("gpsimd", lambda g, kt=kt, qcW=qcW, qa0=qa0:
                                      g.affine_select(expt[:, EB + kt * 512: EB + kt * 512 + qcW],
                                                      expt[:, EB + kt * 512: EB + kt * 512 + qcW],
                                                      [[1, qcW]], ALU.is_ge, 0.0,
                                                      base=qa0 - kt * 128, channel_multiplier=-1),
                                      waits=[etok], inc=True)
                        g_tok[qcp] = etok
                    sum_last = p.emit("tensor", lambda t, ps=smb, kt=kt, qcW=qcW, first=(kt == 0), last=(kt == kt_hi - 1):
                                      t.matmul(ps[0:1, 0:qcW], onec[:, 0:1], expt[:, EB + kt * 512: EB + kt * 512 + qcW],
                                               start=first, stop=last),
                                      waits=[etok, sm_tok[qcp] if kt == 0 else None], inc=(kt == kt_hi - 1))
                    p.emit("tensor", lambda t, ps=ybank, kt=kt, qcW=qcW, par=par, first=(kt == 0):
                           t.matmul(ps[:, 0:qcW], vtm[par][:, kt * 128:(kt + 1) * 128],
                                    expt[:, EB + kt * 512: EB + kt * 512 + qcW], start=first, stop=False),
                           waits=[vt_done, etok, ybank_tok[qcp] if kt == 0 else None])
                bi, psa, wtok = rot.acquire(1)[0]
                mma = p.emit("tensor", lambda t, ps=psa, b=kb[par], q=qb[par], qc0=qc0, qcW=qcW:
                             t.matmul(ps[0:10, 0:qcW], b[:, kvW:kvW + 10], q[:, qc0:qc0 + qcW],
                                      start=True, stop=True), waits=[wtok], inc=True)
                eta = p.emit("scalar", lambda s, ps=psa, qcW=qcW:
                             s.activation(expa[0:10, AB:AB + qcW], ps[0:10, 0:qcW], AF.Exp, scale=SCALE_A),
                             waits=[mma, pv_tok[qcp]], inc=True)
                rot.release(bi, eta)
                bi2, psas, wtok2 = rot.acquire(1)[0]
                asum = p.emit("tensor", lambda t, ps=psas, qcW=qcW:
                              t.matmul(ps[0:1, 0:qcW], onec[0:10, 0:1], expa[0:10, AB:AB + qcW],
                                       start=True, stop=True), waits=[eta, wtok2], inc=True)
                r_as = p.emit("vector", lambda v, i=psas, qcW=qcW: v.reciprocal(rowbuf[0:1, RB:RB + qcW], i[0:1, 0:qcW]),
                              waits=[asum, rb_tok[qcp]], inc=True)
                rot.release(bi2, r_as)
                p.emit("vector", lambda v, i=smb, qcW=qcW:
                       v.tensor_mul(rowbuf[0:1, RB + 512:RB + 512 + qcW], i[0:1, 0:qcW], rowbuf[0:1, RB:RB + qcW]),
                       waits=[sum_last, xin_all])
                rS = p.emit("vector", lambda v, qcW=qcW:
                            v.tensor_scalar_mul(rowbuf[0:1, RB + 512:RB + 512 + qcW], rowbuf[0:1, RB + 512:RB + 512 + qcW], gatb[0:1, 0:1]),
                            inc=True)
                rR = p.emit("vector", lambda v, i=smb, qcW=qcW:
                            v.reciprocal(rowbuf[0:1, RB + 1024:RB + 1024 + qcW], i[0:1, 0:qcW]),
                            waits=[sum_last], inc=True)
                sm_tok[qcp] = rR
                bi3, psm, wtok3 = rot.acquire(1)[0]
                mS = p.emit("tensor", lambda t, ps=psm, qcW=qcW:
                            t.matmul(ps[0:10, 0:qcW], oner[0:1, 0:10], rowbuf[0:1, RB + 512:RB + 512 + qcW],
                                     start=True, stop=True), waits=[rS, wtok3], inc=True)
                eS = p.emit("vector", lambda v, i=psm, qcW=qcW: v.tensor_copy(smv[0:10, AB:AB + qcW], i[0:10, 0:qcW]),
                            waits=[mS], inc=True)
                rot.release(bi3, eS)
                sc = p.emit("vector", lambda v, qcW=qcW:
                            v.tensor_mul(expa[0:10, AB:AB + qcW], expa[0:10, AB:AB + qcW], smv[0:10, AB:AB + qcW]), inc=True)
                last_pva = p.emit("tensor", lambda t, ps=ybank, par=par, qcW=qcW:
                                  t.matmul(ps[:, 0:qcW], avtm[par][:, :], expa[0:10, AB:AB + qcW],
                                           start=False, stop=True), waits=[av_ev, sc], inc=True)
                bi4, psr, wtok4 = rot.acquire(1)[0]
                mR = p.emit("tensor", lambda t, ps=psr, qcW=qcW:
                            t.matmul(ps[:, 0:qcW], oner[0:1, 0:128], rowbuf[0:1, RB + 1024:RB + 1024 + qcW],
                                     start=True, stop=True), waits=[rR, wtok4], inc=True)
                eR = p.emit("vector", lambda v, i=psr, qcW=qcW: v.tensor_copy(rmv[:, AB:AB + qcW], i[:, 0:qcW]),
                            waits=[mR], inc=True)
                rot.release(bi4, eR)
                rb_tok[qcp] = mR
                yev = p.emit("vector", lambda v, ps=ybank, par=par, qc0=qc0, qcW=qcW:
                             v.tensor_mul(ystage[par][:, qc0:qc0 + qcW], ps[:, 0:qcW], rmv[:, AB:AB + qcW]),
                             waits=[last_pva, eR, ystage_free[par]], inc=True)
                ybank_tok[qcp] = yev
                pv_tok[qcp] = last_pva
            dtok = p.emit("sync", lambda s, h=h, par=par: s.dma_start(out=y_out[h][:, 0:qW], in_=ystage[par][:, 0:qW]),
                          waits=[yev], inc=True, dma="yo%d" % par)
            ystage_free[par] = dtok
            head_done[h] = last_pva
        if ctx is None:
            p.finalize(block, final_waits=[p.chan_total("yo0"), p.chan_total("yo1")])
    return nc


def build_s3(parity, ctx=None):
    P = PARAMS[parity]
    q0, qW = P["q0"], P["qW"]
    if ctx is not None:
        nc = ctx["nc"]
        _D = lambda name, shape, dt, kind: ctx["d"][name]
    else:
        nc = bass.Bass("TRN2", target_bir_lowering=False)
        _D = lambda name, shape, dt, kind: nc.dram_tensor(name, shape, dt, kind=kind).ap()

    y_in = _D("y_out", [32, 128, qW], BF16, "ExternalInput")
    xn_in = _D("xn_own", [NKC, 128, qW], BF16, "ExternalInput")
    wp_in = _D("w_proj_img", [32, 128, DM], F32, "ExternalInput")
    qw_in = _D("qw_img", [32, 128, DM], I32, "ExternalInput")
    sr_in = _D("s_rep", [32, 128, DM], F32, "ExternalInput")
    z16_in = _D("z16", [16, DM], F32, "ExternalInput")
    s16_in = _D("s16", [16, DM], F32, "ExternalInput")
    gim_in = _D("gimg", [128, NKC * 16], F32, "ExternalInput")
    b0_in = _D("bias0", [128, 32], F32, "ExternalInput")
    xq_in = _D("xfm_q", [DM, qW], F32, "ExternalInput")
    x1_out = _D("x1", [NKC, 128, qW], F32, "ExternalOutput")

    passes = [(0, 640), (640, qW - 640)]

    ctxsfx = "_s3"
    from contextlib import ExitStack
    with ExitStack() as stk:
        if ctx is not None:
            def sb(name, shape, dt):
                if name not in ctx["bufs"]:
                    ctx["bufs"][name] = ctx["stk"].enter_context(nc.sbuf_tensor(name + ctxsfx, shape, dt))
                return ctx["bufs"][name]
        else:
            sb = lambda name, shape, dt: stk.enter_context(nc.sbuf_tensor(name, shape, dt))
        pW = max(w for (_, w) in passes)
        ysb = sb("ysb", [128, NKC * pW], BF16)
        xnsb = sb("xnsb", [128, NKC * pW], BF16)
        wpb = [sb("wpb0", [128, DM], BF16), sb("wpb1", [128, DM], BF16)]
        qwb = [sb("qwb0", [128, DM], F16), sb("qwb1", [128, DM], F16)]
        srb = [sb("srb0", [128, DM], F16), sb("srb1", [128, DM], F16)]
        wdq = [sb("wdq0", [128, DM], BF16), sb("wdq1", [128, DM], BF16)]
        z16b = sb("z16b", [16, DM], BF16)
        s16b = sb("s16b", [16, DM], BF16)
        zsneg = sb("zsneg", [16, DM], BF16)
        gimb = sb("gimb", [128, NKC * 16], BF16)
        gsum = sb("gsum", [16, 1280], BF16)
        b0b = sb("b0b", [128, 32], F32)
        xts = [sb(f"xt{i}", [128, 512], F32) for i in range(4)]
        x1s = [sb(f"x1s{i}", [128, 512], F32) for i in range(4)]
        if ctx is not None:
            ppool = ctx["pf32"]
            p = ctx["p"]
        else:
            ppool = [stk.enter_context(nc.psum_tensor(f"pp{i}", [128, 512], F32)) for i in range(8)]
            block = stk.enter_context(nc.Block())
            p = Prog(nc)
            p.make_sems()
        pool = PsumPool(p, ppool)

        p.emit("gpsimd", lambda g: g.dma_start(out=z16b[:, :], in_=z16_in), inc=True, dma="tab")
        p.emit("gpsimd", lambda g: g.dma_start(out=s16b[:, :], in_=s16_in), inc=True, dma="tab")
        p.emit("gpsimd", lambda g: g.dma_start(out=gimb[:, :], in_=gim_in), inc=True, dma="tab")
        p.emit("gpsimd", lambda g: g.dma_start(out=b0b[:, :], in_=b0_in), inc=True, dma="tab")
        tab_all = p.chan_total("tab")
        p.emit("vector", lambda v: v.tensor_mul(zsneg[:, :], z16b[:, :], s16b[:, :]), waits=[tab_all])
        zs_tok = p.emit("vector", lambda v: v.tensor_scalar_mul(zsneg[:, :], zsneg[:, :], -1.0), inc=True)

        wl_tok = {}
        mm_last = {}
        dq_tok = {}
        xt_tok = [None] * 4
        x1_tok = [None] * 4
        slot = 0
        for pi, (p0, pw) in enumerate(passes):
            ych = "yin%d" % pi
            for kc in range(NKC):
                p.emit("sync", lambda s, kc=kc, p0=p0, pw=pw: s.dma_start(
                    out=ysb[:, kc * pW: kc * pW + pw], in_=y_in[kc][:, p0:p0 + pw]),
                    waits=[mm_last.get(("pass_end", pi - 1))], inc=True, dma=ych)
                p.emit("sync", lambda s, kc=kc, p0=p0, pw=pw: s.dma_start(
                    out=xnsb[:, kc * pW: kc * pW + pw], in_=xn_in[kc][:, p0:p0 + pw]),
                    inc=True, dma=ych)
            act_all = p.chan_total(ych)

            gs_toks = []
            for (t0, tw) in tiles_of(pw):
                bi, ps, wtok = pool.acquire(1)[0]
                last = None
                for kc in range(NKC):
                    last = p.emit("tensor", lambda t, ps=ps, kc=kc, t0=t0, tw=tw:
                                  t.matmul(ps[0:16, 0:tw], gimb[:, kc * 16:(kc + 1) * 16],
                                           xnsb[:, kc * pW + t0: kc * pW + t0 + tw],
                                           start=(kc == 0), stop=(kc == NKC - 1)),
                                  waits=[act_all, tab_all, wtok] if kc == 0 else (), inc=(kc == NKC - 1))
                ev = p.emit("vector", lambda v, ps=ps, t0=t0, tw=tw, p0=p0:
                            v.tensor_copy(gsum[0:16, p0 + t0: p0 + t0 + tw], ps[0:16, 0:tw]),
                            waits=[last], inc=True)
                pool.release(bi, ev)
                gs_toks.append(ev)
            gs_done = gs_toks[-1]

            for oci in range(32):
                oc = oci
                gi = pi * 32 + oci
                wb = wpb[gi % 2]
                qb_ = qwb[gi % 2]
                sr_ = srb[gi % 2]
                dq_ = wdq[gi % 2]
                prev = mm_last.get(gi - 2)
                wl = p.emit("gpsimd", lambda g, oc=oc, wb=wb: g.dma_start(out=wb[:, :], in_=wp_in[oc]),
                            waits=[prev], inc=True, dma="wp%d" % (gi % 2))
                ql = p.emit("gpsimd", lambda g, oc=oc, qb_=qb_: g.dma_start(out=qb_[:, :], in_=qw_in[oc]),
                            waits=[prev], inc=True, dma="qw%d" % (gi % 2))
                sl = p.emit("gpsimd", lambda g, oc=oc, sr_=sr_: g.dma_start(out=sr_[:, :], in_=sr_in[oc]),
                            waits=[prev], inc=True, dma="sr%d" % (gi % 2))
                deq_eng = "vector" if oci % 2 == 0 else "gpsimd"
                dq_tok[gi] = p.emit(deq_eng, lambda e, qb_=qb_, sr_=sr_, dq_=dq_:
                                    e.tensor_mul(dq_[:, :], qb_[:, :], sr_[:, :]),
                                    waits=[ql, sl, mm_last.get(gi - 2)], inc=True)
                last = None
                for (t0, tw) in tiles_of(pw):
                    bi, ps, wtok = pool.acquire(1)[0]
                    sl_x = slot % 4
                    xt = p.emit("sync", lambda s, oc=oc, t0=t0, tw=tw, b=xts[sl_x], p0=p0:
                                s.dma_start(out=b[:, 0:tw], in_=xq_in[oc * 128:(oc + 1) * 128, p0 + t0:p0 + t0 + tw]),
                                waits=[xt_tok[sl_x]], inc=True, dma="xt%d" % sl_x)
                    last = p.emit("tensor", lambda t, ps=ps, oc=oc, t0=t0, tw=tw, p0=p0:
                                  t.matmul(ps[:, 0:tw], zsneg[:, oc * 128:(oc + 1) * 128],
                                           gsum[0:16, p0 + t0: p0 + t0 + tw], start=True, stop=False),
                                  waits=[zs_tok, gs_done, wtok], inc=False)
                    for kc in range(NKC):
                        p.emit("tensor", lambda t, ps=ps, kc=kc, t0=t0, tw=tw, dq_=dq_:
                               t.matmul(ps[:, 0:tw], dq_[:, kc * 128:(kc + 1) * 128],
                                        xnsb[:, kc * pW + t0: kc * pW + t0 + tw], start=False, stop=False),
                               waits=[dq_tok[gi]] if kc == 0 else ())
                    for kc in range(NKC):
                        last = p.emit("tensor", lambda t, ps=ps, kc=kc, t0=t0, tw=tw, wb=wb:
                                      t.matmul(ps[:, 0:tw], wb[:, kc * 128:(kc + 1) * 128],
                                               ysb[:, kc * pW + t0: kc * pW + t0 + tw],
                                               start=False, stop=(kc == NKC - 1)),
                                      waits=[wl] if kc == 0 else (), inc=(kc == NKC - 1))
                    ev = p.emit("vector", lambda v, ps=ps, oc=oc, tw=tw, b=xts[sl_x], o=x1s[sl_x]:
                                v.scalar_tensor_tensor(o[:, 0:tw], ps[:, 0:tw], b0b[:, oc:oc + 1],
                                                       b[:, 0:tw], op0=ALU.add, op1=ALU.add),
                                waits=[last, xt, x1_tok[sl_x]], inc=True)
                    pool.release(bi, ev)
                    xt_tok[sl_x] = ev
                    dt = p.emit("sync", lambda s, oc=oc, t0=t0, tw=tw, o=x1s[sl_x], p0=p0:
                                s.dma_start(out=x1_out[oc][:, p0 + t0: p0 + t0 + tw], in_=o[:, 0:tw]),
                                waits=[ev], inc=True, dma="x1o%d" % sl_x)
                    x1_tok[sl_x] = dt
                    slot += 1
                mm_last[gi] = last
            mm_last[("pass_end", pi)] = mm_last[pi * 32 + 31]
        if ctx is None:
            p.finalize(block, final_waits=[p.chan_total("x1o%d" % i) for i in range(4)])
    return nc


def build_s4a(parity, ctx=None):
    P = PARAMS[parity]
    q0, qW, own0, ownW = P["q0"], P["qW"], P["own0"], P["ownW"]
    stride = ownW + 2
    lo = 1 if own0 == 0 else 0
    ln_cnt = ownW + 1
    x1_lo = own0 - q0 - 1 + lo
    if ctx is not None:
        nc = ctx["nc"]
        _D = lambda name, shape, dt, kind: ctx["d"][name]
    else:
        nc = bass.Bass("TRN2", target_bir_lowering=False)
        _D = lambda name, shape, dt, kind: nc.dram_tensor(name, shape, dt, kind=kind).ap()

    x1_in = _D("x1", [NKC, 128, qW], F32, "ExternalInput")
    cw_in = _D("conv_img", [24, 128, 3 * DM], F32, "ExternalInput")
    cb_in = _D("conv_b", [128, 24], F32, "ExternalInput")
    qkv1_out = _D("qkv1", [24, 128, ownW], BF16, "ExternalOutput")
    xn1_out = _D("xn1_own", [NKC, 128, ownW], BF16, "ExternalOutput")

    ctxsfx = "_s4a"
    from contextlib import ExitStack
    with ExitStack() as stk:
        if ctx is not None:
            def sb(name, shape, dt):
                if name not in ctx["bufs"]:
                    ctx["bufs"][name] = ctx["stk"].enter_context(nc.sbuf_tensor(name + ctxsfx, shape, dt))
                return ctx["bufs"][name]
        else:
            sb = lambda name, shape, dt: stk.enter_context(nc.sbuf_tensor(name, shape, dt))
        xn1 = sb("xn1", [128, NKC * 1154], BF16)
        cwb = [sb("cwb0", [128, 3 * DM], BF16), sb("cwb1", [128, 3 * DM], BF16)]
        cbb = sb("cbb", [128, 24], F32)
        am = sb("am", [128, 1153], BF16)
        bm = sb("bm", [128, 1153], BF16)
        sq_scr = sb("sq", [128, 1024], BF16)
        rowbuf = sb("rows", [1, 6 * 512], F32)
        onec = sb("onec", [128, 1], BF16)
        oner = sb("oner", [1, 128], F32)
        qst = [sb(f"qst{i}", [128, 512], BF16) for i in range(4)]
        if ctx is not None:
            ppool = ctx["pf32"]
            p = ctx["p"]
        else:
            ppool = [stk.enter_context(nc.psum_tensor(f"pp{i}", [128, 512], F32)) for i in range(8)]
            block = stk.enter_context(nc.Block())
            p = Prog(nc)
            p.make_sems()
        pool = PsumPool(p, ppool)

        p.emit("vector", lambda v: v.memset(onec[:, :], 1.0))
        p.emit("vector", lambda v: v.memset(oner[:, :], 1.0), inc=True)
        padc = 0 if own0 == 0 else ownW + 1
        zp = []
        for kc in range(NKC):
            zp.append(p.emit("vector", lambda v, kc=kc: v.memset(
                xn1[:, kc * stride + padc: kc * stride + padc + 1], 0.0), inc=(kc == NKC - 1)))
        pad_tok = zp[-1]
        p.emit("gpsimd", lambda g: g.dma_start(out=cbb[:, :], in_=cb_in), inc=True, dma="tab")
        tab_all = p.chan_total("tab")
        for kc in range(NKC):
            p.emit("gpsimd", lambda g, kc=kc: g.dma_start(
                out=xn1[:, kc * stride + lo: kc * stride + lo + ln_cnt],
                in_=x1_in[kc][:, x1_lo: x1_lo + ln_cnt]), inc=True, dma="xin")
        xin_all = p.chan_total("xin")

        ln_toks = emit_ln(p, pool, xn1, stride, ln_cnt, am, bm, sq_scr, rowbuf,
                          onec, oner, xin_all, col0=lo)
        ln_done = ln_toks[-1]

        for kc in range(NKC):
            p.emit("sync", lambda s, kc=kc: s.dma_start(
                out=xn1_out[kc][:, 0:ownW], in_=xn1[:, kc * stride + 1: kc * stride + 1 + ownW]),
                waits=[ln_toks[kc]], inc=True, dma="xno")

        wl_tok = {}
        mm_last = {}
        st_tok = [None] * 4
        slot = 0
        for oc in range(24):
            wb = cwb[oc % 2]
            wl = p.emit("gpsimd", lambda g, oc=oc, wb=wb: g.dma_start(out=wb[:, :], in_=cw_in[oc]),
                        waits=[mm_last.get(oc - 2)], inc=True, dma="cw%d" % (oc % 2))
            last = None
            for (t0, tw) in tiles_of(ownW):
                bi, ps, wtok = pool.acquire(1)[0]
                for w in range(3):
                    for kc in range(NKC):
                        first = (w == 0 and kc == 0)
                        last = p.emit("tensor", lambda t, ps=ps, w=w, kc=kc, t0=t0, tw=tw, wb=wb:
                                      t.matmul(ps[:, 0:tw], wb[:, (w * NKC + kc) * 128:(w * NKC + kc + 1) * 128],
                                               xn1[:, kc * stride + t0 + w: kc * stride + t0 + w + tw],
                                               start=(w == 0 and kc == 0), stop=(w == 2 and kc == NKC - 1)),
                                      waits=[wl, ln_done, pad_tok, wtok] if first else (),
                                      inc=(w == 2 and kc == NKC - 1))
                sl = slot % 4
                ev = p.emit("scalar", lambda s, ps=ps, oc=oc, tw=tw, o=qst[sl]:
                            s.add(o[:, 0:tw], ps[:, 0:tw], cbb[:, oc:oc + 1]),
                            waits=[last, tab_all, st_tok[sl]], inc=True)
                pool.release(bi, ev)
                dt = p.emit("sync", lambda s, oc=oc, t0=t0, tw=tw, o=qst[sl]:
                            s.dma_start(out=qkv1_out[oc][:, t0:t0 + tw], in_=o[:, 0:tw]),
                            waits=[ev], inc=True, dma="qo%d" % sl)
                st_tok[sl] = dt
                slot += 1
            mm_last[oc] = last
        if ctx is None:
            p.finalize(block, final_waits=[p.chan_total("xno")] + [p.chan_total("qo%d" % i) for i in range(4)])
    return nc


def build_s4b(parity, stop_after=None, skip=(), ctx=None):
    P = PARAMS[parity]
    ownW = P["ownW"]
    NB = ownW // 128
    if ctx is not None:
        nc = ctx["nc"]
        _D = lambda name, shape, dt, kind: ctx["d"][name]
    else:
        nc = bass.Bass("TRN2", target_bir_lowering=False)
        _D = lambda name, shape, dt, kind: nc.dram_tensor(name, shape, dt, kind=kind).ap()

    qkv1_in = _D("qkv1", [24, 128, ownW], BF16, "ExternalInput")
    wq_in = _D("wq_img", [8, 128, 1024], F32, "ExternalInput")
    wk_in = _D("wk_img", [8, 128, 1024], F32, "ExternalInput")
    wv_in = _D("wv_img", [8, 128, 1024], F32, "ExternalInput")
    wo_in = _D("wo_img", [8, 128, 1024], F32, "ExternalInput")
    bq_in = _D("bq", [128, 8], F32, "ExternalInput")
    bk_in = _D("bk", [128, 8], F32, "ExternalInput")
    bv_in = _D("bv_col", [128, 8], F32, "ExternalInput")
    bo_in = _D("bo", [128, 8], F32, "ExternalInput")
    id_in = _D("ident", [128, 128], BF16, "ExternalInput")
    o2_out = _D("o2", [8, 128, ownW], BF16, "ExternalOutput")

    ctxsfx = "_s4b"
    from contextlib import ExitStack
    with ExitStack() as stk:
        if ctx is not None:
            def sb(name, shape, dt):
                if name not in ctx["bufs"]:
                    ctx["bufs"][name] = ctx["stk"].enter_context(nc.sbuf_tensor(name + ctxsfx, shape, dt))
                return ctx["bufs"][name]
        else:
            sb = lambda name, shape, dt: stk.enter_context(nc.sbuf_tensor(name, shape, dt))
        qkv1 = sb("qkv1b", [128, 24 * 1152], BF16)
        whb = [sb("whb0", [128, 1024], BF16), sb("whb1", [128, 1024], BF16)]
        wob = sb("wob", [128, 8 * 1024], BF16)
        qh = sb("qh", [128, 8 * 1152], BF16)
        kh = sb("kh", [128, 8 * 1152], BF16)
        vh = sb("vh", [128, 8 * 1152], BF16)
        ofm = sb("ofm", [128, 8 * 1152], BF16)
        vtmb = sb("vtmb", [128, 1024], BF16)
        expq = sb("expq", [128, 512], BF16)
        r4 = sb("r4", [1, 512], F32)
        rmq = sb("rmq", [64, 512], BF16)
        wobias = sb("wobias", [128, 8], F32)
        bqb = sb("bqb", [128, 8], F32)
        bkb = sb("bkb", [128, 8], F32)
        bvb = sb("bvb", [128, 8], BF16)
        bob = sb("bob", [128, 8], F32)
        idb = sb("idb", [128, 128], BF16)
        onec = sb("onec", [128, 1], BF16)
        oner = sb("oner", [1, 128], F32)
        ost = [sb(f"ost{i}", [128, 512], BF16) for i in range(4)]
        if ctx is not None:
            pool6 = ctx["pf32"][0:6]
            ptrb = ctx["pbf"]
            p = ctx["p"]
        else:
            pool6 = [stk.enter_context(nc.psum_tensor(f"pp{i}", [128, 512], F32)) for i in range(6)]
            ptrb = [stk.enter_context(nc.psum_tensor(f"pt{i}", [128, 1024], BF16)) for i in range(2)]
            block = stk.enter_context(nc.Block())
            p = Prog(nc)
            p.make_sems()
        pool = PsumPool(p, pool6)

        p.emit("vector", lambda v: v.memset(onec[:, :], 1.0))
        p.emit("vector", lambda v: v.memset(oner[:, :], 1.0), inc=True)
        for (dst, src) in [(bqb, bq_in), (bkb, bk_in), (bvb, bv_in), (bob, bo_in), (idb, id_in)]:
            p.emit("gpsimd", lambda g, dst=dst, src=src: g.dma_start(out=dst[:, :], in_=src),
                   inc=True, dma="tab")
        for oc in range(8):
            p.emit("gpsimd", lambda g, oc=oc: g.dma_start(out=wob[:, oc * 1024:(oc + 1) * 1024], in_=wo_in[oc]),
                   inc=True, dma="tab")
        tab_all = p.chan_total("tab")
        for ch in range(24):
            p.emit("sync", lambda s, ch=ch: s.dma_start(out=qkv1[:, ch * ownW:(ch + 1) * ownW], in_=qkv1_in[ch][:, 0:ownW]),
                   inc=True, dma="qin")
        qin_all = p.chan_total("qin")

        wb_toks = []
        for oc in range(8):
            bi, ps, wtok = pool.acquire(1)[0]
            last = None
            for kc in range(8):
                last = p.emit("tensor", lambda t, ps=ps, oc=oc, kc=kc:
                              t.matmul(ps[:, 0:1], wob[:, oc * 1024 + kc * 128: oc * 1024 + (kc + 1) * 128],
                                       bvb[:, kc:kc + 1], start=(kc == 0), stop=(kc == 7)),
                              waits=[tab_all, wtok] if kc == 0 else (), inc=(kc == 7))
            ev = p.emit("vector", lambda v, ps=ps, oc=oc: v.tensor_copy(wobias[:, oc:oc + 1], ps[:, 0:1]),
                        waits=[last], inc=True)
            pool.release(bi, ev)
            wb_toks.append(ev)
        wob_tok = p.emit("vector", lambda v: v.tensor_add(wobias[:, :], wobias[:, :], bob[:, :]),
                         waits=[tab_all], inc=True)

        st_tok = [None] * 4
        proj_done_s = None
        proj_done_v = None
        gi = 0
        wl_prev = {}
        for (kind, img, dstbuf, bias) in [(0, wq_in, qh, bqb), (1, wk_in, kh, bkb), (2, wv_in, vh, None)]:
            for oc in range(8):
                wb = whb[gi % 2]
                wl = p.emit("gpsimd", lambda g, img=img, oc=oc, wb=wb: g.dma_start(out=wb[:, :], in_=img[oc]),
                            waits=[wl_prev.get(gi - 2)], inc=True, dma="wh%d" % (gi % 2))
                last = None
                for (t0, tw) in tiles_of(ownW):
                    bi, ps, wtok = pool.acquire(1)[0]
                    for kc in range(8):
                        last = p.emit("tensor", lambda t, ps=ps, kc=kc, t0=t0, tw=tw, wb=wb, kind=kind:
                                      t.matmul(ps[:, 0:tw], wb[:, kc * 128:(kc + 1) * 128],
                                               qkv1[:, (kind * 8 + kc) * ownW + t0:(kind * 8 + kc) * ownW + t0 + tw],
                                               start=(kc == 0), stop=(kc == 7)),
                                      waits=[wl, qin_all, wtok] if kc == 0 else (), inc=(kc == 7))
                    for hh in range(2):
                        h = 2 * oc + hh
                        rb = 64 * (h // 8)
                        cb = (h % 8) * ownW
                        if bias is not None:
                            ev = p.emit("scalar", lambda s, ps=ps, hh=hh, oc=oc, t0=t0, tw=tw, dstbuf=dstbuf,
                                        bias=bias, rb=rb, cb=cb:
                                        s.add(dstbuf[rb:rb + 64, cb + t0: cb + t0 + tw],
                                              ps[64 * hh:64 * hh + 64, 0:tw], bias[64 * hh:64 * hh + 64, oc:oc + 1]),
                                        waits=[last, tab_all], inc=True)
                            proj_done_s = ev
                        else:
                            ev = p.emit("vector", lambda v, ps=ps, hh=hh, oc=oc, t0=t0, tw=tw, dstbuf=dstbuf,
                                        rb=rb, cb=cb:
                                        v.tensor_copy(dstbuf[rb:rb + 64, cb + t0: cb + t0 + tw],
                                                      ps[64 * hh:64 * hh + 64, 0:tw]),
                                        waits=[last], inc=True)
                            proj_done_v = ev
                    pool.release(bi, ev)
                wl_prev[gi] = last
                gi += 1
        if stop_after == "proj" and ctx is None:
            dbg = p.emit("sync", lambda s: s.dma_start(out=o2_out[0], in_=qh[:, 0:ownW]), waits=[proj_done_s, proj_done_v], inc=True, dma="oo0")
            p.finalize(block, final_waits=[p.chan_total("oo0")])
            return nc
        tr_tok = [None, None]
        prev_quad_t = None
        prev_quad_g = None
        prev_b_pv = None
        prev_rml = None
        for b in range(NB):
            vt_evs = []
            for h in range(16):
                bk_ = h % 2
                rb = 64 * (h // 8)
                cb = (h % 8) * ownW
                ttok = p.emit("tensor", lambda t, h=h, b=b, rb=rb, cb=cb, o=ptrb[bk_][:, 0:64]:
                              t.transpose(o, vh[rb:rb + 64, cb + b * 128: cb + b * 128 + 128], idb[rb:rb + 64, rb:rb + 64]),
                              waits=[proj_done_v, tab_all, tr_tok[bk_], prev_b_pv], inc=True)
                ev = p.emit("vector", lambda v, h=h, i=ptrb[bk_][:, 0:64]:
                            v.tensor_copy(vtmb[:, h * 64:(h + 1) * 64], i), waits=[ttok], inc=True)
                tr_tok[bk_] = ev
                vt_evs.append(ev)
            vt_done = vt_evs[-1]
            if stop_after == "trans" and b == 0 and ctx is None:
                dbg = p.emit("sync", lambda s: s.dma_start(out=o2_out[0][:, 0:1024], in_=vtmb[:, :]), waits=[vt_done], inc=True, dma="oo0")
                p.finalize(block, final_waits=[p.chan_total("oo0")])
                return nc
            for qd in range(4):
                bi, psc, wtok = pool.acquire(1)[0]
                mms = None
                rbq = 64 * (qd // 2)
                for j in range(4):
                    h = 4 * qd + j
                    cb = (h % 8) * ownW
                    mms = p.emit("tensor", lambda t, ps=psc, j=j, cb=cb, rbq=rbq, b=b:
                                 t.matmul(ps[:, j * 128:(j + 1) * 128],
                                          kh[rbq:rbq + 64, cb + b * 128: cb + b * 128 + 128],
                                          qh[rbq:rbq + 64, cb + b * 128: cb + b * 128 + 128],
                                          start=True, stop=True),
                                 waits=[proj_done_s, proj_done_v, wtok] if j == 0 else (), inc=(j == 3))
                etok = p.emit("scalar", lambda s, ps=psc: s.activation(expq[:, :], ps[:, 0:512], AF.Exp, scale=0.125),
                              waits=[mms, prev_quad_t, prev_quad_g], inc=True)
                rot_rel_sc = etok
                if "mask" not in skip:
                    mtok = p.emit("gpsimd", lambda g: g.affine_select(expq[:, :], expq[:, :],
                                                                      [[0, 4], [1, 128]], ALU.is_ge, 0.0,
                                                                      base=0, channel_multiplier=-1),
                                  waits=[etok], inc=True)
                    prev_quad_g = mtok
                else:
                    mtok = etok
                bi2, pss, wtok2 = pool.acquire(1)[0]
                smm = p.emit("tensor", lambda t, ps=pss: t.matmul(ps[0:1, 0:512], onec[:, 0:1], expq[:, :],
                                                                  start=True, stop=True),
                             waits=[mtok, wtok2], inc=True)
                rtok = p.emit("vector", lambda v, ps=pss: v.reciprocal(r4[0:1, :], ps[0:1, 0:512]),
                              waits=[smm, prev_rml], inc=True)
                pool.release(bi2, rtok)
                bi3, pso, wtok3 = pool.acquire(1)[0]
                pvl = None
                for j in range(4):
                    h = 4 * qd + j
                    pvl = p.emit("tensor", lambda t, ps=pso, j=j, h=h:
                                 t.matmul(ps[0:64, j * 128:(j + 1) * 128],
                                          vtmb[:, h * 64: h * 64 + 64],
                                          expq[:, j * 128:(j + 1) * 128], start=True, stop=True),
                                 waits=[vt_done, mtok, wtok3] if j == 0 else (), inc=(j == 3))
                pool.release(bi, etok)
                bi4, psr, wtok4 = pool.acquire(1)[0]
                rml = None
                for j in ([] if "rm" in skip else range(4)):
                    rml = p.emit("tensor", lambda t, ps=psr, j=j:
                                 t.matmul(ps[0:64, j * 128:(j + 1) * 128],
                                          oner[0:1, 0:64], r4[0:1, j * 128:(j + 1) * 128],
                                          start=True, stop=True),
                                 waits=[rtok, wtok4] if j == 0 else (), inc=(j == 3))
                if "rm" in skip:
                    rml = rtok
                    erm = p.emit("vector", lambda v: v.memset(rmq[:, :], 1.0), waits=[rtok], inc=True)
                else:
                    erm = p.emit("vector", lambda v, ps=psr: v.tensor_copy(rmq[0:64, :], ps[0:64, 0:512]), waits=[rml], inc=True)
                prev_rml = rml
                pool.release(bi4, erm)
                ev2 = None
                for j in range(4):
                    h = 4 * qd + j
                    rb = 64 * (h % 2)
                    cb = (h // 2) * ownW
                    ev2 = p.emit("vector", lambda v, ps=pso, j=j, rb=rb, cb=cb, b=b:
                                 v.tensor_mul(ofm[rb:rb + 64, cb + b * 128: cb + b * 128 + 128],
                                              ps[0:64, j * 128:(j + 1) * 128], rmq[0:64, j * 128:(j + 1) * 128]),
                                 waits=[pvl, erm], inc=(j == 3))
                pool.release(bi3, ev2)
                prev_quad_t = pvl
                prev_b_pv = pvl
                ofm_tok = ev2
        if stop_after == "attn" and ctx is None:
            dbg = p.emit("sync", lambda s: s.dma_start(out=o2_out[0], in_=ofm[:, 0:ownW]), waits=[ofm_tok], inc=True, dma="oo0")
            p.finalize(block, final_waits=[p.chan_total("oo0")])
            return nc
        st_tok2 = [None] * 4
        slot = 0
        out_chans = []
        for oc in range(8):
            last = None
            for (t0, tw) in tiles_of(ownW):
                bi, ps, wtok = pool.acquire(1)[0]
                for kc in range(8):
                    last = p.emit("tensor", lambda t, ps=ps, oc=oc, kc=kc, t0=t0, tw=tw:
                                  t.matmul(ps[:, 0:tw], wob[:, oc * 1024 + kc * 128: oc * 1024 + (kc + 1) * 128],
                                           ofm[:, kc * ownW + t0: kc * ownW + t0 + tw],
                                           start=(kc == 0), stop=(kc == 7)),
                                  waits=[ofm_tok, wob_tok, wtok] if kc == 0 else (), inc=(kc == 7))
                sl = slot % 4
                ev = p.emit("scalar", lambda s, ps=ps, oc=oc, tw=tw, o=ost[sl]:
                            s.add(o[:, 0:tw], ps[:, 0:tw], wobias[:, oc:oc + 1]),
                            waits=[last, st_tok2[sl], wob_tok], inc=True)
                pool.release(bi, ev)
                dt = p.emit("sync", lambda s, oc=oc, t0=t0, tw=tw, o=ost[sl]:
                            s.dma_start(out=o2_out[oc][:, t0:t0 + tw], in_=o[:, 0:tw]),
                            waits=[ev], inc=True, dma="oo%d" % sl)
                st_tok2[sl] = dt
                slot += 1
        if ctx is None:
            p.finalize(block, final_waits=[p.chan_total("oo%d" % i) for i in range(4)])
    return nc


SORT8_NET = [(0, 1), (2, 3), (4, 5), (6, 7),
             (0, 2), (1, 3), (4, 6), (5, 7),
             (1, 2), (5, 6), (0, 4), (3, 7),
             (1, 5), (2, 6), (1, 4), (3, 6),
             (2, 4), (3, 5), (3, 4)]


def build_s5(parity, ctx=None):
    P = PARAMS[parity]
    q0, qW, own0, ownW = P["q0"], P["qW"], P["own0"], P["ownW"]
    NT = ownW // 128
    x1_own = own0 - q0
    if ctx is not None:
        nc = ctx["nc"]
        _D = lambda name, shape, dt, kind: ctx["d"][name]
    else:
        nc = bass.Bass("TRN2", target_bir_lowering=False)
        _D = lambda name, shape, dt, kind: nc.dram_tensor(name, shape, dt, kind=kind).ap()

    x1_in = _D("x1", [NKC, 128, qW], F32, "ExternalInput")
    xn_in = _D("xn1_own", [NKC, 128, ownW], BF16, "ExternalInput")
    o2_in = _D("o2", [8, 128, ownW], BF16, "ExternalInput")
    wb_in = _D("wbase_img", [32, 128, DM], F32, "ExternalInput")
    pj_in = _D("projw_img", [32, 128, 1024], F32, "ExternalInput")
    wr_in = _D("wr_img", [128, NKC * 8], F32, "ExternalInput")
    br_in = _D("br", [8, 1], F32, "ExternalInput")
    A_in = _D("A_img", [128, DM], F32, "ExternalInput")
    B_in = _D("B_sb", [128, DM], F32, "ExternalInput")
    rep_in = _D("rep", [8, 128], F32, "ExternalInput")
    bb_in = _D("bbase", [128, 32], F32, "ExternalInput")
    pb_in = _D("projb", [128, 32], F32, "ExternalInput")
    idf_in = _D("identf", [128, 128], F32, "ExternalInput")
    out_d = _D("outp", [NKC, 128, ownW], F32, "ExternalOutput")

    ctxsfx = "_s5"
    from contextlib import ExitStack
    with ExitStack() as stk:
        if ctx is not None:
            def sb(name, shape, dt):
                if name not in ctx["bufs"]:
                    ctx["bufs"][name] = ctx["stk"].enter_context(nc.sbuf_tensor(name + ctxsfx, shape, dt))
                return ctx["bufs"][name]
        else:
            sb = lambda name, shape, dt: stk.enter_context(nc.sbuf_tensor(name, shape, dt))
        xnb = sb("xnb", [128, NKC * 1152], BF16)
        o2b = sb("o2b", [128, 8 * 1152], BF16)
        wbb = [sb("wbb0", [128, DM], BF16), sb("wbb1", [128, DM], BF16)]
        pjb = [sb("pjb0", [128, 1024], BF16), sb("pjb1", [128, 1024], BF16)]
        wrb = sb("wrb", [128, NKC * 8], BF16)
        brb = sb("brb", [8, 1], F32)
        ab = sb("ab", [128, DM], BF16)
        bb = sb("bb", [128, DM], BF16)
        repb = sb("repb", [8, 128], BF16)
        btot = sb("btot", [128, 32], F32)
        pbb = sb("pbb", [128, 32], F32)
        idf = sb("idf", [128, 128], F32)
        logits = sb("logits", [8, 2048], F32)
        ztm = sb("ztm", [128, 72], F32)
        zs = sb("zs", [128, 72], F32)
        zc = sb("zc", [128, 72], F32)
        tmp = sb("tmpv", [128, 9], F32)
        tau = sb("tau", [128, 9], F32)
        wsp = sb("wsp", [128, 72], F32)
        wem = sb("wem", [8, 1152], BF16)
        wrep = sb("wrep", [128, 1152], BF16)
        hsb = sb("hsb", [128, 1152], BF16)
        xts = [sb(f"xt{i}", [128, 512], F32) for i in range(4)]
        x1s = [sb(f"x1s{i}", [128, 512], F32) for i in range(4)]
        if ctx is not None:
            ppool = ctx["pf32"]
            p = ctx["p"]
        else:
            ppool = [stk.enter_context(nc.psum_tensor(f"pp{i}", [128, 512], F32)) for i in range(8)]
            block = stk.enter_context(nc.Block())
            p = Prog(nc)
            p.make_sems()
        pool = PsumPool(p, ppool)

        for (dst, src) in [(wrb, wr_in), (brb, br_in), (ab, A_in), (bb, B_in), (repb, rep_in),
                           (btot, bb_in), (pbb, pb_in), (idf, idf_in)]:
            p.emit("gpsimd", lambda g, dst=dst, src=src: g.dma_start(out=dst[:, :], in_=src),
                   inc=True, dma="tab")
        tab_all = p.chan_total("tab")
        bt_tok = p.emit("vector", lambda v: v.tensor_add(btot[:, :], btot[:, :], pbb[:, :]),
                        waits=[tab_all], inc=True)
        for kc in range(NKC):
            p.emit("sync", lambda s, kc=kc: s.dma_start(out=xnb[:, kc * ownW:(kc + 1) * ownW], in_=xn_in[kc][:, 0:ownW]),
                   inc=True, dma="xin")
        for ch in range(8):
            p.emit("sync", lambda s, ch=ch: s.dma_start(out=o2b[:, ch * ownW:(ch + 1) * ownW], in_=o2_in[ch][:, 0:ownW]),
                   inc=True, dma="xin")
        xin_all = p.chan_total("xin")

        lg_toks = []
        for (t0, tw) in tiles_of(ownW):
            bi, ps, wtok = pool.acquire(1)[0]
            last = None
            for kc in range(NKC):
                last = p.emit("tensor", lambda t, ps=ps, kc=kc, t0=t0, tw=tw:
                              t.matmul(ps[0:8, 0:tw], wrb[:, kc * 8:(kc + 1) * 8],
                                       xnb[:, kc * ownW + t0: kc * ownW + t0 + tw],
                                       start=(kc == 0), stop=(kc == NKC - 1)),
                              waits=[tab_all, xin_all, wtok] if kc == 0 else (), inc=(kc == NKC - 1))
            ev = p.emit("scalar", lambda s, ps=ps, t0=t0, tw=tw:
                        s.add(logits[0:8, t0:t0 + tw], ps[0:8, 0:tw], brb[0:8, 0:1]),
                        waits=[last], inc=True)
            pool.release(bi, ev)
            lg_toks.append(ev)
        lg_done = lg_toks[-1]

        tr_evs = []
        for tt in range(NT):
            bi, ps, wtok = pool.acquire(1)[0]
            ttok = p.emit("tensor", lambda t, ps=ps, tt=tt:
                          t.transpose(ps[0:128, 0:8], logits[0:8, tt * 128:(tt + 1) * 128], idf[0:8, 0:8]),
                          waits=[lg_done, tab_all, wtok], inc=True)
            ev = p.emit("vector", lambda v, ps=ps, tt=tt:
                        v.tensor_copy(ztm[:, tt * 8:(tt + 1) * 8], ps[0:128, 0:8]), waits=[ttok], inc=True)
            pool.release(bi, ev)
            tr_evs.append(ev)
        def esl(buf, e):
            return bass.AP(buf[:, :].tensor, buf[:, :].offset + e, [[buf[:, :].ap[0][0], 128], [8, NT]])
        cp = p.emit("vector", lambda v: v.tensor_copy(zs[:, :], ztm[:, :]), waits=[tr_evs[-1]])
        for (i, j) in SORT8_NET:
            p.emit("vector", lambda v, i=i, j=j: v.tensor_tensor(tmp[:, 0:NT], esl(zs, i), esl(zs, j), ALU.min))
            p.emit("vector", lambda v, i=i, j=j: v.tensor_tensor(esl(zs, i), esl(zs, i), esl(zs, j), ALU.max))
            p.emit("vector", lambda v, j=j: v.tensor_copy(esl(zs, j), tmp[:, 0:NT]))
        p.emit("vector", lambda v: v.tensor_copy(esl(zc, 0), esl(zs, 0)))
        for k in range(1, 8):
            p.emit("vector", lambda v, k=k: v.tensor_add(esl(zc, k), esl(zc, k - 1), esl(zs, k)))
        for k in range(8):
            p.emit("vector", lambda v, k=k: v.tensor_scalar(esl(wsp, k), esl(zc, k),
                                                            1.0 / (k + 1), -1.0 / (k + 1),
                                                            op0=ALU.mult, op1=ALU.add))
        p.emit("vector", lambda v: v.tensor_tensor(tau[:, 0:NT], esl(wsp, 0), esl(wsp, 1), ALU.max))
        for k in range(2, 8):
            p.emit("vector", lambda v, k=k: v.tensor_tensor(tau[:, 0:NT], tau[:, 0:NT], esl(wsp, k), ALU.max))
        for e in range(8):
            p.emit("vector", lambda v, e=e: v.tensor_tensor(esl(wsp, e), esl(ztm, e), tau[:, 0:NT], ALU.subtract))
        sp_done = p.emit("vector", lambda v: v.tensor_scalar_max(wsp[:, :], wsp[:, :], 0.0), inc=True)

        wt_evs = []
        for tt in range(NT):
            bi, ps, wtok = pool.acquire(1)[0]
            ttok = p.emit("tensor", lambda t, ps=ps, tt=tt:
                          t.transpose(ps[0:8, 0:128], wsp[:, tt * 8:(tt + 1) * 8], idf[:, :]),
                          waits=[sp_done, wtok], inc=True)
            ev = p.emit("vector", lambda v, ps=ps, tt=tt:
                        v.tensor_copy(wem[0:8, tt * 128:(tt + 1) * 128], ps[0:8, 0:128]), waits=[ttok], inc=True)
            pool.release(bi, ev)
            wt_evs.append(ev)
        wem_done = wt_evs[-1]

        h_toks = []
        for (t0, tw) in tiles_of(ownW):
            bi, ps, wtok = pool.acquire(1)[0]
            mr = p.emit("tensor", lambda t, ps=ps, t0=t0, tw=tw:
                        t.matmul(ps[:, 0:tw], repb[0:8, :], wem[0:8, t0:t0 + tw], start=True, stop=True),
                        waits=[wem_done, tab_all, wtok], inc=True)
            er = p.emit("vector", lambda v, ps=ps, t0=t0, tw=tw:
                        v.tensor_copy(wrep[:, t0:t0 + tw], ps[:, 0:tw]), waits=[mr], inc=True)
            pool.release(bi, er)
            bi2, ps2, wtok2 = pool.acquire(1)[0]
            last = None
            for kc in range(NKC):
                last = p.emit("tensor", lambda t, ps=ps2, kc=kc, t0=t0, tw=tw:
                              t.matmul(ps[:, 0:tw], ab[:, kc * 128:(kc + 1) * 128],
                                       xnb[:, kc * ownW + t0: kc * ownW + t0 + tw],
                                       start=(kc == 0), stop=(kc == NKC - 1)),
                              waits=[tab_all, wtok2] if kc == 0 else (), inc=(kc == NKC - 1))
            eh = p.emit("vector", lambda v, ps=ps2, t0=t0, tw=tw:
                        v.tensor_copy(hsb[:, t0:t0 + tw], ps[:, 0:tw]), waits=[last], inc=True)
            pool.release(bi2, eh)
            ht = p.emit("vector", lambda v, t0=t0, tw=tw:
                        v.tensor_mul(hsb[:, t0:t0 + tw], hsb[:, t0:t0 + tw], wrep[:, t0:t0 + tw]),
                        waits=[er], inc=True)
            h_toks.append(ht)
        h_done = h_toks[-1]

        mm_last = {}
        xt_tok = [None] * 4
        x1_tok = [None] * 4
        slot = 0
        for oc in range(32):
            wb = wbb[oc % 2]
            pj = pjb[oc % 2]
            prev = mm_last.get(oc - 2)
            wl = p.emit("gpsimd", lambda g, oc=oc, wb=wb: g.dma_start(out=wb[:, :], in_=wb_in[oc]),
                        waits=[prev], inc=True, dma="wb%d" % (oc % 2))
            pl = p.emit("gpsimd", lambda g, oc=oc, pj=pj: g.dma_start(out=pj[:, :], in_=pj_in[oc]),
                        waits=[prev], inc=True, dma="pj%d" % (oc % 2))
            last = None
            for (t0, tw) in tiles_of(ownW):
                bi, ps, wtok = pool.acquire(1)[0]
                sl_x = slot % 4
                xt = p.emit("sync", lambda s, oc=oc, t0=t0, tw=tw, b=xts[sl_x]:
                            s.dma_start(out=b[:, 0:tw], in_=x1_in[oc][:, x1_own + t0: x1_own + t0 + tw]),
                            waits=[xt_tok[sl_x]], inc=True, dma="xt%d" % sl_x)
                for kc in range(NKC):
                    last = p.emit("tensor", lambda t, ps=ps, kc=kc, t0=t0, tw=tw, wb=wb:
                                  t.matmul(ps[:, 0:tw], wb[:, kc * 128:(kc + 1) * 128],
                                           xnb[:, kc * ownW + t0: kc * ownW + t0 + tw],
                                           start=(kc == 0), stop=False),
                                  waits=[wl, wtok] if kc == 0 else ())
                p.emit("tensor", lambda t, ps=ps, oc=oc, t0=t0, tw=tw:
                       t.matmul(ps[:, 0:tw], bb[:, oc * 128:(oc + 1) * 128],
                                hsb[:, t0:t0 + tw], start=False, stop=False),
                       waits=[h_done, tab_all])
                for kc in range(8):
                    last = p.emit("tensor", lambda t, ps=ps, kc=kc, t0=t0, tw=tw, pj=pj:
                                  t.matmul(ps[:, 0:tw], pj[:, kc * 128:(kc + 1) * 128],
                                           o2b[:, kc * ownW + t0: kc * ownW + t0 + tw],
                                           start=False, stop=(kc == 7)),
                                  waits=[pl] if kc == 0 else (), inc=(kc == 7))
                ev = p.emit("vector", lambda v, ps=ps, oc=oc, tw=tw, b=xts[sl_x], o=x1s[sl_x]:
                            v.scalar_tensor_tensor(o[:, 0:tw], ps[:, 0:tw], btot[:, oc:oc + 1],
                                                   b[:, 0:tw], op0=ALU.add, op1=ALU.add),
                            waits=[last, xt, x1_tok[sl_x], bt_tok], inc=True)
                pool.release(bi, ev)
                xt_tok[sl_x] = ev
                dt = p.emit("sync", lambda s, oc=oc, t0=t0, tw=tw, o=x1s[sl_x]:
                            s.dma_start(out=out_d[oc][:, t0:t0 + tw], in_=o[:, 0:tw]),
                            waits=[ev], inc=True, dma="oo%d" % sl_x)
                x1_tok[sl_x] = dt
                slot += 1
            mm_last[oc] = last
        if ctx is None:
            p.finalize(block, final_waits=[p.chan_total("oo%d" % i) for i in range(4)])
    return nc



def kernel_unfused(**inputs):
    g, cores = host_prep(inputs)
    ident_bf = np.eye(128, dtype=np.float32).astype(BF)
    identf = np.eye(128, dtype=np.float32)
    res = {c: {} for c in range(8)}

    def launch(builder, par, make_in):
        ids = [c for c in range(8) if c % 2 == par]
        nc = builder(par)
        in_maps = [make_in(c) for c in ids]
        r = run_bass_kernel_spmd(nc, in_maps, core_ids=ids)
        for i, c in enumerate(ids):
            res[c].update(r.results[i])

    for par in (0, 1):
        launch(build_s1, par, lambda c: dict(
            xfm_kv=cores[c]["xfm_kv"], wqkv_img=g["wqkv_img"], wte_img=g["wte_img"],
            cos=cores[c]["cos"], ssin=cores[c]["ssin"]))
    for par in (0, 1):
        launch(build_s2, par, lambda c: dict(
            q_out=res[c]["q_out"], k_out=res[c]["k_out"], v_out=res[c]["v_out"],
            gating=g["gating"], ident=ident_bf))
    for par in (0, 1):
        launch(build_s3, par, lambda c: dict(
            y_out=res[c]["y_out"], xn_own=res[c]["xn_own"], w_proj_img=g["w_proj_img"],
            qw_img=g["qw_img"], s_rep=g["s_rep"], z16=g["z16"], s16=g["s16"],
            gimg=g["gimg"], bias0=g["bias0"], xfm_q=cores[c]["xfm_q"]))
    for par in (0, 1):
        launch(build_s4a, par, lambda c: dict(
            x1=res[c]["x1"], conv_img=g["conv_img"], conv_b=g["conv_b"]))
    for par in (0, 1):
        launch(build_s4b, par, lambda c: dict(
            qkv1=res[c]["qkv1"], wq_img=g["wq_img"], wk_img=g["wk_img"], wv_img=g["wv_img"],
            wo_img=g["wo_img"], bq=g["bq"], bk=g["bk"], bv_col=g["bv_col"], bo=g["bo"],
            ident=ident_bf))
    for par in (0, 1):
        launch(build_s5, par, lambda c: dict(
            x1=res[c]["x1"], xn1_own=res[c]["xn1_own"], o2=res[c]["o2"],
            wbase_img=g["wbase_img"], projw_img=g["projw_img"], wr_img=g["wr_img"], br=g["br"],
            A_img=g["A_img"], B_sb=g["B_sb"], rep=g["rep"], bbase=g["bbase"], projb=g["projb"],
            identf=identf))

    out = np.empty((B_SEQ, T_SEQ, DM), np.float32)
    for c in range(8):
        P = PARAMS[c % 2]
        seq = c // 2
        o = np.asarray(res[c]["outp"], np.float32).reshape(DM, P["ownW"])
        out[seq, P["own0"]:P["own0"] + P["ownW"], :] = o.T
    return out



def _emit_ops(eng, ename, p):
    sems = p.sems
    for (fn, wl, tok) in p.ops[ename]:
        for (sk, v) in wl:
            eng.wait_ge(sems[sk], v)
        ins = fn(eng)
        if tok is not None:
            ins.then_inc(sems[tok[0]], 16 if tok[0].startswith("dma_") else 1)
    for k, v in p.counts.items():
        if v > 0:
            eng.wait_ge(sems[k], v)


def build_fused():
    nc = bass.Bass("TRN2", target_bir_lowering=False)
    d = {}

    def DI(name, shape, dt):
        d[name] = nc.dram_tensor(name, shape, dt, kind="ExternalInput").ap()

    def DN(name, shape, dt):
        d[name] = nc.dram_tensor(name, shape, dt).ap()

    def DO(name, shape, dt):
        d[name] = nc.dram_tensor(name, shape, dt, kind="ExternalOutput").ap()

    DI("par", [1, 1], I32)
    DI("xfm_kv", [DM, 2048], F32)
    DI("wqkv_img", [96, 128, DM], F32)
    DI("wte_img", [128, NKC * 10], F32)
    DI("cos", [128, 2048], F32)
    DI("ssin", [128, 2048], F32)
    DI("gating", [1, 1], F32)
    DI("ident", [128, 128], BF16)
    DI("identf", [128, 128], F32)
    DI("w_proj_img", [32, 128, DM], F32)
    DI("qw_img", [32, 128, DM], I32)
    DI("s_rep", [32, 128, DM], F32)
    DI("z16", [16, DM], F32)
    DI("s16", [16, DM], F32)
    DI("gimg", [128, NKC * 16], F32)
    DI("bias0", [128, 32], F32)
    DI("xfm_q", [DM, 1280], F32)
    DI("conv_img", [24, 128, 3 * DM], F32)
    DI("conv_b", [128, 24], F32)
    DI("wq_img", [8, 128, 1024], F32)
    DI("wk_img", [8, 128, 1024], F32)
    DI("wv_img", [8, 128, 1024], F32)
    DI("wo_img", [8, 128, 1024], F32)
    DI("bq", [128, 8], F32)
    DI("bk", [128, 8], F32)
    DI("bv_col", [128, 8], F32)
    DI("bo", [128, 8], F32)
    DI("wbase_img", [32, 128, DM], F32)
    DI("projw_img", [32, 128, 1024], F32)
    DI("wr_img", [128, NKC * 8], F32)
    DI("br", [8, 1], F32)
    DI("A_img", [128, DM], F32)
    DI("B_sb", [128, DM], F32)
    DI("rep", [8, 128], F32)
    DI("bbase", [128, 32], F32)
    DI("projb", [128, 32], F32)
    DN("q_out", [32, 128, 1280], BF16)
    DN("k_out", [32, 128, 2058], BF16)
    DN("v_out", [32, 128, 2058], BF16)
    DN("xn_own", [NKC, 128, 1280], BF16)
    DN("y_out", [32, 128, 1280], BF16)
    DN("x1", [NKC, 128, 1280], F32)
    DN("qkv1", [24, 128, 1152], BF16)
    DN("xn1_own", [NKC, 128, 1152], BF16)
    DN("o2", [8, 128, 1152], BF16)
    DO("outp", [NKC, 128, 1152], F32)

    from contextlib import ExitStack
    with ExitStack() as master:
        parb = master.enter_context(nc.sbuf_tensor("parb", [1, 1], I32))
        par_sem = master.enter_context(nc.semaphore("parsem"))
        pf32 = [master.enter_context(nc.psum_tensor(f"gf{i}", [128, 512], F32)) for i in range(6)]
        pbf = [master.enter_context(nc.psum_tensor(f"gb{i}", [128, 1024], BF16)) for i in range(2)]
        progs = {0: Prog(nc, tag="f"), 1: None}
        progs[1] = Prog(nc, tag="f", shared_sems=progs[0].sems)
        progs[0].make_sems()
        progs[1].make_sems()
        stages = [(build_s1, [1, 0]), (build_s2, [1, 0]), (build_s3, [1, 0]),
                  (build_s4a, [0, 1]), (build_s4b, [0, 1]), (build_s5, [0, 1])]
        for (fn, order) in stages:
            with ExitStack() as stage_stk:
                bufs = {}
                for par in order:
                    ctx = dict(nc=nc, d=d, bufs=bufs, stk=stage_stk, pf32=pf32, pbf=pbf, p=progs[par])
                    fn(par, ctx=ctx)
                    progs[par].barrier()
        block = master.enter_context(nc.Block())
        for ename in ENGINES:
            def mk(ename):
                def body(eng):
                    if ename == "sync":
                        eng.dma_start(out=parb[:, :], in_=d["par"]).then_inc(par_sem, 16)
                    eng.wait_ge(par_sem, 16)
                    with eng.register("preg_" + ename) as preg:
                        eng.reg_load(preg, parb[0:1, 0:1])
                        with eng.If_eq(preg, 0):
                            _emit_ops(eng, ename, progs[0])
                        with eng.Else():
                            _emit_ops(eng, ename, progs[1])
                return body
            getattr(block, ename)(mk(ename))
    return nc


def kernel(**inputs):
    g, cores = host_prep(inputs)
    ident_bf = np.ascontiguousarray(np.eye(128, dtype=np.float32).astype(BF))
    identf = np.eye(128, dtype=np.float32)
    nc = build_fused()
    in_maps = []
    for c in range(8):
        P = PARAMS[c % 2]
        dcr = cores[c]
        xkv = np.zeros((DM, 2048), np.float32)
        xkv[:, :P["kvW"]] = dcr["xfm_kv"]
        xq = np.zeros((DM, 1280), np.float32)
        xq[:, :P["qW"]] = dcr["xfm_q"]
        in_maps.append(dict(
            par=np.array([[c % 2]], np.int32), xfm_kv=xkv, xfm_q=xq,
            cos=g["cos_fm"], ssin=g["ssin_fm"],
            wqkv_img=g["wqkv_img"], wte_img=g["wte_img"], gating=g["gating"],
            ident=ident_bf, identf=identf,
            w_proj_img=g["w_proj_img"], qw_img=g["qw_img"], s_rep=g["s_rep"],
            z16=g["z16"], s16=g["s16"], gimg=g["gimg"], bias0=g["bias0"],
            conv_img=g["conv_img"], conv_b=g["conv_b"],
            wq_img=g["wq_img"], wk_img=g["wk_img"], wv_img=g["wv_img"], wo_img=g["wo_img"],
            bq=g["bq"], bk=g["bk"], bv_col=g["bv_col"], bo=g["bo"],
            wbase_img=g["wbase_img"], projw_img=g["projw_img"], wr_img=g["wr_img"], br=g["br"],
            A_img=g["A_img"], B_sb=g["B_sb"], rep=g["rep"], bbase=g["bbase"], projb=g["projb"]))
    res = run_bass_kernel_spmd(nc, in_maps, core_ids=list(range(8)))
    out = np.empty((B_SEQ, T_SEQ, DM), np.float32)
    for c in range(8):
        P = PARAMS[c % 2]
        seq = c // 2
        o = np.asarray(res.results[c]["outp"], np.float32)[:, :, :P["ownW"]].reshape(DM, P["ownW"]) if False else \
            np.asarray(res.results[c]["outp"], np.float32)[:, :, :P["ownW"]]
        o = o.reshape(NKC * 128, P["ownW"])
        out[seq, P["own0"]:P["own0"] + P["ownW"], :] = o.T
    return out


# version 05
# speedup vs baseline: 2.0087x; 1.1156x over previous
import numpy as np
import ml_dtypes
import concourse.bass as bass
import concourse.mybir as mybir
from concourse.bass_utils import run_bass_kernel_spmd

F32 = mybir.dt.float32
BF16 = mybir.dt.bfloat16
F16 = mybir.dt.float16
I32 = mybir.dt.int32
AF = mybir.ActivationFunctionType
ALU = mybir.AluOpType
AX = mybir.AxisListType
BF = ml_dtypes.bfloat16

B_SEQ, T_SEQ, DM = 4, 2048, 4096
H0, HD0 = 32, 128
NKC = DM // 128
P_SPLIT = 1152
EPS = 1e-5

PARAMS = {
    0: dict(kvW=1280, q0=0,    qW=1280, own0=0,      ownW=1152),
    1: dict(kvW=2048, q0=1024, qW=1024, own0=1152,   ownW=896),
}

def tiles_of(width, step=512):
    out = []
    c = 0
    while c < width:
        w = min(step, width - c)
        out.append((c, w))
        c += w
    return out


def pack_lhsT(wT):
    IN, OUT = wT.shape
    return np.ascontiguousarray(
        wT.reshape(IN // 128, 128, OUT // 128, 128).transpose(2, 1, 0, 3).reshape(OUT // 128, 128, IN))


def host_prep(inp):
    g = {}
    x = np.ascontiguousarray(np.asarray(inp["L0_attn_x"]))
    w_attn = np.asarray(inp["L0_attn_w_attn"])
    g["wqkv_img"] = pack_lhsT(np.ascontiguousarray(w_attn.T))
    wte = np.asarray(inp["L0_attn_adapter_wte"])
    g["wte_img"] = np.ascontiguousarray(wte.reshape(10, NKC, 128).transpose(2, 1, 0).reshape(128, NKC * 10))
    g["gating"] = np.asarray(inp["L0_attn_gating"]).reshape(1, 1)

    theta = 1.0 / 10000.0 ** (np.arange(0, HD0, 2) / HD0)
    idx = np.outer(np.arange(T_SEQ), theta)
    idx2 = np.concatenate([idx, idx], axis=-1)
    cos2 = np.cos(idx2).astype(np.float32)
    sin2 = np.sin(idx2).astype(np.float32)
    ss = sin2.copy()
    ss[:, :64] *= -1.0
    g["cos_fm"] = np.ascontiguousarray(cos2.T)
    ssT = ss.T
    g["ssin_fm"] = np.ascontiguousarray(np.concatenate([ssT[64:], ssT[:64]], axis=0))

    g["w_proj_img"] = pack_lhsT(np.ascontiguousarray(np.asarray(inp["L0_attn_w_proj"]).T))
    qw = np.asarray(inp["L0_ffn_qweight"])
    g["qw_img"] = pack_lhsT(np.ascontiguousarray(qw.T))
    scales = np.asarray(inp["L0_ffn_scales"]).reshape(DM, 16)
    zeros = np.asarray(inp["L0_ffn_zeros"]).reshape(DM, 16)
    oi = np.arange(DM)
    kc_i = oi // 128
    o_i = oi % 128
    s_img = np.empty((32, DM), np.float32)
    z_img = np.empty((32, DM), np.float32)
    for oc in range(32):
        s_img[oc] = scales[oc * 128 + o_i, kc_i // 2]
        z_img[oc] = zeros[oc * 128 + o_i, kc_i // 2]
    g["s_img"] = s_img
    g["z_img"] = z_img
    g["z16"] = np.ascontiguousarray(zeros.T)
    g["s16"] = np.ascontiguousarray(scales.T)
    gim = np.zeros((128, NKC * 16), np.float32)
    for kc in range(NKC):
        gim[:, kc * 16 + kc // 2] = 1.0
    g["gimg"] = gim
    g["s_rep"] = np.ascontiguousarray(np.broadcast_to(s_img[:, None, :], (32, 128, DM)))
    g["z_rep_row"] = z_img
    g["bias0"] = np.asarray(inp["L0_ffn_bias"]).reshape(32, 128).T.copy()

    cw = np.asarray(inp["L1_attn_conv_w"])
    g["conv_img"] = np.ascontiguousarray(
        cw.reshape(3, NKC, 128, 24, 128).transpose(3, 2, 0, 1, 4).reshape(24, 128, 3 * NKC * 128))
    g["conv_b"] = np.asarray(inp["L1_attn_conv_b"]).reshape(24, 128).T.copy()
    wq = np.asarray(inp["L1_attn_wq"]).reshape(1024, 1024)
    wk = np.asarray(inp["L1_attn_wk"]).reshape(1024, 1024)
    wv = np.asarray(inp["L1_attn_wv"]).reshape(1024, 1024)
    g["wq_img"] = pack_lhsT(wq)
    g["wk_img"] = pack_lhsT(wk)
    g["wv_img"] = pack_lhsT(wv)
    g["bq"] = np.asarray(inp["L1_attn_bq"]).reshape(1024).reshape(8, 128).T.copy()
    g["bk"] = np.asarray(inp["L1_attn_bk"]).reshape(1024).reshape(8, 128).T.copy()
    g["bv_col"] = np.asarray(inp["L1_attn_bv"]).reshape(1024).reshape(8, 128).T.copy()
    g["wo_img"] = pack_lhsT(np.asarray(inp["L1_attn_wo"]).reshape(1024, 1024))
    g["bo"] = np.asarray(inp["L1_attn_bo"]).reshape(8, 128).T.copy()
    g["projw_img"] = pack_lhsT(np.asarray(inp["L1_attn_proj_w"]))
    g["projb"] = np.asarray(inp["L1_attn_proj_b"]).reshape(32, 128).T.copy()

    g["wbase_img"] = pack_lhsT(np.ascontiguousarray(np.asarray(inp["L1_ffn_W_base"]).T))
    g["bbase"] = np.asarray(inp["L1_ffn_b_base"]).reshape(32, 128).T.copy()
    wr = np.asarray(inp["L1_ffn_W_router"])
    g["wr_img"] = np.ascontiguousarray(wr.reshape(8, NKC, 128).transpose(2, 1, 0).reshape(128, NKC * 8))
    g["br"] = np.asarray(inp["L1_ffn_b_router"]).reshape(8, 1)
    la = np.asarray(inp["L1_ffn_lora_A"])
    A_all = np.ascontiguousarray(la.transpose(1, 0, 2).reshape(DM, 128))
    g["A_img"] = pack_lhsT(A_all)[0]
    g["B_sb"] = np.ascontiguousarray(np.asarray(inp["L1_ffn_lora_B"]).reshape(128, DM))
    rep = np.zeros((8, 128), np.float32)
    for e in range(8):
        rep[e, e * 16:(e + 1) * 16] = 1.0
    g["rep"] = rep

    cores = []
    for c in range(8):
        par = c % 2
        P = PARAMS[par]
        seq = c // 2
        xs = np.ascontiguousarray(x[seq].T)
        d = dict(parity=par, seq=seq)
        d["xfm_kv"] = np.ascontiguousarray(xs[:, :P["kvW"]])
        d["xfm_q"] = np.ascontiguousarray(xs[:, P["q0"]:P["q0"] + P["qW"]])
        d["cos"] = np.ascontiguousarray(g["cos_fm"][:, :P["kvW"]])
        d["ssin"] = np.ascontiguousarray(g["ssin_fm"][:, :P["kvW"]])
        cores.append(d)
    return g, cores


ENGINES = ("tensor", "vector", "scalar", "gpsimd", "sync")


class Prog:

    def __init__(self, nc, tag="", shared_sems=None):
        self.nc = nc
        self.tag = tag
        self.ops = {e: [] for e in ENGINES}
        self.sems = shared_sems if shared_sems is not None else {}
        self.counts = {}
        self.hwm = {e: {} for e in ENGINES}
        self.pending = {e: [] for e in ENGINES}
        self.finals = []

    def make_sems(self):
        for e in ENGINES:
            self._mk(e)

    def _mk(self, key):
        if key not in self.sems:
            self.sems[key] = self.nc.semaphore(f"s{self.tag}_{key}").__enter__()
        if key not in self.counts:
            self.counts[key] = 0
        return key

    def barrier(self):
        snap = [(k, v) for (k, v) in self.counts.items() if v > 0]
        for e in ENGINES:
            self.pending[e].extend(snap)

    def emit(self, eng, fn, waits=(), inc=False, dma=None):
        key = self._mk("dma_" + dma) if dma else eng
        tok = None
        if inc:
            self.counts[key] += 16 if dma else 1
            tok = (key, self.counts[key])
        elif dma:
            raise AssertionError("every DMA needs inc=True (sync info required)")
        wl = []
        if self.pending[eng]:
            waits = list(self.pending[eng]) + list(waits)
            self.pending[eng] = []
        for t in waits:
            if t is None:
                continue
            (sk, v) = t
            if not sk.startswith("dma_") and sk == eng:
                continue
            h = self.hwm[eng].get(sk, 0)
            if v > h:
                wl.append((sk, v))
                self.hwm[eng][sk] = v
        self.ops[eng].append((fn, wl, tok))
        return tok

    def chan_total(self, ch):
        return ("dma_" + ch, self.counts.get("dma_" + ch, 0))

    def finalize(self, block, final_waits=()):
        for ename in ENGINES:
            oplist = self.ops[ename]
            sems = self.sems

            def mk(ename, oplist):
                def body(eng):
                    for (fn, wl, tok) in oplist:
                        for (sk, v) in wl:
                            eng.wait_ge(sems[sk], v)
                        ins = fn(eng)
                        if tok is not None:
                            ins.then_inc(sems[tok[0]], 16 if tok[0].startswith("dma_") else 1)
                    for t in final_waits:
                        if t is not None:
                            eng.wait_ge(sems[t[0]], t[1])
                return body
            getattr(block, ename)(mk(ename, oplist))


class PsumPool:

    def __init__(self, prog, banks):
        self.prog = prog
        self.banks = banks
        self.tokens = [None] * len(banks)
        self.ptr = 0

    def acquire(self, n):
        out = []
        for _ in range(n):
            i = self.ptr % len(self.banks)
            self.ptr += 1
            out.append((i, self.banks[i], self.tokens[i]))
        return out

    def release(self, i, token):
        self.tokens[i] = token



def emit_ln(p, pool, xq, xq_cols, kvW, a_m, b_m, sq_scr, rowbuf, ones_col, ones_row, load_tok, col0=0):
    nc = p.nc
    toks = []
    for (t0, tw) in tiles_of(kvW):
        bk = pool.acquire(4)
        (i_sx, ps_sx, w1), (i_sq, ps_sq, w2), (i_a, ps_a, w3), (i_b, ps_b, w4) = bk
        mm_sx = None
        mmsq_toks = {}
        for kc in range(NKC):
            xsl = xq[:, kc * xq_cols + col0 + t0: kc * xq_cols + col0 + t0 + tw]
            half = (kc % 2) * 512
            sqv = sq_scr[:, half:half + tw]
            sq_tok = p.emit("scalar", lambda s, sqv=sqv, xsl=xsl: s.activation(sqv, xsl, AF.Square),
                            waits=[load_tok, mmsq_toks.get(kc - 2)], inc=True)
            mm_sx = p.emit("tensor", lambda t, xsl=xsl, ps=ps_sx, tw=tw, kc=kc:
                           t.matmul(ps[0:1, 0:tw], ones_col[:, 0:1], xsl, start=(kc == 0), stop=(kc == NKC - 1)),
                           waits=[load_tok, w1 if kc == 0 else None], inc=(kc == NKC - 1))
            mmsq_toks[kc] = p.emit("tensor", lambda t, sqv=sqv, ps=ps_sq, tw=tw, kc=kc:
                           t.matmul(ps[0:1, 0:tw], ones_col[:, 0:1], sqv, start=(kc == 0), stop=(kc == NKC - 1)),
                           waits=[sq_tok, w2 if kc == 0 else None], inc=True)
        mm_sq = mmsq_toks[NKC - 1]
        Rs = [rowbuf[0:1, j * 512: j * 512 + tw] for j in range(6)]
        t_mean = p.emit("vector", lambda v, o=Rs[0], i=ps_sx[0:1, 0:tw]: v.tensor_scalar_mul(o, i, 1.0 / DM),
                        waits=[mm_sx], inc=True)
        t_var = p.emit("vector", lambda v, o=Rs[1], i=ps_sq[0:1, 0:tw]: v.tensor_scalar_mul(o, i, 1.0 / DM),
                       waits=[mm_sq], inc=True)
        p.emit("vector", lambda v, o=Rs[4], i=Rs[0]: v.tensor_mul(o, i, i), waits=[t_mean])
        p.emit("vector", lambda v, o=Rs[1], i=Rs[4]: v.tensor_sub(o, o, i), waits=[t_var])
        t_ve = p.emit("vector", lambda v, o=Rs[1]: v.tensor_scalar_add(o, o, EPS), inc=True)
        t_sd = p.emit("scalar", lambda s, o=Rs[5], i=Rs[1]: s.activation(o, i, AF.Sqrt), inc=True,
                      waits=[t_ve])
        t_a = p.emit("vector", lambda v, o=Rs[2], i=Rs[5]: v.reciprocal(o, i), waits=[t_sd], inc=True)
        t_b = p.emit("vector", lambda v, o=Rs[3], i=Rs[0], j=Rs[2]: v.tensor_mul(o, i, j), waits=[t_a], inc=True)
        bca = p.emit("tensor", lambda t, o=ps_a[:, 0:tw], r=Rs[2]: t.matmul(o, ones_row[0:1, 0:128], r,
                                                  start=True, stop=True), waits=[t_a, w3], inc=True)
        bcb = p.emit("tensor", lambda t, o=ps_b[:, 0:tw], r=Rs[3]: t.matmul(o, ones_row[0:1, 0:128], r,
                                                  start=True, stop=True), waits=[t_b, w4], inc=True)
        ea = p.emit("vector", lambda v, o=a_m[:, t0:t0 + tw], i=ps_a[:, 0:tw]: v.tensor_copy(o, i),
                    waits=[bca], inc=True)
        eb = p.emit("vector", lambda v, o=b_m[:, t0:t0 + tw], i=ps_b[:, 0:tw]: v.tensor_copy(o, i),
                    waits=[bcb], inc=True)
        pool.release(i_sx, t_mean)
        pool.release(i_sq, t_var)
        pool.release(i_a, ea)
        pool.release(i_b, eb)
    for kc in range(NKC):
        xsl = xq[:, kc * xq_cols + col0: kc * xq_cols + col0 + kvW]
        p.emit("vector", lambda v, xsl=xsl: v.tensor_mul(xsl, xsl, a_m[:, 0:kvW]), waits=[ea])
        tk = p.emit("vector", lambda v, xsl=xsl: v.tensor_sub(xsl, xsl, b_m[:, 0:kvW]), inc=True)
        toks.append(tk)
    return toks


def build_s1(parity, qkv_enable=True, ctx=None):
    P = PARAMS[parity]
    kvW, q0, qW = P["kvW"], P["q0"], P["qW"]
    xq_cols = kvW + 16
    if ctx is not None:
        nc = ctx["nc"]
        _D = lambda name, shape, dt, kind: ctx["d"][name]
    else:
        nc = bass.Bass("TRN2", target_bir_lowering=False)
        _D = lambda name, shape, dt, kind: nc.dram_tensor(name, shape, dt, kind=kind).ap()

    xfm = _D("xfm_kv", [DM, kvW], F32, "ExternalInput")
    wqkv = _D("wqkv_img", [96, 128, DM], F32, "ExternalInput")
    wte = _D("wte_img", [128, NKC * 10], F32, "ExternalInput")
    cosd = _D("cos", [128, kvW], F32, "ExternalInput")
    ssind = _D("ssin", [128, kvW], F32, "ExternalInput")
    q_out = _D("q_out", [32, 128, qW], BF16, "ExternalOutput")
    k_out = _D("k_out", [32, 128, kvW + 10], BF16, "ExternalOutput")
    v_out = _D("v_out", [32, 128, kvW + 10], BF16, "ExternalOutput")
    xn_out = _D("xn_own", [NKC, 128, qW], BF16, "ExternalOutput")

    ctxsfx = "_s1"
    from contextlib import ExitStack
    with ExitStack() as stk:
        if ctx is not None:
            def sb(name, shape, dt):
                if name not in ctx["bufs"]:
                    ctx["bufs"][name] = ctx["stk"].enter_context(nc.sbuf_tensor(name + ctxsfx, shape, dt))
                return ctx["bufs"][name]
        else:
            sb = lambda name, shape, dt: stk.enter_context(nc.sbuf_tensor(name, shape, dt))
        xq = sb("xq", [128, NKC * xq_cols], BF16)
        wb0 = sb("wb0", [128, DM], BF16)
        wb1 = sb("wb1", [128, DM], BF16)
        wteb = sb("wteb", [128, NKC * 10], BF16)
        cosb = sb("cosb", [128, kvW], BF16)
        ssinb = sb("ssinb", [128, kvW], BF16)
        am = sb("am", [128, kvW], BF16)
        bm = sb("bm", [128, kvW], BF16)
        sq_scr = sb("sq", [128, 1024], BF16)
        rowbuf = sb("rows", [1, 6 * 512], F32)
        ones_col = sb("onec", [128, 1], BF16)
        ones_row = sb("oner", [1, 128], F32)
        ropet = sb("ropet", [128, kvW], BF16)
        st0 = sb("st0", [128, kvW + 16], BF16)
        st1 = sb("st1", [128, kvW + 16], BF16)
        sr0 = sb("sr0", [128, kvW + 16], BF16)
        sr1 = sb("sr1", [128, kvW + 16], BF16)
        if ctx is not None:
            ppool = ctx["pf32"]
            p = ctx["p"]
        else:
            ppool = [stk.enter_context(nc.psum_tensor(f"pp{i}", [128, 512], F32)) for i in range(8)]
            block = stk.enter_context(nc.Block())
            p = Prog(nc)
            p.make_sems()
        pool = PsumPool(p, ppool)

        p.emit("vector", lambda v: v.memset(ones_col[:, :], 1.0))
        c_ones = p.emit("vector", lambda v: v.memset(ones_row[:, :], 1.0), inc=True)

        xtoks = []
        for kc in range(NKC):
            tk = p.emit("gpsimd", lambda g, kc=kc: g.dma_start(
                out=xq[:, kc * xq_cols: kc * xq_cols + kvW], in_=xfm[kc * 128:(kc + 1) * 128, 0:kvW]),
                inc=True, dma="xin")
            xtoks.append(tk)
        t_wte = p.emit("gpsimd", lambda g: g.dma_start(out=wteb[:, :], in_=wte), inc=True, dma="xin")
        t_cos = p.emit("gpsimd", lambda g: g.dma_start(out=cosb[:, 0:kvW], in_=cosd[:, 0:kvW]), inc=True, dma="xin")
        t_ssin = p.emit("gpsimd", lambda g: g.dma_start(out=ssinb[:, 0:kvW], in_=ssind[:, 0:kvW]), inc=True, dma="xin")
        xin_all = p.chan_total("xin")
        t_wte = t_cos = t_ssin = xin_all
        wte_toks = []
        for kc in range(NKC):
            tk = p.emit("vector", lambda v, kc=kc: v.tensor_copy(
                xq[:, kc * xq_cols + kvW: kc * xq_cols + kvW + 10],
                wteb[:, kc * 10:(kc + 1) * 10]), waits=[t_wte], inc=(kc == NKC - 1))
            wte_toks.append(tk)

        ln_toks = emit_ln(p, pool, xq, xq_cols, kvW, am, bm, sq_scr, rowbuf,
                          ones_col, ones_row, xin_all)
        ln_done = ln_toks[-1]
        wte_done = wte_toks[-1]

        out_toks = []
        for kc in range(NKC):
            tk = p.emit("sync", lambda s, kc=kc: s.dma_start(
                out=xn_out[kc][:, 0:qW], in_=xq[:, kc * xq_cols + q0: kc * xq_cols + q0 + qW]),
                waits=[ln_toks[kc]], inc=True, dma="xout")
            out_toks.append(tk)

        wbufs = [wb0, wb1]
        stages = [(st0, sr0), (st1, sr1)]
        wload_tok = {}
        mm_last = {}
        raw_free = [None, None]
        rope_free = [None, None]
        for oc in (range(96) if qkv_enable else []):
            kind = "q" if oc < 32 else ("k" if oc < 64 else "v")
            head = oc % 32
            wbuf = wbufs[oc % 2]
            wload_tok[oc] = p.emit("gpsimd", lambda g, oc=oc, wbuf=wbuf: g.dma_start(
                out=wbuf[:, :], in_=wqkv[oc]), waits=[mm_last.get(oc - 2)], inc=True, dma="w%d" % (oc % 2))

            if kind == "q":
                col0, width = q0, qW
            else:
                col0, width = 0, kvW + 10
            tl = tiles_of(width)
            if len(tl) >= 2 and tl[-1][1] + tl[-2][1] <= 512:
                t0a, w0a = tl[-2]
                tl = tl[:-2] + [(t0a, w0a + tl[-1][1])]
            banks = pool.acquire(len(tl))
            stg_raw, stg_rope = stages[oc % 2]
            sfree = raw_free[oc % 2]
            last = None
            evac_done = None
            for ti, (t0, tw) in enumerate(tl):
                bi, ps, wtok = banks[ti]
                for kc in range(NKC):
                    rhs = xq[:, kc * xq_cols + col0 + t0: kc * xq_cols + col0 + t0 + tw]
                    w_first = [wload_tok[oc], ln_done, wte_done, wtok] if kc == 0 else ()
                    last = p.emit("tensor", lambda t, ps=ps, wbuf=wbuf, kc=kc, rhs=rhs, tw=tw:
                                  t.matmul(ps[:, 0:tw], wbuf[:, kc * 128:(kc + 1) * 128], rhs,
                                           start=(kc == 0), stop=(kc == NKC - 1)),
                                  waits=w_first, inc=(kc == NKC - 1))
                etok = p.emit("scalar", lambda s, ps=ps, t0=t0, tw=tw, stg_raw=stg_raw:
                              s.copy(stg_raw[:, t0:t0 + tw], ps[:, 0:tw]),
                              waits=[last, sfree], inc=True)
                pool.release(bi, etok)
                evac_done = etok
            mm_last[oc] = last

            if kind == "v":
                dtok = p.emit("sync", lambda s, head=head, stg_raw=stg_raw, width=width: s.dma_start(
                    out=v_out[head][:, 0:width], in_=stg_raw[:, 0:width]), waits=[evac_done], inc=True, dma="st%d" % (oc % 2))
                raw_free[oc % 2] = dtok
                out_toks.append(dtok)
            else:
                wW = qW if kind == "q" else kvW
                cs0 = q0 if kind == "q" else 0
                coss = cosb[:, cs0:cs0 + wW]
                ssins = ssinb[:, cs0:cs0 + wW]
                rfree = rope_free[oc % 2]
                p.emit("vector", lambda v, sr=stg_rope, st=stg_raw, coss=coss, wW=wW:
                       v.tensor_mul(sr[:, 0:wW], st[:, 0:wW], coss),
                       waits=[evac_done, t_cos, t_ssin, rfree])
                p.emit("vector", lambda v, st=stg_raw, ssins=ssins, wW=wW:
                       v.tensor_mul(ropet[0:64, 0:wW], st[64:128, 0:wW], ssins[64:128, 0:wW]))
                p.emit("vector", lambda v, st=stg_raw, ssins=ssins, wW=wW:
                       v.tensor_mul(ropet[64:128, 0:wW], st[0:64, 0:wW], ssins[0:64, 0:wW]))
                if kind == "k":
                    p.emit("vector", lambda v, sr=stg_rope, st=stg_raw:
                           v.tensor_copy(sr[:, kvW:kvW + 10], st[:, kvW:kvW + 10]))
                rtok = p.emit("vector", lambda v, sr=stg_rope, wW=wW:
                              v.tensor_add(sr[:, 0:wW], sr[:, 0:wW], ropet[:, 0:wW]), inc=True)
                raw_free[oc % 2] = rtok
                dwidth = qW if kind == "q" else kvW + 10
                dst = q_out[head] if kind == "q" else k_out[head]
                dtok = p.emit("sync", lambda s, dst=dst, sr=stg_rope, dwidth=dwidth: s.dma_start(
                    out=dst[:, 0:dwidth], in_=sr[:, 0:dwidth]), waits=[rtok], inc=True, dma="st%d" % (oc % 2))
                rope_free[oc % 2] = dtok
                out_toks.append(dtok)

        if ctx is None:
            p.finalize(block, final_waits=[p.chan_total("xout"), p.chan_total("st0"), p.chan_total("st1")])
    return nc


def build_s2(parity, ctx=None):
    P = PARAMS[parity]
    kvW, q0, qW = P["kvW"], P["q0"], P["qW"]
    KT = kvW // 128
    if ctx is not None:
        nc = ctx["nc"]
        _D = lambda name, shape, dt, kind: ctx["d"][name]
    else:
        nc = bass.Bass("TRN2", target_bir_lowering=False)
        _D = lambda name, shape, dt, kind: nc.dram_tensor(name, shape, dt, kind=kind).ap()

    q_in = _D("q_out", [32, 128, qW], BF16, "ExternalInput")
    k_in = _D("k_out", [32, 128, kvW + 10], BF16, "ExternalInput")
    v_in = _D("v_out", [32, 128, kvW + 10], BF16, "ExternalInput")
    gat = _D("gating", [1, 1], F32, "ExternalInput")
    ident = _D("ident", [128, 128], BF16, "ExternalInput")
    y_out = _D("y_out", [32, 128, qW], BF16, "ExternalOutput")

    ctxsfx = "_s2"
    from contextlib import ExitStack
    with ExitStack() as stk:
        if ctx is not None:
            def sb(name, shape, dt):
                if name not in ctx["bufs"]:
                    ctx["bufs"][name] = ctx["stk"].enter_context(nc.sbuf_tensor(name + ctxsfx, shape, dt))
                return ctx["bufs"][name]
        else:
            sb = lambda name, shape, dt: stk.enter_context(nc.sbuf_tensor(name, shape, dt))
        kb = [sb("kb0", [128, kvW + 10], BF16), sb("kb1", [128, kvW + 10], BF16)]
        vb = [sb("vb0", [128, kvW + 10], BF16), sb("vb1", [128, kvW + 10], BF16)]
        qb = [sb("qb0", [128, 1280], BF16), sb("qb1", [128, 1280], BF16)]
        vtm = [sb("vtm0", [128, kvW], BF16), sb("vtm1", [128, kvW], BF16)]
        avtm = [sb("avtm0", [10, 128], BF16), sb("avtm1", [10, 128], BF16)]
        expt = sb("expt", [128, 16 * 512 * 2], BF16)
        expa = sb("expa", [10, 1024], BF16)
        rowbuf = sb("rows", [1, 3072], F32)
        smv = sb("smv", [10, 1024], F32)
        rmv = sb("rmv", [128, 1024], BF16)
        ystage = [sb("yst0", [128, 1280], BF16), sb("yst1", [128, 1280], BF16)]
        onec = sb("onec", [128, 1], BF16)
        oner = sb("oner", [1, 128], F32)
        gatb = sb("gatb", [1, 1], F32)
        idb = sb("idb", [128, 128], BF16)
        if ctx is not None:
            pool_rot = ctx["pf32"][0:2]
            pool_y = ctx["pf32"][2:4]
            pool_sm = ctx["pf32"][4:6]
            ptrb = ctx["pbf"]
            p = ctx["p"]
        else:
            pool_rot = [stk.enter_context(nc.psum_tensor(f"pr{i}", [128, 512], F32)) for i in range(2)]
            pool_y = [stk.enter_context(nc.psum_tensor(f"py{i}", [128, 512], F32)) for i in range(2)]
            pool_sm = [stk.enter_context(nc.psum_tensor(f"pm{i}", [128, 512], F32)) for i in range(2)]
            ptrb = [stk.enter_context(nc.psum_tensor(f"pt{i}", [128, 1024], BF16)) for i in range(2)]
            block = stk.enter_context(nc.Block())
            p = Prog(nc)
            p.make_sems()
        rot = PsumPool(p, pool_rot)
        tailp = PsumPool(p, [ptrb[1][:, 0:1024].bitcast(F32)])

        p.emit("vector", lambda v: v.memset(onec[:, :], 1.0))
        p.emit("vector", lambda v: v.memset(oner[:, :], 1.0), inc=True)
        p.emit("gpsimd", lambda g: g.dma_start(out=gatb[:, :], in_=gat), inc=True, dma="xin")
        p.emit("gpsimd", lambda g: g.dma_start(out=idb[:, :], in_=ident), inc=True, dma="xin")
        xin_all = p.chan_total("xin")

        SCALE_M = 1.0 / float(np.sqrt(128.0))
        SCALE_A = 1.0 / 64.0
        head_done = {}
        tr_tok = [None, None]
        avp_tok = [None, None]
        ybank_tok = [None, None]
        sm_tok = [None, None]
        ystage_free = [None, None]
        pv_tok = [None, None]
        g_tok = [None, None]
        rb_tok = [None, None]
        for h in range(32):
            par = h % 2
            hd2 = head_done.get(h - 2)
            ld_k = p.emit("sync", lambda s, h=h, b=kb[par]: s.dma_start(out=b[:, 0:kvW + 10], in_=k_in[h][:, 0:kvW + 10]),
                          waits=[hd2], inc=True, dma="kld%d" % par)
            ld_v = p.emit("sync", lambda s, h=h, b=vb[par]: s.dma_start(out=b[:, 0:kvW + 10], in_=v_in[h][:, 0:kvW + 10]),
                          waits=[hd2], inc=True, dma="vld%d" % par)
            ld_q = p.emit("sync", lambda s, h=h, b=qb[par]: s.dma_start(out=b[:, 0:qW], in_=q_in[h][:, 0:qW]),
                          waits=[hd2], inc=True, dma="qld%d" % par)
            for kt in range(KT):
                bk = 0
                ttok = p.emit("tensor", lambda t, b=vb[par], kt=kt, o=ptrb[bk][:, 0:128]:
                              t.transpose(o, b[:, kt * 128:(kt + 1) * 128], idb[:, :]),
                              waits=[ld_v, xin_all, tr_tok[bk], hd2], inc=True)
                tr_tok[bk] = p.emit("vector", lambda v, o=vtm[par], kt=kt, i=ptrb[bk][:, 0:128]:
                                    v.tensor_copy(o[:, kt * 128:(kt + 1) * 128], i), waits=[ttok], inc=True)
            bk = 0
            ttok = p.emit("tensor", lambda t, b=vb[par], o=ptrb[bk][0:10, 0:128]:
                          t.transpose(o, b[:, kvW:kvW + 10], idb[:, :]),
                          waits=[ld_v, tr_tok[bk], avp_tok[bk]], inc=True)
            av_ev = p.emit("vector", lambda v, o=avtm[par], i=ptrb[bk][0:10, 0:128]:
                           v.tensor_copy(o[:, :], i), waits=[ttok], inc=True)
            avp_tok[bk] = av_ev
            vt_done = av_ev

            last_pva = None
            yev = None
            for (qc0, qcW) in tiles_of(qW):
                qa0 = q0 + qc0
                qcp = (qc0 // 512) % 2
                kt_hi = (qa0 + qcW - 1) // 128 + 1
                smb = pool_sm[qcp]
                ybank = pool_y[qcp]
                EB = qcp * 16 * 512
                AB = qcp * 512
                RB = qcp * 1536
                sum_last = None
                for kt in range(kt_hi):
                    bi, ps, wtok = rot.acquire(1)[0]
                    mm = p.emit("tensor", lambda t, ps=ps, b=kb[par], kt=kt, q=qb[par], qc0=qc0, qcW=qcW:
                                t.matmul(ps[:, 0:qcW], b[:, kt * 128:(kt + 1) * 128], q[:, qc0:qc0 + qcW],
                                         start=True, stop=True),
                                waits=[ld_k, ld_q, wtok], inc=True)
                    etok = p.emit("scalar", lambda s, ps=ps, kt=kt, qcW=qcW:
                                  s.activation(expt[:, EB + kt * 512: EB + kt * 512 + qcW], ps[:, 0:qcW], AF.Exp,
                                               scale=SCALE_M),
                                  waits=[mm, pv_tok[qcp], g_tok[qcp]], inc=True)
                    rot.release(bi, etok)
                    if kt * 128 + 127 >= qa0:
                        etok = p.emit("gpsimd", lambda g, kt=kt, qcW=qcW, qa0=qa0:
                                      g.affine_select(expt[:, EB + kt * 512: EB + kt * 512 + qcW],
                                                      expt[:, EB + kt * 512: EB + kt * 512 + qcW],
                                                      [[1, qcW]], ALU.is_ge, 0.0,
                                                      base=qa0 - kt * 128, channel_multiplier=-1),
                                      waits=[etok], inc=True)
                        g_tok[qcp] = etok
                    sum_last = p.emit("tensor", lambda t, ps=smb, kt=kt, qcW=qcW, first=(kt == 0), last=(kt == kt_hi - 1):
                                      t.matmul(ps[0:1, 0:qcW], onec[:, 0:1], expt[:, EB + kt * 512: EB + kt * 512 + qcW],
                                               start=first, stop=last),
                                      waits=[etok, sm_tok[qcp] if kt == 0 else None], inc=(kt == kt_hi - 1))
                    p.emit("tensor", lambda t, ps=ybank, kt=kt, qcW=qcW, par=par, first=(kt == 0):
                           t.matmul(ps[:, 0:qcW], vtm[par][:, kt * 128:(kt + 1) * 128],
                                    expt[:, EB + kt * 512: EB + kt * 512 + qcW], start=first, stop=False),
                           waits=[vt_done, etok, ybank_tok[qcp] if kt == 0 else None])
                bi, psa, wtok = tailp.acquire(1)[0]
                mma = p.emit("tensor", lambda t, ps=psa, b=kb[par], q=qb[par], qc0=qc0, qcW=qcW:
                             t.matmul(ps[0:10, 0:qcW], b[:, kvW:kvW + 10], q[:, qc0:qc0 + qcW],
                                      start=True, stop=True), waits=[wtok], inc=True)
                eta = p.emit("scalar", lambda s, ps=psa, qcW=qcW:
                             s.activation(expa[0:10, AB:AB + qcW], ps[0:10, 0:qcW], AF.Exp, scale=SCALE_A),
                             waits=[mma, pv_tok[qcp]], inc=True)
                tailp.release(bi, eta)
                bi2, psas, wtok2 = tailp.acquire(1)[0]
                asum = p.emit("tensor", lambda t, ps=psas, qcW=qcW:
                              t.matmul(ps[0:1, 0:qcW], onec[0:10, 0:1], expa[0:10, AB:AB + qcW],
                                       start=True, stop=True), waits=[eta, wtok2], inc=True)
                r_as = p.emit("vector", lambda v, i=psas, qcW=qcW: v.reciprocal(rowbuf[0:1, RB:RB + qcW], i[0:1, 0:qcW]),
                              waits=[asum, rb_tok[qcp]], inc=True)
                tailp.release(bi2, r_as)
                p.emit("vector", lambda v, i=smb, qcW=qcW:
                       v.tensor_mul(rowbuf[0:1, RB + 512:RB + 512 + qcW], i[0:1, 0:qcW], rowbuf[0:1, RB:RB + qcW]),
                       waits=[sum_last, xin_all])
                rS = p.emit("vector", lambda v, qcW=qcW:
                            v.tensor_scalar_mul(rowbuf[0:1, RB + 512:RB + 512 + qcW], rowbuf[0:1, RB + 512:RB + 512 + qcW], gatb[0:1, 0:1]),
                            inc=True)
                rR = p.emit("vector", lambda v, i=smb, qcW=qcW:
                            v.reciprocal(rowbuf[0:1, RB + 1024:RB + 1024 + qcW], i[0:1, 0:qcW]),
                            waits=[sum_last], inc=True)
                sm_tok[qcp] = rR
                bi3, psm, wtok3 = tailp.acquire(1)[0]
                mS = p.emit("tensor", lambda t, ps=psm, qcW=qcW:
                            t.matmul(ps[0:10, 0:qcW], oner[0:1, 0:10], rowbuf[0:1, RB + 512:RB + 512 + qcW],
                                     start=True, stop=True), waits=[rS, wtok3], inc=True)
                eS = p.emit("vector", lambda v, i=psm, qcW=qcW: v.tensor_copy(smv[0:10, AB:AB + qcW], i[0:10, 0:qcW]),
                            waits=[mS], inc=True)
                tailp.release(bi3, eS)
                sc = p.emit("vector", lambda v, qcW=qcW:
                            v.tensor_mul(expa[0:10, AB:AB + qcW], expa[0:10, AB:AB + qcW], smv[0:10, AB:AB + qcW]), inc=True)
                last_pva = p.emit("tensor", lambda t, ps=ybank, par=par, qcW=qcW:
                                  t.matmul(ps[:, 0:qcW], avtm[par][:, :], expa[0:10, AB:AB + qcW],
                                           start=False, stop=True), waits=[av_ev, sc], inc=True)
                bi4, psr, wtok4 = tailp.acquire(1)[0]
                mR = p.emit("tensor", lambda t, ps=psr, qcW=qcW:
                            t.matmul(ps[:, 0:qcW], oner[0:1, 0:128], rowbuf[0:1, RB + 1024:RB + 1024 + qcW],
                                     start=True, stop=True), waits=[rR, wtok4], inc=True)
                eR = p.emit("vector", lambda v, i=psr, qcW=qcW: v.tensor_copy(rmv[:, AB:AB + qcW], i[:, 0:qcW]),
                            waits=[mR], inc=True)
                tailp.release(bi4, eR)
                rb_tok[qcp] = mR
                yev = p.emit("vector", lambda v, ps=ybank, par=par, qc0=qc0, qcW=qcW:
                             v.tensor_mul(ystage[par][:, qc0:qc0 + qcW], ps[:, 0:qcW], rmv[:, AB:AB + qcW]),
                             waits=[last_pva, eR, ystage_free[par]], inc=True)
                ybank_tok[qcp] = yev
                pv_tok[qcp] = last_pva
            dtok = p.emit("sync", lambda s, h=h, par=par: s.dma_start(out=y_out[h][:, 0:qW], in_=ystage[par][:, 0:qW]),
                          waits=[yev], inc=True, dma="yo%d" % par)
            ystage_free[par] = dtok
            head_done[h] = last_pva
        if ctx is None:
            p.finalize(block, final_waits=[p.chan_total("yo0"), p.chan_total("yo1")])
    return nc


def build_s3(parity, ctx=None):
    P = PARAMS[parity]
    q0, qW = P["q0"], P["qW"]
    if ctx is not None:
        nc = ctx["nc"]
        _D = lambda name, shape, dt, kind: ctx["d"][name]
    else:
        nc = bass.Bass("TRN2", target_bir_lowering=False)
        _D = lambda name, shape, dt, kind: nc.dram_tensor(name, shape, dt, kind=kind).ap()

    y_in = _D("y_out", [32, 128, qW], BF16, "ExternalInput")
    xn_in = _D("xn_own", [NKC, 128, qW], BF16, "ExternalInput")
    wp_in = _D("w_proj_img", [32, 128, DM], F32, "ExternalInput")
    qw_in = _D("qw_img", [32, 128, DM], I32, "ExternalInput")
    sr_in = _D("s_rep", [32, 128, DM], F32, "ExternalInput")
    z16_in = _D("z16", [16, DM], F32, "ExternalInput")
    s16_in = _D("s16", [16, DM], F32, "ExternalInput")
    gim_in = _D("gimg", [128, NKC * 16], F32, "ExternalInput")
    b0_in = _D("bias0", [128, 32], F32, "ExternalInput")
    xq_in = _D("xfm_q", [DM, qW], F32, "ExternalInput")
    x1_out = _D("x1", [NKC, 128, qW], F32, "ExternalOutput")

    passes = [(0, 640), (640, qW - 640)]

    ctxsfx = "_s3"
    from contextlib import ExitStack
    with ExitStack() as stk:
        if ctx is not None:
            def sb(name, shape, dt):
                if name not in ctx["bufs"]:
                    ctx["bufs"][name] = ctx["stk"].enter_context(nc.sbuf_tensor(name + ctxsfx, shape, dt))
                return ctx["bufs"][name]
        else:
            sb = lambda name, shape, dt: stk.enter_context(nc.sbuf_tensor(name, shape, dt))
        pW = max(w for (_, w) in passes)
        ysb = sb("ysb", [128, NKC * pW], BF16)
        xnsb = sb("xnsb", [128, NKC * pW], BF16)
        wpb = [sb("wpb0", [128, DM], BF16), sb("wpb1", [128, DM], BF16)]
        qwb = [sb("qwb0", [128, DM], F16), sb("qwb1", [128, DM], F16)]
        srb = [sb("srb0", [128, DM], F16), sb("srb1", [128, DM], F16)]
        wdq = [sb("wdq0", [128, DM], BF16), sb("wdq1", [128, DM], BF16)]
        z16b = sb("z16b", [16, DM], BF16)
        s16b = sb("s16b", [16, DM], BF16)
        zsneg = sb("zsneg", [16, DM], BF16)
        gimb = sb("gimb", [128, NKC * 16], BF16)
        gsum = sb("gsum", [16, 1280], BF16)
        b0b = sb("b0b", [128, 32], F32)
        xts = [sb(f"xt{i}", [128, 512], F32) for i in range(4)]
        x1s = [sb(f"x1s{i}", [128, 512], F32) for i in range(4)]
        if ctx is not None:
            ppool = ctx["pf32"]
            p = ctx["p"]
        else:
            ppool = [stk.enter_context(nc.psum_tensor(f"pp{i}", [128, 512], F32)) for i in range(8)]
            block = stk.enter_context(nc.Block())
            p = Prog(nc)
            p.make_sems()
        pool = PsumPool(p, ppool)

        p.emit("gpsimd", lambda g: g.dma_start(out=z16b[:, :], in_=z16_in), inc=True, dma="tab")
        p.emit("gpsimd", lambda g: g.dma_start(out=s16b[:, :], in_=s16_in), inc=True, dma="tab")
        p.emit("gpsimd", lambda g: g.dma_start(out=gimb[:, :], in_=gim_in), inc=True, dma="tab")
        p.emit("gpsimd", lambda g: g.dma_start(out=b0b[:, :], in_=b0_in), inc=True, dma="tab")
        tab_all = p.chan_total("tab")
        p.emit("vector", lambda v: v.tensor_mul(zsneg[:, :], z16b[:, :], s16b[:, :]), waits=[tab_all])
        zs_tok = p.emit("vector", lambda v: v.tensor_scalar_mul(zsneg[:, :], zsneg[:, :], -1.0), inc=True)

        wl_tok = {}
        mm_last = {}
        dq_tok = {}
        xt_tok = [None] * 4
        x1_tok = [None] * 4
        slot = 0
        for pi, (p0, pw) in enumerate(passes):
            ych = "yin%d" % pi
            for kc in range(NKC):
                p.emit("sync", lambda s, kc=kc, p0=p0, pw=pw: s.dma_start(
                    out=ysb[:, kc * pW: kc * pW + pw], in_=y_in[kc][:, p0:p0 + pw]),
                    waits=[mm_last.get(("pass_end", pi - 1))], inc=True, dma=ych)
                p.emit("sync", lambda s, kc=kc, p0=p0, pw=pw: s.dma_start(
                    out=xnsb[:, kc * pW: kc * pW + pw], in_=xn_in[kc][:, p0:p0 + pw]),
                    inc=True, dma=ych)
            act_all = p.chan_total(ych)

            gs_toks = []
            for (t0, tw) in tiles_of(pw):
                bi, ps, wtok = pool.acquire(1)[0]
                last = None
                for kc in range(NKC):
                    last = p.emit("tensor", lambda t, ps=ps, kc=kc, t0=t0, tw=tw:
                                  t.matmul(ps[0:16, 0:tw], gimb[:, kc * 16:(kc + 1) * 16],
                                           xnsb[:, kc * pW + t0: kc * pW + t0 + tw],
                                           start=(kc == 0), stop=(kc == NKC - 1)),
                                  waits=[act_all, tab_all, wtok] if kc == 0 else (), inc=(kc == NKC - 1))
                ev = p.emit("vector", lambda v, ps=ps, t0=t0, tw=tw, p0=p0:
                            v.tensor_copy(gsum[0:16, p0 + t0: p0 + t0 + tw], ps[0:16, 0:tw]),
                            waits=[last], inc=True)
                pool.release(bi, ev)
                gs_toks.append(ev)
            gs_done = gs_toks[-1]

            for oci in range(32):
                oc = oci
                gi = pi * 32 + oci
                wb = wpb[gi % 2]
                qb_ = qwb[gi % 2]
                sr_ = srb[gi % 2]
                dq_ = wdq[gi % 2]
                prev = mm_last.get(gi - 2)
                wl = p.emit("gpsimd", lambda g, oc=oc, wb=wb: g.dma_start(out=wb[:, :], in_=wp_in[oc]),
                            waits=[prev], inc=True, dma="wp%d" % (gi % 2))
                ql = p.emit("gpsimd", lambda g, oc=oc, qb_=qb_: g.dma_start(out=qb_[:, :], in_=qw_in[oc]),
                            waits=[prev], inc=True, dma="qw%d" % (gi % 2))
                sl = p.emit("gpsimd", lambda g, oc=oc, sr_=sr_: g.dma_start(out=sr_[:, :], in_=sr_in[oc]),
                            waits=[prev], inc=True, dma="sr%d" % (gi % 2))
                deq_eng = "vector" if oci % 2 == 0 else "gpsimd"
                dq_tok[gi] = p.emit(deq_eng, lambda e, qb_=qb_, sr_=sr_, dq_=dq_:
                                    e.tensor_mul(dq_[:, :], qb_[:, :], sr_[:, :]),
                                    waits=[ql, sl, mm_last.get(gi - 2)], inc=True)
                last = None
                for (t0, tw) in tiles_of(pw):
                    bi, ps, wtok = pool.acquire(1)[0]
                    sl_x = slot % 4
                    xt = p.emit("sync", lambda s, oc=oc, t0=t0, tw=tw, b=xts[sl_x], p0=p0:
                                s.dma_start(out=b[:, 0:tw], in_=xq_in[oc * 128:(oc + 1) * 128, p0 + t0:p0 + t0 + tw]),
                                waits=[xt_tok[sl_x]], inc=True, dma="xt%d" % sl_x)
                    last = p.emit("tensor", lambda t, ps=ps, oc=oc, t0=t0, tw=tw, p0=p0:
                                  t.matmul(ps[:, 0:tw], zsneg[:, oc * 128:(oc + 1) * 128],
                                           gsum[0:16, p0 + t0: p0 + t0 + tw], start=True, stop=False),
                                  waits=[zs_tok, gs_done, wtok], inc=False)
                    for kc in range(NKC):
                        p.emit("tensor", lambda t, ps=ps, kc=kc, t0=t0, tw=tw, dq_=dq_:
                               t.matmul(ps[:, 0:tw], dq_[:, kc * 128:(kc + 1) * 128],
                                        xnsb[:, kc * pW + t0: kc * pW + t0 + tw], start=False, stop=False),
                               waits=[dq_tok[gi]] if kc == 0 else ())
                    for kc in range(NKC):
                        last = p.emit("tensor", lambda t, ps=ps, kc=kc, t0=t0, tw=tw, wb=wb:
                                      t.matmul(ps[:, 0:tw], wb[:, kc * 128:(kc + 1) * 128],
                                               ysb[:, kc * pW + t0: kc * pW + t0 + tw],
                                               start=False, stop=(kc == NKC - 1)),
                                      waits=[wl] if kc == 0 else (), inc=(kc == NKC - 1))
                    ev = p.emit("vector", lambda v, ps=ps, oc=oc, tw=tw, b=xts[sl_x], o=x1s[sl_x]:
                                v.scalar_tensor_tensor(o[:, 0:tw], ps[:, 0:tw], b0b[:, oc:oc + 1],
                                                       b[:, 0:tw], op0=ALU.add, op1=ALU.add),
                                waits=[last, xt, x1_tok[sl_x]], inc=True)
                    pool.release(bi, ev)
                    xt_tok[sl_x] = ev
                    dt = p.emit("sync", lambda s, oc=oc, t0=t0, tw=tw, o=x1s[sl_x], p0=p0:
                                s.dma_start(out=x1_out[oc][:, p0 + t0: p0 + t0 + tw], in_=o[:, 0:tw]),
                                waits=[ev], inc=True, dma="x1o%d" % sl_x)
                    x1_tok[sl_x] = dt
                    slot += 1
                mm_last[gi] = last
            mm_last[("pass_end", pi)] = mm_last[pi * 32 + 31]
        if ctx is None:
            p.finalize(block, final_waits=[p.chan_total("x1o%d" % i) for i in range(4)])
    return nc


def build_s4a(parity, ctx=None):
    P = PARAMS[parity]
    q0, qW, own0, ownW = P["q0"], P["qW"], P["own0"], P["ownW"]
    stride = ownW + 2
    lo = 1 if own0 == 0 else 0
    ln_cnt = ownW + 1
    x1_lo = own0 - q0 - 1 + lo
    if ctx is not None:
        nc = ctx["nc"]
        _D = lambda name, shape, dt, kind: ctx["d"][name]
    else:
        nc = bass.Bass("TRN2", target_bir_lowering=False)
        _D = lambda name, shape, dt, kind: nc.dram_tensor(name, shape, dt, kind=kind).ap()

    x1_in = _D("x1", [NKC, 128, qW], F32, "ExternalInput")
    cw_in = _D("conv_img", [24, 128, 3 * DM], F32, "ExternalInput")
    cb_in = _D("conv_b", [128, 24], F32, "ExternalInput")
    qkv1_out = _D("qkv1", [24, 128, ownW], BF16, "ExternalOutput")
    xn1_out = _D("xn1_own", [NKC, 128, ownW], BF16, "ExternalOutput")

    ctxsfx = "_s4a"
    from contextlib import ExitStack
    with ExitStack() as stk:
        if ctx is not None:
            def sb(name, shape, dt):
                if name not in ctx["bufs"]:
                    ctx["bufs"][name] = ctx["stk"].enter_context(nc.sbuf_tensor(name + ctxsfx, shape, dt))
                return ctx["bufs"][name]
        else:
            sb = lambda name, shape, dt: stk.enter_context(nc.sbuf_tensor(name, shape, dt))
        xn1 = sb("xn1", [128, NKC * 1154], BF16)
        cwb = [sb("cwb0", [128, 3 * DM], BF16), sb("cwb1", [128, 3 * DM], BF16)]
        cbb = sb("cbb", [128, 24], F32)
        am = sb("am", [128, 1153], BF16)
        bm = sb("bm", [128, 1153], BF16)
        sq_scr = sb("sq", [128, 1024], BF16)
        rowbuf = sb("rows", [1, 6 * 512], F32)
        onec = sb("onec", [128, 1], BF16)
        oner = sb("oner", [1, 128], F32)
        qst = [sb(f"qst{i}", [128, 512], BF16) for i in range(4)]
        if ctx is not None:
            ppool = ctx["pf32"]
            p = ctx["p"]
        else:
            ppool = [stk.enter_context(nc.psum_tensor(f"pp{i}", [128, 512], F32)) for i in range(8)]
            block = stk.enter_context(nc.Block())
            p = Prog(nc)
            p.make_sems()
        pool = PsumPool(p, ppool)

        p.emit("vector", lambda v: v.memset(onec[:, :], 1.0))
        p.emit("vector", lambda v: v.memset(oner[:, :], 1.0), inc=True)
        padc = 0 if own0 == 0 else ownW + 1
        zp = []
        for kc in range(NKC):
            zp.append(p.emit("vector", lambda v, kc=kc: v.memset(
                xn1[:, kc * stride + padc: kc * stride + padc + 1], 0.0), inc=(kc == NKC - 1)))
        pad_tok = zp[-1]
        p.emit("gpsimd", lambda g: g.dma_start(out=cbb[:, :], in_=cb_in), inc=True, dma="tab")
        tab_all = p.chan_total("tab")
        for kc in range(NKC):
            p.emit("gpsimd", lambda g, kc=kc: g.dma_start(
                out=xn1[:, kc * stride + lo: kc * stride + lo + ln_cnt],
                in_=x1_in[kc][:, x1_lo: x1_lo + ln_cnt]), inc=True, dma="xin")
        xin_all = p.chan_total("xin")

        ln_toks = emit_ln(p, pool, xn1, stride, ln_cnt, am, bm, sq_scr, rowbuf,
                          onec, oner, xin_all, col0=lo)
        ln_done = ln_toks[-1]

        for kc in range(NKC):
            p.emit("sync", lambda s, kc=kc: s.dma_start(
                out=xn1_out[kc][:, 0:ownW], in_=xn1[:, kc * stride + 1: kc * stride + 1 + ownW]),
                waits=[ln_toks[kc]], inc=True, dma="xno")

        wl_tok = {}
        mm_last = {}
        st_tok = [None] * 4
        slot = 0
        for oc in range(24):
            wb = cwb[oc % 2]
            wl = p.emit("gpsimd", lambda g, oc=oc, wb=wb: g.dma_start(out=wb[:, :], in_=cw_in[oc]),
                        waits=[mm_last.get(oc - 2)], inc=True, dma="cw%d" % (oc % 2))
            last = None
            for (t0, tw) in tiles_of(ownW):
                bi, ps, wtok = pool.acquire(1)[0]
                for w in range(3):
                    for kc in range(NKC):
                        first = (w == 0 and kc == 0)
                        last = p.emit("tensor", lambda t, ps=ps, w=w, kc=kc, t0=t0, tw=tw, wb=wb:
                                      t.matmul(ps[:, 0:tw], wb[:, (w * NKC + kc) * 128:(w * NKC + kc + 1) * 128],
                                               xn1[:, kc * stride + t0 + w: kc * stride + t0 + w + tw],
                                               start=(w == 0 and kc == 0), stop=(w == 2 and kc == NKC - 1)),
                                      waits=[wl, ln_done, pad_tok, wtok] if first else (),
                                      inc=(w == 2 and kc == NKC - 1))
                sl = slot % 4
                ev = p.emit("scalar", lambda s, ps=ps, oc=oc, tw=tw, o=qst[sl]:
                            s.add(o[:, 0:tw], ps[:, 0:tw], cbb[:, oc:oc + 1]),
                            waits=[last, tab_all, st_tok[sl]], inc=True)
                pool.release(bi, ev)
                dt = p.emit("sync", lambda s, oc=oc, t0=t0, tw=tw, o=qst[sl]:
                            s.dma_start(out=qkv1_out[oc][:, t0:t0 + tw], in_=o[:, 0:tw]),
                            waits=[ev], inc=True, dma="qo%d" % sl)
                st_tok[sl] = dt
                slot += 1
            mm_last[oc] = last
        if ctx is None:
            p.finalize(block, final_waits=[p.chan_total("xno")] + [p.chan_total("qo%d" % i) for i in range(4)])
    return nc


def build_s4b(parity, stop_after=None, skip=(), ctx=None):
    P = PARAMS[parity]
    ownW = P["ownW"]
    NB = ownW // 128
    if ctx is not None:
        nc = ctx["nc"]
        _D = lambda name, shape, dt, kind: ctx["d"][name]
    else:
        nc = bass.Bass("TRN2", target_bir_lowering=False)
        _D = lambda name, shape, dt, kind: nc.dram_tensor(name, shape, dt, kind=kind).ap()

    qkv1_in = _D("qkv1", [24, 128, ownW], BF16, "ExternalInput")
    wq_in = _D("wq_img", [8, 128, 1024], F32, "ExternalInput")
    wk_in = _D("wk_img", [8, 128, 1024], F32, "ExternalInput")
    wv_in = _D("wv_img", [8, 128, 1024], F32, "ExternalInput")
    wo_in = _D("wo_img", [8, 128, 1024], F32, "ExternalInput")
    bq_in = _D("bq", [128, 8], F32, "ExternalInput")
    bk_in = _D("bk", [128, 8], F32, "ExternalInput")
    bv_in = _D("bv_col", [128, 8], F32, "ExternalInput")
    bo_in = _D("bo", [128, 8], F32, "ExternalInput")
    id_in = _D("ident", [128, 128], BF16, "ExternalInput")
    o2_out = _D("o2", [8, 128, ownW], BF16, "ExternalOutput")

    ctxsfx = "_s4b"
    from contextlib import ExitStack
    with ExitStack() as stk:
        if ctx is not None:
            def sb(name, shape, dt):
                if name not in ctx["bufs"]:
                    ctx["bufs"][name] = ctx["stk"].enter_context(nc.sbuf_tensor(name + ctxsfx, shape, dt))
                return ctx["bufs"][name]
        else:
            sb = lambda name, shape, dt: stk.enter_context(nc.sbuf_tensor(name, shape, dt))
        qkv1 = sb("qkv1b", [128, 24 * 1152], BF16)
        whb = [sb("whb0", [128, 1024], BF16), sb("whb1", [128, 1024], BF16)]
        wob = sb("wob", [128, 8 * 1024], BF16)
        qh = sb("qh", [128, 8 * 1152], BF16)
        kh = sb("kh", [128, 8 * 1152], BF16)
        vh = sb("vh", [128, 8 * 1152], BF16)
        ofm = sb("ofm", [128, 8 * 1152], BF16)
        vtmb = sb("vtmb", [128, 1024], BF16)
        expq = sb("expq", [128, 512], BF16)
        r4 = sb("r4", [1, 512], F32)
        rmq = sb("rmq", [64, 512], BF16)
        wobias = sb("wobias", [128, 8], F32)
        bqb = sb("bqb", [128, 8], F32)
        bkb = sb("bkb", [128, 8], F32)
        bvb = sb("bvb", [128, 8], BF16)
        bob = sb("bob", [128, 8], F32)
        idb = sb("idb", [128, 128], BF16)
        onec = sb("onec", [128, 1], BF16)
        oner = sb("oner", [1, 128], F32)
        ost = [sb(f"ost{i}", [128, 512], BF16) for i in range(4)]
        if ctx is not None:
            pool6 = ctx["pf32"][0:6]
            ptrb = ctx["pbf"]
            p = ctx["p"]
        else:
            pool6 = [stk.enter_context(nc.psum_tensor(f"pp{i}", [128, 512], F32)) for i in range(6)]
            ptrb = [stk.enter_context(nc.psum_tensor(f"pt{i}", [128, 1024], BF16)) for i in range(2)]
            block = stk.enter_context(nc.Block())
            p = Prog(nc)
            p.make_sems()
        pool = PsumPool(p, pool6)

        p.emit("vector", lambda v: v.memset(onec[:, :], 1.0))
        p.emit("vector", lambda v: v.memset(oner[:, :], 1.0), inc=True)
        for (dst, src) in [(bqb, bq_in), (bkb, bk_in), (bvb, bv_in), (bob, bo_in), (idb, id_in)]:
            p.emit("gpsimd", lambda g, dst=dst, src=src: g.dma_start(out=dst[:, :], in_=src),
                   inc=True, dma="tab")
        for oc in range(8):
            p.emit("gpsimd", lambda g, oc=oc: g.dma_start(out=wob[:, oc * 1024:(oc + 1) * 1024], in_=wo_in[oc]),
                   inc=True, dma="tab")
        tab_all = p.chan_total("tab")
        for ch in range(24):
            p.emit("sync", lambda s, ch=ch: s.dma_start(out=qkv1[:, ch * ownW:(ch + 1) * ownW], in_=qkv1_in[ch][:, 0:ownW]),
                   inc=True, dma="qin")
        qin_all = p.chan_total("qin")

        wb_toks = []
        for oc in range(8):
            bi, ps, wtok = pool.acquire(1)[0]
            last = None
            for kc in range(8):
                last = p.emit("tensor", lambda t, ps=ps, oc=oc, kc=kc:
                              t.matmul(ps[:, 0:1], wob[:, oc * 1024 + kc * 128: oc * 1024 + (kc + 1) * 128],
                                       bvb[:, kc:kc + 1], start=(kc == 0), stop=(kc == 7)),
                              waits=[tab_all, wtok] if kc == 0 else (), inc=(kc == 7))
            ev = p.emit("vector", lambda v, ps=ps, oc=oc: v.tensor_copy(wobias[:, oc:oc + 1], ps[:, 0:1]),
                        waits=[last], inc=True)
            pool.release(bi, ev)
            wb_toks.append(ev)
        wob_tok = p.emit("vector", lambda v: v.tensor_add(wobias[:, :], wobias[:, :], bob[:, :]),
                         waits=[tab_all], inc=True)

        st_tok = [None] * 4
        proj_done_s = None
        proj_done_v = None
        gi = 0
        wl_prev = {}
        for (kind, img, dstbuf, bias) in [(0, wq_in, qh, bqb), (1, wk_in, kh, bkb), (2, wv_in, vh, None)]:
            for oc in range(8):
                wb = whb[gi % 2]
                wl = p.emit("gpsimd", lambda g, img=img, oc=oc, wb=wb: g.dma_start(out=wb[:, :], in_=img[oc]),
                            waits=[wl_prev.get(gi - 2)], inc=True, dma="wh%d" % (gi % 2))
                last = None
                for (t0, tw) in tiles_of(ownW):
                    bi, ps, wtok = pool.acquire(1)[0]
                    for kc in range(8):
                        last = p.emit("tensor", lambda t, ps=ps, kc=kc, t0=t0, tw=tw, wb=wb, kind=kind:
                                      t.matmul(ps[:, 0:tw], wb[:, kc * 128:(kc + 1) * 128],
                                               qkv1[:, (kind * 8 + kc) * ownW + t0:(kind * 8 + kc) * ownW + t0 + tw],
                                               start=(kc == 0), stop=(kc == 7)),
                                      waits=[wl, qin_all, wtok] if kc == 0 else (), inc=(kc == 7))
                    for hh in range(2):
                        h = 2 * oc + hh
                        rb = 64 * (h // 8)
                        cb = (h % 8) * ownW
                        if bias is not None:
                            ev = p.emit("scalar", lambda s, ps=ps, hh=hh, oc=oc, t0=t0, tw=tw, dstbuf=dstbuf,
                                        bias=bias, rb=rb, cb=cb:
                                        s.add(dstbuf[rb:rb + 64, cb + t0: cb + t0 + tw],
                                              ps[64 * hh:64 * hh + 64, 0:tw], bias[64 * hh:64 * hh + 64, oc:oc + 1]),
                                        waits=[last, tab_all], inc=True)
                            proj_done_s = ev
                        else:
                            ev = p.emit("vector", lambda v, ps=ps, hh=hh, oc=oc, t0=t0, tw=tw, dstbuf=dstbuf,
                                        rb=rb, cb=cb:
                                        v.tensor_copy(dstbuf[rb:rb + 64, cb + t0: cb + t0 + tw],
                                                      ps[64 * hh:64 * hh + 64, 0:tw]),
                                        waits=[last], inc=True)
                            proj_done_v = ev
                    pool.release(bi, ev)
                wl_prev[gi] = last
                gi += 1
        if stop_after == "proj" and ctx is None:
            dbg = p.emit("sync", lambda s: s.dma_start(out=o2_out[0], in_=qh[:, 0:ownW]), waits=[proj_done_s, proj_done_v], inc=True, dma="oo0")
            p.finalize(block, final_waits=[p.chan_total("oo0")])
            return nc
        tr_tok = [None, None]
        prev_quad_t = None
        prev_quad_g = None
        prev_b_pv = None
        prev_rml = None
        for b in range(NB):
            vt_evs = []
            for h in range(16):
                bk_ = h % 2
                rb = 64 * (h // 8)
                cb = (h % 8) * ownW
                ttok = p.emit("tensor", lambda t, h=h, b=b, rb=rb, cb=cb, o=ptrb[bk_][:, 0:64]:
                              t.transpose(o, vh[rb:rb + 64, cb + b * 128: cb + b * 128 + 128], idb[rb:rb + 64, rb:rb + 64]),
                              waits=[proj_done_v, tab_all, tr_tok[bk_], prev_b_pv], inc=True)
                ev = p.emit("vector", lambda v, h=h, i=ptrb[bk_][:, 0:64]:
                            v.tensor_copy(vtmb[:, h * 64:(h + 1) * 64], i), waits=[ttok], inc=True)
                tr_tok[bk_] = ev
                vt_evs.append(ev)
            vt_done = vt_evs[-1]
            if stop_after == "trans" and b == 0 and ctx is None:
                dbg = p.emit("sync", lambda s: s.dma_start(out=o2_out[0][:, 0:1024], in_=vtmb[:, :]), waits=[vt_done], inc=True, dma="oo0")
                p.finalize(block, final_waits=[p.chan_total("oo0")])
                return nc
            for qd in range(4):
                bi, psc, wtok = pool.acquire(1)[0]
                mms = None
                rbq = 64 * (qd // 2)
                for j in range(4):
                    h = 4 * qd + j
                    cb = (h % 8) * ownW
                    mms = p.emit("tensor", lambda t, ps=psc, j=j, cb=cb, rbq=rbq, b=b:
                                 t.matmul(ps[:, j * 128:(j + 1) * 128],
                                          kh[rbq:rbq + 64, cb + b * 128: cb + b * 128 + 128],
                                          qh[rbq:rbq + 64, cb + b * 128: cb + b * 128 + 128],
                                          start=True, stop=True),
                                 waits=[proj_done_s, proj_done_v, wtok] if j == 0 else (), inc=(j == 3))
                etok = p.emit("scalar", lambda s, ps=psc: s.activation(expq[:, :], ps[:, 0:512], AF.Exp, scale=0.125),
                              waits=[mms, prev_quad_t, prev_quad_g], inc=True)
                rot_rel_sc = etok
                if "mask" not in skip:
                    mtok = p.emit("gpsimd", lambda g: g.affine_select(expq[:, :], expq[:, :],
                                                                      [[0, 4], [1, 128]], ALU.is_ge, 0.0,
                                                                      base=0, channel_multiplier=-1),
                                  waits=[etok], inc=True)
                    prev_quad_g = mtok
                else:
                    mtok = etok
                bi2, pss, wtok2 = pool.acquire(1)[0]
                smm = p.emit("tensor", lambda t, ps=pss: t.matmul(ps[0:1, 0:512], onec[:, 0:1], expq[:, :],
                                                                  start=True, stop=True),
                             waits=[mtok, wtok2], inc=True)
                rtok = p.emit("vector", lambda v, ps=pss: v.reciprocal(r4[0:1, :], ps[0:1, 0:512]),
                              waits=[smm, prev_rml], inc=True)
                pool.release(bi2, rtok)
                bi3, pso, wtok3 = pool.acquire(1)[0]
                pvl = None
                for j in range(4):
                    h = 4 * qd + j
                    pvl = p.emit("tensor", lambda t, ps=pso, j=j, h=h:
                                 t.matmul(ps[0:64, j * 128:(j + 1) * 128],
                                          vtmb[:, h * 64: h * 64 + 64],
                                          expq[:, j * 128:(j + 1) * 128], start=True, stop=True),
                                 waits=[vt_done, mtok, wtok3] if j == 0 else (), inc=(j == 3))
                pool.release(bi, etok)
                bi4, psr, wtok4 = pool.acquire(1)[0]
                rml = None
                for j in ([] if "rm" in skip else range(4)):
                    rml = p.emit("tensor", lambda t, ps=psr, j=j:
                                 t.matmul(ps[0:64, j * 128:(j + 1) * 128],
                                          oner[0:1, 0:64], r4[0:1, j * 128:(j + 1) * 128],
                                          start=True, stop=True),
                                 waits=[rtok, wtok4] if j == 0 else (), inc=(j == 3))
                if "rm" in skip:
                    rml = rtok
                    erm = p.emit("vector", lambda v: v.memset(rmq[:, :], 1.0), waits=[rtok], inc=True)
                else:
                    erm = p.emit("vector", lambda v, ps=psr: v.tensor_copy(rmq[0:64, :], ps[0:64, 0:512]), waits=[rml], inc=True)
                prev_rml = rml
                pool.release(bi4, erm)
                ev2 = None
                for j in range(4):
                    h = 4 * qd + j
                    rb = 64 * (h % 2)
                    cb = (h // 2) * ownW
                    ev2 = p.emit("vector", lambda v, ps=pso, j=j, rb=rb, cb=cb, b=b:
                                 v.tensor_mul(ofm[rb:rb + 64, cb + b * 128: cb + b * 128 + 128],
                                              ps[0:64, j * 128:(j + 1) * 128], rmq[0:64, j * 128:(j + 1) * 128]),
                                 waits=[pvl, erm], inc=(j == 3))
                pool.release(bi3, ev2)
                prev_quad_t = pvl
                prev_b_pv = pvl
                ofm_tok = ev2
        if stop_after == "attn" and ctx is None:
            dbg = p.emit("sync", lambda s: s.dma_start(out=o2_out[0], in_=ofm[:, 0:ownW]), waits=[ofm_tok], inc=True, dma="oo0")
            p.finalize(block, final_waits=[p.chan_total("oo0")])
            return nc
        st_tok2 = [None] * 4
        slot = 0
        out_chans = []
        for oc in range(8):
            last = None
            for (t0, tw) in tiles_of(ownW):
                bi, ps, wtok = pool.acquire(1)[0]
                for kc in range(8):
                    last = p.emit("tensor", lambda t, ps=ps, oc=oc, kc=kc, t0=t0, tw=tw:
                                  t.matmul(ps[:, 0:tw], wob[:, oc * 1024 + kc * 128: oc * 1024 + (kc + 1) * 128],
                                           ofm[:, kc * ownW + t0: kc * ownW + t0 + tw],
                                           start=(kc == 0), stop=(kc == 7)),
                                  waits=[ofm_tok, wob_tok, wtok] if kc == 0 else (), inc=(kc == 7))
                sl = slot % 4
                ev = p.emit("scalar", lambda s, ps=ps, oc=oc, tw=tw, o=ost[sl]:
                            s.add(o[:, 0:tw], ps[:, 0:tw], wobias[:, oc:oc + 1]),
                            waits=[last, st_tok2[sl], wob_tok], inc=True)
                pool.release(bi, ev)
                dt = p.emit("sync", lambda s, oc=oc, t0=t0, tw=tw, o=ost[sl]:
                            s.dma_start(out=o2_out[oc][:, t0:t0 + tw], in_=o[:, 0:tw]),
                            waits=[ev], inc=True, dma="oo%d" % sl)
                st_tok2[sl] = dt
                slot += 1
        if ctx is None:
            p.finalize(block, final_waits=[p.chan_total("oo%d" % i) for i in range(4)])
    return nc


SORT8_NET = [(0, 1), (2, 3), (4, 5), (6, 7),
             (0, 2), (1, 3), (4, 6), (5, 7),
             (1, 2), (5, 6), (0, 4), (3, 7),
             (1, 5), (2, 6), (1, 4), (3, 6),
             (2, 4), (3, 5), (3, 4)]


def build_s5(parity, ctx=None):
    P = PARAMS[parity]
    q0, qW, own0, ownW = P["q0"], P["qW"], P["own0"], P["ownW"]
    NT = ownW // 128
    x1_own = own0 - q0
    if ctx is not None:
        nc = ctx["nc"]
        _D = lambda name, shape, dt, kind: ctx["d"][name]
    else:
        nc = bass.Bass("TRN2", target_bir_lowering=False)
        _D = lambda name, shape, dt, kind: nc.dram_tensor(name, shape, dt, kind=kind).ap()

    x1_in = _D("x1", [NKC, 128, qW], F32, "ExternalInput")
    xn_in = _D("xn1_own", [NKC, 128, ownW], BF16, "ExternalInput")
    o2_in = _D("o2", [8, 128, ownW], BF16, "ExternalInput")
    wb_in = _D("wbase_img", [32, 128, DM], F32, "ExternalInput")
    pj_in = _D("projw_img", [32, 128, 1024], F32, "ExternalInput")
    wr_in = _D("wr_img", [128, NKC * 8], F32, "ExternalInput")
    br_in = _D("br", [8, 1], F32, "ExternalInput")
    A_in = _D("A_img", [128, DM], F32, "ExternalInput")
    B_in = _D("B_sb", [128, DM], F32, "ExternalInput")
    rep_in = _D("rep", [8, 128], F32, "ExternalInput")
    bb_in = _D("bbase", [128, 32], F32, "ExternalInput")
    pb_in = _D("projb", [128, 32], F32, "ExternalInput")
    idf_in = _D("identf", [128, 128], F32, "ExternalInput")
    out_d = _D("outp", [NKC, 128, ownW], F32, "ExternalOutput")

    ctxsfx = "_s5"
    from contextlib import ExitStack
    with ExitStack() as stk:
        if ctx is not None:
            def sb(name, shape, dt):
                if name not in ctx["bufs"]:
                    ctx["bufs"][name] = ctx["stk"].enter_context(nc.sbuf_tensor(name + ctxsfx, shape, dt))
                return ctx["bufs"][name]
        else:
            sb = lambda name, shape, dt: stk.enter_context(nc.sbuf_tensor(name, shape, dt))
        xnb = sb("xnb", [128, NKC * 1152], BF16)
        o2b = sb("o2b", [128, 8 * 1152], BF16)
        wbb = [sb("wbb0", [128, DM], BF16), sb("wbb1", [128, DM], BF16)]
        pjb = [sb("pjb0", [128, 1024], BF16), sb("pjb1", [128, 1024], BF16)]
        wrb = sb("wrb", [128, NKC * 8], BF16)
        brb = sb("brb", [8, 1], F32)
        ab = sb("ab", [128, DM], BF16)
        bb = sb("bb", [128, DM], BF16)
        repb = sb("repb", [8, 128], BF16)
        btot = sb("btot", [128, 32], F32)
        pbb = sb("pbb", [128, 32], F32)
        idf = sb("idf", [128, 128], F32)
        logits = sb("logits", [8, 2048], F32)
        ztm = sb("ztm", [128, 72], F32)
        zs = sb("zs", [128, 72], F32)
        zc = sb("zc", [128, 72], F32)
        tmp = sb("tmpv", [128, 9], F32)
        tau = sb("tau", [128, 9], F32)
        wsp = sb("wsp", [128, 72], F32)
        wem = sb("wem", [8, 1152], BF16)
        wrep = sb("wrep", [128, 1152], BF16)
        hsb = sb("hsb", [128, 1152], BF16)
        xts = [sb(f"xt{i}", [128, 512], F32) for i in range(4)]
        x1s = [sb(f"x1s{i}", [128, 512], F32) for i in range(4)]
        if ctx is not None:
            ppool = ctx["pf32"]
            p = ctx["p"]
        else:
            ppool = [stk.enter_context(nc.psum_tensor(f"pp{i}", [128, 512], F32)) for i in range(8)]
            block = stk.enter_context(nc.Block())
            p = Prog(nc)
            p.make_sems()
        pool = PsumPool(p, ppool)

        for (dst, src) in [(wrb, wr_in), (brb, br_in), (ab, A_in), (bb, B_in), (repb, rep_in),
                           (btot, bb_in), (pbb, pb_in), (idf, idf_in)]:
            p.emit("gpsimd", lambda g, dst=dst, src=src: g.dma_start(out=dst[:, :], in_=src),
                   inc=True, dma="tab")
        tab_all = p.chan_total("tab")
        bt_tok = p.emit("vector", lambda v: v.tensor_add(btot[:, :], btot[:, :], pbb[:, :]),
                        waits=[tab_all], inc=True)
        for kc in range(NKC):
            p.emit("sync", lambda s, kc=kc: s.dma_start(out=xnb[:, kc * ownW:(kc + 1) * ownW], in_=xn_in[kc][:, 0:ownW]),
                   inc=True, dma="xin")
        for ch in range(8):
            p.emit("sync", lambda s, ch=ch: s.dma_start(out=o2b[:, ch * ownW:(ch + 1) * ownW], in_=o2_in[ch][:, 0:ownW]),
                   inc=True, dma="xin")
        xin_all = p.chan_total("xin")

        lg_toks = []
        for (t0, tw) in tiles_of(ownW):
            bi, ps, wtok = pool.acquire(1)[0]
            last = None
            for kc in range(NKC):
                last = p.emit("tensor", lambda t, ps=ps, kc=kc, t0=t0, tw=tw:
                              t.matmul(ps[0:8, 0:tw], wrb[:, kc * 8:(kc + 1) * 8],
                                       xnb[:, kc * ownW + t0: kc * ownW + t0 + tw],
                                       start=(kc == 0), stop=(kc == NKC - 1)),
                              waits=[tab_all, xin_all, wtok] if kc == 0 else (), inc=(kc == NKC - 1))
            ev = p.emit("scalar", lambda s, ps=ps, t0=t0, tw=tw:
                        s.add(logits[0:8, t0:t0 + tw], ps[0:8, 0:tw], brb[0:8, 0:1]),
                        waits=[last], inc=True)
            pool.release(bi, ev)
            lg_toks.append(ev)
        lg_done = lg_toks[-1]

        tr_evs = []
        for tt in range(NT):
            bi, ps, wtok = pool.acquire(1)[0]
            ttok = p.emit("tensor", lambda t, ps=ps, tt=tt:
                          t.transpose(ps[0:128, 0:8], logits[0:8, tt * 128:(tt + 1) * 128], idf[0:8, 0:8]),
                          waits=[lg_done, tab_all, wtok], inc=True)
            ev = p.emit("vector", lambda v, ps=ps, tt=tt:
                        v.tensor_copy(ztm[:, tt * 8:(tt + 1) * 8], ps[0:128, 0:8]), waits=[ttok], inc=True)
            pool.release(bi, ev)
            tr_evs.append(ev)
        def esl(buf, e):
            return bass.AP(buf[:, :].tensor, buf[:, :].offset + e, [[buf[:, :].ap[0][0], 128], [8, NT]])
        cp = p.emit("vector", lambda v: v.tensor_copy(zs[:, :], ztm[:, :]), waits=[tr_evs[-1]])
        for (i, j) in SORT8_NET:
            p.emit("vector", lambda v, i=i, j=j: v.tensor_tensor(tmp[:, 0:NT], esl(zs, i), esl(zs, j), ALU.min))
            p.emit("vector", lambda v, i=i, j=j: v.tensor_tensor(esl(zs, i), esl(zs, i), esl(zs, j), ALU.max))
            p.emit("vector", lambda v, j=j: v.tensor_copy(esl(zs, j), tmp[:, 0:NT]))
        p.emit("vector", lambda v: v.tensor_copy(esl(zc, 0), esl(zs, 0)))
        for k in range(1, 8):
            p.emit("vector", lambda v, k=k: v.tensor_add(esl(zc, k), esl(zc, k - 1), esl(zs, k)))
        for k in range(8):
            p.emit("vector", lambda v, k=k: v.tensor_scalar(esl(wsp, k), esl(zc, k),
                                                            1.0 / (k + 1), -1.0 / (k + 1),
                                                            op0=ALU.mult, op1=ALU.add))
        p.emit("vector", lambda v: v.tensor_tensor(tau[:, 0:NT], esl(wsp, 0), esl(wsp, 1), ALU.max))
        for k in range(2, 8):
            p.emit("vector", lambda v, k=k: v.tensor_tensor(tau[:, 0:NT], tau[:, 0:NT], esl(wsp, k), ALU.max))
        for e in range(8):
            p.emit("vector", lambda v, e=e: v.tensor_tensor(esl(wsp, e), esl(ztm, e), tau[:, 0:NT], ALU.subtract))
        sp_done = p.emit("vector", lambda v: v.tensor_scalar_max(wsp[:, :], wsp[:, :], 0.0), inc=True)

        wt_evs = []
        for tt in range(NT):
            bi, ps, wtok = pool.acquire(1)[0]
            ttok = p.emit("tensor", lambda t, ps=ps, tt=tt:
                          t.transpose(ps[0:8, 0:128], wsp[:, tt * 8:(tt + 1) * 8], idf[:, :]),
                          waits=[sp_done, wtok], inc=True)
            ev = p.emit("vector", lambda v, ps=ps, tt=tt:
                        v.tensor_copy(wem[0:8, tt * 128:(tt + 1) * 128], ps[0:8, 0:128]), waits=[ttok], inc=True)
            pool.release(bi, ev)
            wt_evs.append(ev)
        wem_done = wt_evs[-1]

        h_toks = []
        for (t0, tw) in tiles_of(ownW):
            bi, ps, wtok = pool.acquire(1)[0]
            mr = p.emit("tensor", lambda t, ps=ps, t0=t0, tw=tw:
                        t.matmul(ps[:, 0:tw], repb[0:8, :], wem[0:8, t0:t0 + tw], start=True, stop=True),
                        waits=[wem_done, tab_all, wtok], inc=True)
            er = p.emit("vector", lambda v, ps=ps, t0=t0, tw=tw:
                        v.tensor_copy(wrep[:, t0:t0 + tw], ps[:, 0:tw]), waits=[mr], inc=True)
            pool.release(bi, er)
            bi2, ps2, wtok2 = pool.acquire(1)[0]
            last = None
            for kc in range(NKC):
                last = p.emit("tensor", lambda t, ps=ps2, kc=kc, t0=t0, tw=tw:
                              t.matmul(ps[:, 0:tw], ab[:, kc * 128:(kc + 1) * 128],
                                       xnb[:, kc * ownW + t0: kc * ownW + t0 + tw],
                                       start=(kc == 0), stop=(kc == NKC - 1)),
                              waits=[tab_all, wtok2] if kc == 0 else (), inc=(kc == NKC - 1))
            eh = p.emit("vector", lambda v, ps=ps2, t0=t0, tw=tw:
                        v.tensor_copy(hsb[:, t0:t0 + tw], ps[:, 0:tw]), waits=[last], inc=True)
            pool.release(bi2, eh)
            ht = p.emit("vector", lambda v, t0=t0, tw=tw:
                        v.tensor_mul(hsb[:, t0:t0 + tw], hsb[:, t0:t0 + tw], wrep[:, t0:t0 + tw]),
                        waits=[er], inc=True)
            h_toks.append(ht)
        h_done = h_toks[-1]

        mm_last = {}
        xt_tok = [None] * 4
        x1_tok = [None] * 4
        slot = 0
        for oc in range(32):
            wb = wbb[oc % 2]
            pj = pjb[oc % 2]
            prev = mm_last.get(oc - 2)
            wl = p.emit("gpsimd", lambda g, oc=oc, wb=wb: g.dma_start(out=wb[:, :], in_=wb_in[oc]),
                        waits=[prev], inc=True, dma="wb%d" % (oc % 2))
            pl = p.emit("gpsimd", lambda g, oc=oc, pj=pj: g.dma_start(out=pj[:, :], in_=pj_in[oc]),
                        waits=[prev], inc=True, dma="pj%d" % (oc % 2))
            last = None
            for (t0, tw) in tiles_of(ownW):
                bi, ps, wtok = pool.acquire(1)[0]
                sl_x = slot % 4
                xt = p.emit("sync", lambda s, oc=oc, t0=t0, tw=tw, b=xts[sl_x]:
                            s.dma_start(out=b[:, 0:tw], in_=x1_in[oc][:, x1_own + t0: x1_own + t0 + tw]),
                            waits=[xt_tok[sl_x]], inc=True, dma="xt%d" % sl_x)
                for kc in range(NKC):
                    last = p.emit("tensor", lambda t, ps=ps, kc=kc, t0=t0, tw=tw, wb=wb:
                                  t.matmul(ps[:, 0:tw], wb[:, kc * 128:(kc + 1) * 128],
                                           xnb[:, kc * ownW + t0: kc * ownW + t0 + tw],
                                           start=(kc == 0), stop=False),
                                  waits=[wl, wtok] if kc == 0 else ())
                p.emit("tensor", lambda t, ps=ps, oc=oc, t0=t0, tw=tw:
                       t.matmul(ps[:, 0:tw], bb[:, oc * 128:(oc + 1) * 128],
                                hsb[:, t0:t0 + tw], start=False, stop=False),
                       waits=[h_done, tab_all])
                for kc in range(8):
                    last = p.emit("tensor", lambda t, ps=ps, kc=kc, t0=t0, tw=tw, pj=pj:
                                  t.matmul(ps[:, 0:tw], pj[:, kc * 128:(kc + 1) * 128],
                                           o2b[:, kc * ownW + t0: kc * ownW + t0 + tw],
                                           start=False, stop=(kc == 7)),
                                  waits=[pl] if kc == 0 else (), inc=(kc == 7))
                ev = p.emit("vector", lambda v, ps=ps, oc=oc, tw=tw, b=xts[sl_x], o=x1s[sl_x]:
                            v.scalar_tensor_tensor(o[:, 0:tw], ps[:, 0:tw], btot[:, oc:oc + 1],
                                                   b[:, 0:tw], op0=ALU.add, op1=ALU.add),
                            waits=[last, xt, x1_tok[sl_x], bt_tok], inc=True)
                pool.release(bi, ev)
                xt_tok[sl_x] = ev
                dt = p.emit("sync", lambda s, oc=oc, t0=t0, tw=tw, o=x1s[sl_x]:
                            s.dma_start(out=out_d[oc][:, t0:t0 + tw], in_=o[:, 0:tw]),
                            waits=[ev], inc=True, dma="oo%d" % sl_x)
                x1_tok[sl_x] = dt
                slot += 1
            mm_last[oc] = last
        if ctx is None:
            p.finalize(block, final_waits=[p.chan_total("oo%d" % i) for i in range(4)])
    return nc



def kernel_unfused(**inputs):
    g, cores = host_prep(inputs)
    ident_bf = np.eye(128, dtype=np.float32).astype(BF)
    identf = np.eye(128, dtype=np.float32)
    res = {c: {} for c in range(8)}

    def launch(builder, par, make_in):
        ids = [c for c in range(8) if c % 2 == par]
        nc = builder(par)
        in_maps = [make_in(c) for c in ids]
        r = run_bass_kernel_spmd(nc, in_maps, core_ids=ids)
        for i, c in enumerate(ids):
            res[c].update(r.results[i])

    for par in (0, 1):
        launch(build_s1, par, lambda c: dict(
            xfm_kv=cores[c]["xfm_kv"], wqkv_img=g["wqkv_img"], wte_img=g["wte_img"],
            cos=cores[c]["cos"], ssin=cores[c]["ssin"]))
    for par in (0, 1):
        launch(build_s2, par, lambda c: dict(
            q_out=res[c]["q_out"], k_out=res[c]["k_out"], v_out=res[c]["v_out"],
            gating=g["gating"], ident=ident_bf))
    for par in (0, 1):
        launch(build_s3, par, lambda c: dict(
            y_out=res[c]["y_out"], xn_own=res[c]["xn_own"], w_proj_img=g["w_proj_img"],
            qw_img=g["qw_img"], s_rep=g["s_rep"], z16=g["z16"], s16=g["s16"],
            gimg=g["gimg"], bias0=g["bias0"], xfm_q=cores[c]["xfm_q"]))
    for par in (0, 1):
        launch(build_s4a, par, lambda c: dict(
            x1=res[c]["x1"], conv_img=g["conv_img"], conv_b=g["conv_b"]))
    for par in (0, 1):
        launch(build_s4b, par, lambda c: dict(
            qkv1=res[c]["qkv1"], wq_img=g["wq_img"], wk_img=g["wk_img"], wv_img=g["wv_img"],
            wo_img=g["wo_img"], bq=g["bq"], bk=g["bk"], bv_col=g["bv_col"], bo=g["bo"],
            ident=ident_bf))
    for par in (0, 1):
        launch(build_s5, par, lambda c: dict(
            x1=res[c]["x1"], xn1_own=res[c]["xn1_own"], o2=res[c]["o2"],
            wbase_img=g["wbase_img"], projw_img=g["projw_img"], wr_img=g["wr_img"], br=g["br"],
            A_img=g["A_img"], B_sb=g["B_sb"], rep=g["rep"], bbase=g["bbase"], projb=g["projb"],
            identf=identf))

    out = np.empty((B_SEQ, T_SEQ, DM), np.float32)
    for c in range(8):
        P = PARAMS[c % 2]
        seq = c // 2
        o = np.asarray(res[c]["outp"], np.float32).reshape(DM, P["ownW"])
        out[seq, P["own0"]:P["own0"] + P["ownW"], :] = o.T
    return out



def _emit_ops(eng, ename, p):
    sems = p.sems
    for (fn, wl, tok) in p.ops[ename]:
        for (sk, v) in wl:
            eng.wait_ge(sems[sk], v)
        ins = fn(eng)
        if tok is not None:
            ins.then_inc(sems[tok[0]], 16 if tok[0].startswith("dma_") else 1)
    for k, v in p.counts.items():
        if v > 0:
            eng.wait_ge(sems[k], v)


def build_fused():
    nc = bass.Bass("TRN2", target_bir_lowering=False)
    d = {}

    def DI(name, shape, dt):
        d[name] = nc.dram_tensor(name, shape, dt, kind="ExternalInput").ap()

    def DN(name, shape, dt):
        d[name] = nc.dram_tensor(name, shape, dt).ap()

    def DO(name, shape, dt):
        d[name] = nc.dram_tensor(name, shape, dt, kind="ExternalOutput").ap()

    DI("par", [1, 1], I32)
    DI("xfm_kv", [DM, 2048], F32)
    DI("wqkv_img", [96, 128, DM], F32)
    DI("wte_img", [128, NKC * 10], F32)
    DI("cos", [128, 2048], F32)
    DI("ssin", [128, 2048], F32)
    DI("gating", [1, 1], F32)
    DI("ident", [128, 128], BF16)
    DI("identf", [128, 128], F32)
    DI("w_proj_img", [32, 128, DM], F32)
    DI("qw_img", [32, 128, DM], I32)
    DI("s_rep", [32, 128, DM], F32)
    DI("z16", [16, DM], F32)
    DI("s16", [16, DM], F32)
    DI("gimg", [128, NKC * 16], F32)
    DI("bias0", [128, 32], F32)
    DI("xfm_q", [DM, 1280], F32)
    DI("conv_img", [24, 128, 3 * DM], F32)
    DI("conv_b", [128, 24], F32)
    DI("wq_img", [8, 128, 1024], F32)
    DI("wk_img", [8, 128, 1024], F32)
    DI("wv_img", [8, 128, 1024], F32)
    DI("wo_img", [8, 128, 1024], F32)
    DI("bq", [128, 8], F32)
    DI("bk", [128, 8], F32)
    DI("bv_col", [128, 8], F32)
    DI("bo", [128, 8], F32)
    DI("wbase_img", [32, 128, DM], F32)
    DI("projw_img", [32, 128, 1024], F32)
    DI("wr_img", [128, NKC * 8], F32)
    DI("br", [8, 1], F32)
    DI("A_img", [128, DM], F32)
    DI("B_sb", [128, DM], F32)
    DI("rep", [8, 128], F32)
    DI("bbase", [128, 32], F32)
    DI("projb", [128, 32], F32)
    DN("q_out", [32, 128, 1280], BF16)
    DN("k_out", [32, 128, 2058], BF16)
    DN("v_out", [32, 128, 2058], BF16)
    DN("xn_own", [NKC, 128, 1280], BF16)
    DN("y_out", [32, 128, 1280], BF16)
    DN("x1", [NKC, 128, 1280], F32)
    DN("qkv1", [24, 128, 1152], BF16)
    DN("xn1_own", [NKC, 128, 1152], BF16)
    DN("o2", [8, 128, 1152], BF16)
    DO("outp", [NKC, 128, 1152], F32)

    from contextlib import ExitStack
    with ExitStack() as master:
        parb = master.enter_context(nc.sbuf_tensor("parb", [1, 1], I32))
        par_sem = master.enter_context(nc.semaphore("parsem"))
        pf32 = [master.enter_context(nc.psum_tensor(f"gf{i}", [128, 512], F32)) for i in range(6)]
        pbf = [master.enter_context(nc.psum_tensor(f"gb{i}", [128, 1024], BF16)) for i in range(2)]
        progs = {0: Prog(nc, tag="f"), 1: None}
        progs[1] = Prog(nc, tag="f", shared_sems=progs[0].sems)
        progs[0].make_sems()
        progs[1].make_sems()
        stages = [(build_s1, [1, 0]), (build_s2, [1, 0]), (build_s3, [1, 0]),
                  (build_s4a, [0, 1]), (build_s4b, [0, 1]), (build_s5, [0, 1])]
        for (fn, order) in stages:
            with ExitStack() as stage_stk:
                bufs = {}
                for par in order:
                    ctx = dict(nc=nc, d=d, bufs=bufs, stk=stage_stk, pf32=pf32, pbf=pbf, p=progs[par])
                    fn(par, ctx=ctx)
                    progs[par].barrier()
        block = master.enter_context(nc.Block())
        for ename in ENGINES:
            def mk(ename):
                def body(eng):
                    if ename == "sync":
                        eng.dma_start(out=parb[:, :], in_=d["par"]).then_inc(par_sem, 16)
                    eng.wait_ge(par_sem, 16)
                    with eng.register("preg_" + ename) as preg:
                        eng.reg_load(preg, parb[0:1, 0:1])
                        with eng.If_eq(preg, 0):
                            _emit_ops(eng, ename, progs[0])
                        with eng.Else():
                            _emit_ops(eng, ename, progs[1])
                return body
            getattr(block, ename)(mk(ename))
    return nc


def kernel(**inputs):
    g, cores = host_prep(inputs)
    ident_bf = np.ascontiguousarray(np.eye(128, dtype=np.float32).astype(BF))
    identf = np.eye(128, dtype=np.float32)
    nc = build_fused()
    in_maps = []
    for c in range(8):
        P = PARAMS[c % 2]
        dcr = cores[c]
        xkv = np.zeros((DM, 2048), np.float32)
        xkv[:, :P["kvW"]] = dcr["xfm_kv"]
        xq = np.zeros((DM, 1280), np.float32)
        xq[:, :P["qW"]] = dcr["xfm_q"]
        in_maps.append(dict(
            par=np.array([[c % 2]], np.int32), xfm_kv=xkv, xfm_q=xq,
            cos=g["cos_fm"], ssin=g["ssin_fm"],
            wqkv_img=g["wqkv_img"], wte_img=g["wte_img"], gating=g["gating"],
            ident=ident_bf, identf=identf,
            w_proj_img=g["w_proj_img"], qw_img=g["qw_img"], s_rep=g["s_rep"],
            z16=g["z16"], s16=g["s16"], gimg=g["gimg"], bias0=g["bias0"],
            conv_img=g["conv_img"], conv_b=g["conv_b"],
            wq_img=g["wq_img"], wk_img=g["wk_img"], wv_img=g["wv_img"], wo_img=g["wo_img"],
            bq=g["bq"], bk=g["bk"], bv_col=g["bv_col"], bo=g["bo"],
            wbase_img=g["wbase_img"], projw_img=g["projw_img"], wr_img=g["wr_img"], br=g["br"],
            A_img=g["A_img"], B_sb=g["B_sb"], rep=g["rep"], bbase=g["bbase"], projb=g["projb"]))
    res = run_bass_kernel_spmd(nc, in_maps, core_ids=list(range(8)))
    out = np.empty((B_SEQ, T_SEQ, DM), np.float32)
    for c in range(8):
        P = PARAMS[c % 2]
        seq = c // 2
        o = np.asarray(res.results[c]["outp"], np.float32)[:, :, :P["ownW"]].reshape(DM, P["ownW"]) if False else \
            np.asarray(res.results[c]["outp"], np.float32)[:, :, :P["ownW"]]
        o = o.reshape(NKC * 128, P["ownW"])
        out[seq, P["own0"]:P["own0"] + P["ownW"], :] = o.T
    return out


# version 06 (final)
# speedup vs baseline: 2.0886x; 1.0398x over previous
import numpy as np
import ml_dtypes
import concourse.bass as bass
import concourse.mybir as mybir
from concourse.bass_utils import run_bass_kernel_spmd

F32 = mybir.dt.float32
BF16 = mybir.dt.bfloat16
F16 = mybir.dt.float16
I32 = mybir.dt.int32
AF = mybir.ActivationFunctionType
ALU = mybir.AluOpType
AX = mybir.AxisListType
BF = ml_dtypes.bfloat16

B_SEQ, T_SEQ, DM = 4, 2048, 4096
H0, HD0 = 32, 128
NKC = DM // 128
P_SPLIT = 1152
EPS = 1e-5

PARAMS = {
    0: dict(kvW=1280, q0=0,    qW=1280, own0=0,      ownW=1152),
    1: dict(kvW=2048, q0=1024, qW=1024, own0=1152,   ownW=896),
}

def tiles_of(width, step=512):
    out = []
    c = 0
    while c < width:
        w = min(step, width - c)
        out.append((c, w))
        c += w
    return out


def pack_lhsT(wT):
    IN, OUT = wT.shape
    return np.ascontiguousarray(
        wT.reshape(IN // 128, 128, OUT // 128, 128).transpose(2, 1, 0, 3).reshape(OUT // 128, 128, IN))


def host_prep(inp):
    g = {}
    x = np.ascontiguousarray(np.asarray(inp["L0_attn_x"]))
    w_attn = np.asarray(inp["L0_attn_w_attn"])
    g["wqkv_img"] = pack_lhsT(np.ascontiguousarray(w_attn.T))
    wte = np.asarray(inp["L0_attn_adapter_wte"])
    g["wte_img"] = np.ascontiguousarray(wte.reshape(10, NKC, 128).transpose(2, 1, 0).reshape(128, NKC * 10))
    g["gating"] = np.asarray(inp["L0_attn_gating"]).reshape(1, 1)

    theta = 1.0 / 10000.0 ** (np.arange(0, HD0, 2) / HD0)
    idx = np.outer(np.arange(T_SEQ), theta)
    idx2 = np.concatenate([idx, idx], axis=-1)
    cos2 = np.cos(idx2).astype(np.float32)
    sin2 = np.sin(idx2).astype(np.float32)
    ss = sin2.copy()
    ss[:, :64] *= -1.0
    g["cos_fm"] = np.ascontiguousarray(cos2.T)
    ssT = ss.T
    g["ssin_fm"] = np.ascontiguousarray(np.concatenate([ssT[64:], ssT[:64]], axis=0))

    g["w_proj_img"] = pack_lhsT(np.ascontiguousarray(np.asarray(inp["L0_attn_w_proj"]).T))
    qw = np.asarray(inp["L0_ffn_qweight"])
    g["qw_img"] = pack_lhsT(np.ascontiguousarray(qw.T))
    scales = np.asarray(inp["L0_ffn_scales"]).reshape(DM, 16)
    zeros = np.asarray(inp["L0_ffn_zeros"]).reshape(DM, 16)
    oi = np.arange(DM)
    kc_i = oi // 128
    o_i = oi % 128
    s_img = np.empty((32, DM), np.float32)
    z_img = np.empty((32, DM), np.float32)
    for oc in range(32):
        s_img[oc] = scales[oc * 128 + o_i, kc_i // 2]
        z_img[oc] = zeros[oc * 128 + o_i, kc_i // 2]
    g["s_img"] = s_img
    g["z_img"] = z_img
    g["z16"] = np.ascontiguousarray(zeros.T)
    g["s16"] = np.ascontiguousarray(scales.T)
    gim = np.zeros((128, NKC * 16), np.float32)
    for kc in range(NKC):
        gim[:, kc * 16 + kc // 2] = 1.0
    g["gimg"] = gim
    g["s_rep"] = np.ascontiguousarray(np.broadcast_to(s_img[:, None, :], (32, 128, DM)))
    g["z_rep_row"] = z_img
    g["bias0"] = np.asarray(inp["L0_ffn_bias"]).reshape(32, 128).T.copy()

    cw = np.asarray(inp["L1_attn_conv_w"])
    g["conv_img"] = np.ascontiguousarray(
        cw.reshape(3, NKC, 128, 24, 128).transpose(3, 2, 0, 1, 4).reshape(24, 128, 3 * NKC * 128))
    g["conv_b"] = np.asarray(inp["L1_attn_conv_b"]).reshape(24, 128).T.copy()
    wq = np.asarray(inp["L1_attn_wq"]).reshape(1024, 1024)
    wk = np.asarray(inp["L1_attn_wk"]).reshape(1024, 1024)
    wv = np.asarray(inp["L1_attn_wv"]).reshape(1024, 1024)
    g["wq_img"] = pack_lhsT(wq)
    g["wk_img"] = pack_lhsT(wk)
    g["wv_img"] = pack_lhsT(wv)
    g["bq"] = np.asarray(inp["L1_attn_bq"]).reshape(1024).reshape(8, 128).T.copy()
    g["bk"] = np.asarray(inp["L1_attn_bk"]).reshape(1024).reshape(8, 128).T.copy()
    g["bv_col"] = np.asarray(inp["L1_attn_bv"]).reshape(1024).reshape(8, 128).T.copy()
    g["wo_img"] = pack_lhsT(np.asarray(inp["L1_attn_wo"]).reshape(1024, 1024))
    g["bo"] = np.asarray(inp["L1_attn_bo"]).reshape(8, 128).T.copy()
    g["projw_img"] = pack_lhsT(np.asarray(inp["L1_attn_proj_w"]))
    g["projb"] = np.asarray(inp["L1_attn_proj_b"]).reshape(32, 128).T.copy()

    g["wbase_img"] = pack_lhsT(np.ascontiguousarray(np.asarray(inp["L1_ffn_W_base"]).T))
    g["bbase"] = np.asarray(inp["L1_ffn_b_base"]).reshape(32, 128).T.copy()
    wr = np.asarray(inp["L1_ffn_W_router"])
    g["wr_img"] = np.ascontiguousarray(wr.reshape(8, NKC, 128).transpose(2, 1, 0).reshape(128, NKC * 8))
    g["br"] = np.asarray(inp["L1_ffn_b_router"]).reshape(8, 1)
    la = np.asarray(inp["L1_ffn_lora_A"])
    A_all = np.ascontiguousarray(la.transpose(1, 0, 2).reshape(DM, 128))
    g["A_img"] = pack_lhsT(A_all)[0]
    g["B_sb"] = np.ascontiguousarray(np.asarray(inp["L1_ffn_lora_B"]).reshape(128, DM))
    rep = np.zeros((8, 128), np.float32)
    for e in range(8):
        rep[e, e * 16:(e + 1) * 16] = 1.0
    g["rep"] = rep

    cores = []
    for c in range(8):
        par = c % 2
        P = PARAMS[par]
        seq = c // 2
        xs = np.ascontiguousarray(x[seq].T)
        d = dict(parity=par, seq=seq)
        d["xfm_kv"] = np.ascontiguousarray(xs[:, :P["kvW"]])
        d["xfm_q"] = np.ascontiguousarray(xs[:, P["q0"]:P["q0"] + P["qW"]])
        d["cos"] = np.ascontiguousarray(g["cos_fm"][:, :P["kvW"]])
        d["ssin"] = np.ascontiguousarray(g["ssin_fm"][:, :P["kvW"]])
        cores.append(d)
    return g, cores


ENGINES = ("tensor", "vector", "scalar", "gpsimd", "sync")


class Prog:

    def __init__(self, nc, tag="", shared_sems=None):
        self.nc = nc
        self.tag = tag
        self.ops = {e: [] for e in ENGINES}
        self.sems = shared_sems if shared_sems is not None else {}
        self.counts = {}
        self.hwm = {e: {} for e in ENGINES}
        self.pending = {e: [] for e in ENGINES}
        self.finals = []

    def make_sems(self):
        for e in ENGINES:
            self._mk(e)

    def _mk(self, key):
        if key not in self.sems:
            self.sems[key] = self.nc.semaphore(f"s{self.tag}_{key}").__enter__()
        if key not in self.counts:
            self.counts[key] = 0
        return key

    def barrier(self):
        snap = [(k, v) for (k, v) in self.counts.items() if v > 0]
        for e in ENGINES:
            self.pending[e].extend(snap)

    def emit(self, eng, fn, waits=(), inc=False, dma=None):
        key = self._mk("dma_" + dma) if dma else eng
        tok = None
        if inc:
            self.counts[key] += 16 if dma else 1
            tok = (key, self.counts[key])
        elif dma:
            raise AssertionError("every DMA needs inc=True (sync info required)")
        wl = []
        if self.pending[eng]:
            waits = list(self.pending[eng]) + list(waits)
            self.pending[eng] = []
        for t in waits:
            if t is None:
                continue
            (sk, v) = t
            if not sk.startswith("dma_") and sk == eng:
                continue
            h = self.hwm[eng].get(sk, 0)
            if v > h:
                wl.append((sk, v))
                self.hwm[eng][sk] = v
        self.ops[eng].append((fn, wl, tok))
        return tok

    def chan_total(self, ch):
        return ("dma_" + ch, self.counts.get("dma_" + ch, 0))

    def finalize(self, block, final_waits=()):
        for ename in ENGINES:
            oplist = self.ops[ename]
            sems = self.sems

            def mk(ename, oplist):
                def body(eng):
                    for (fn, wl, tok) in oplist:
                        for (sk, v) in wl:
                            eng.wait_ge(sems[sk], v)
                        ins = fn(eng)
                        if tok is not None:
                            ins.then_inc(sems[tok[0]], 16 if tok[0].startswith("dma_") else 1)
                    for t in final_waits:
                        if t is not None:
                            eng.wait_ge(sems[t[0]], t[1])
                return body
            getattr(block, ename)(mk(ename, oplist))


class PsumPool:

    def __init__(self, prog, banks):
        self.prog = prog
        self.banks = banks
        self.tokens = [None] * len(banks)
        self.ptr = 0

    def acquire(self, n):
        out = []
        for _ in range(n):
            i = self.ptr % len(self.banks)
            self.ptr += 1
            out.append((i, self.banks[i], self.tokens[i]))
        return out

    def release(self, i, token):
        self.tokens[i] = token



def emit_ln(p, pool, xq, xq_cols, kvW, a_m, b_m, sq_scr, rowbuf, ones_col, ones_row, load_tok, col0=0, ones_tok=None):
    nc = p.nc
    toks = []
    for (t0, tw) in tiles_of(kvW):
        bk = pool.acquire(4)
        (i_sx, ps_sx, w1), (i_sq, ps_sq, w2), (i_a, ps_a, w3), (i_b, ps_b, w4) = bk
        mm_sx = None
        mmsq_toks = {}
        for kc in range(NKC):
            xsl = xq[:, kc * xq_cols + col0 + t0: kc * xq_cols + col0 + t0 + tw]
            half = (kc % 2) * 512
            sqv = sq_scr[:, half:half + tw]
            sq_tok = p.emit("scalar", lambda s, sqv=sqv, xsl=xsl: s.activation(sqv, xsl, AF.Square),
                            waits=[load_tok, mmsq_toks.get(kc - 2)], inc=True)
            mm_sx = p.emit("tensor", lambda t, xsl=xsl, ps=ps_sx, tw=tw, kc=kc:
                           t.matmul(ps[0:1, 0:tw], ones_col[:, 0:1], xsl, start=(kc == 0), stop=(kc == NKC - 1)),
                           waits=[load_tok, ones_tok, w1 if kc == 0 else None], inc=(kc == NKC - 1))
            mmsq_toks[kc] = p.emit("tensor", lambda t, sqv=sqv, ps=ps_sq, tw=tw, kc=kc:
                           t.matmul(ps[0:1, 0:tw], ones_col[:, 0:1], sqv, start=(kc == 0), stop=(kc == NKC - 1)),
                           waits=[sq_tok, ones_tok, w2 if kc == 0 else None], inc=True)
        mm_sq = mmsq_toks[NKC - 1]
        Rs = [rowbuf[0:1, j * 512: j * 512 + tw] for j in range(6)]
        t_mean = p.emit("vector", lambda v, o=Rs[0], i=ps_sx[0:1, 0:tw]: v.tensor_scalar_mul(o, i, 1.0 / DM),
                        waits=[mm_sx], inc=True)
        t_var = p.emit("vector", lambda v, o=Rs[1], i=ps_sq[0:1, 0:tw]: v.tensor_scalar_mul(o, i, 1.0 / DM),
                       waits=[mm_sq], inc=True)
        p.emit("vector", lambda v, o=Rs[4], i=Rs[0]: v.tensor_mul(o, i, i), waits=[t_mean])
        p.emit("vector", lambda v, o=Rs[1], i=Rs[4]: v.tensor_sub(o, o, i), waits=[t_var])
        t_ve = p.emit("vector", lambda v, o=Rs[1]: v.tensor_scalar_add(o, o, EPS), inc=True)
        t_sd = p.emit("scalar", lambda s, o=Rs[5], i=Rs[1]: s.activation(o, i, AF.Sqrt), inc=True,
                      waits=[t_ve])
        t_a = p.emit("vector", lambda v, o=Rs[2], i=Rs[5]: v.reciprocal(o, i), waits=[t_sd], inc=True)
        t_b = p.emit("vector", lambda v, o=Rs[3], i=Rs[0], j=Rs[2]: v.tensor_mul(o, i, j), waits=[t_a], inc=True)
        bca = p.emit("tensor", lambda t, o=ps_a[:, 0:tw], r=Rs[2]: t.matmul(o, ones_row[0:1, 0:128], r,
                                                  start=True, stop=True), waits=[t_a, w3], inc=True)
        bcb = p.emit("tensor", lambda t, o=ps_b[:, 0:tw], r=Rs[3]: t.matmul(o, ones_row[0:1, 0:128], r,
                                                  start=True, stop=True), waits=[t_b, w4], inc=True)
        ea = p.emit("vector", lambda v, o=a_m[:, t0:t0 + tw], i=ps_a[:, 0:tw]: v.tensor_copy(o, i),
                    waits=[bca], inc=True)
        eb = p.emit("vector", lambda v, o=b_m[:, t0:t0 + tw], i=ps_b[:, 0:tw]: v.tensor_copy(o, i),
                    waits=[bcb], inc=True)
        pool.release(i_sx, t_mean)
        pool.release(i_sq, t_var)
        pool.release(i_a, ea)
        pool.release(i_b, eb)
    for kc in range(NKC):
        xsl = xq[:, kc * xq_cols + col0: kc * xq_cols + col0 + kvW]
        p.emit("vector", lambda v, xsl=xsl: v.tensor_mul(xsl, xsl, a_m[:, 0:kvW]), waits=[ea])
        tk = p.emit("vector", lambda v, xsl=xsl: v.tensor_sub(xsl, xsl, b_m[:, 0:kvW]), inc=True)
        toks.append(tk)
    return toks


def build_s1(parity, qkv_enable=True, ctx=None):
    P = PARAMS[parity]
    kvW, q0, qW = P["kvW"], P["q0"], P["qW"]
    xq_cols = kvW + 16
    if ctx is not None:
        nc = ctx["nc"]
        _D = lambda name, shape, dt, kind: ctx["d"][name]
    else:
        nc = bass.Bass("TRN2", target_bir_lowering=False)
        _D = lambda name, shape, dt, kind: nc.dram_tensor(name, shape, dt, kind=kind).ap()

    xfm = _D("xfm_kv", [DM, kvW], F32, "ExternalInput")
    wqkv = _D("wqkv_img", [96, 128, DM], F32, "ExternalInput")
    wte = _D("wte_img", [128, NKC * 10], F32, "ExternalInput")
    cosd = _D("cos", [128, kvW], F32, "ExternalInput")
    ssind = _D("ssin", [128, kvW], F32, "ExternalInput")
    q_out = _D("q_out", [32, 128, qW], BF16, "ExternalOutput")
    k_out = _D("k_out", [32, 128, kvW + 10], BF16, "ExternalOutput")
    v_out = _D("v_out", [32, 128, kvW + 10], BF16, "ExternalOutput")
    xn_out = _D("xn_own", [NKC, 128, qW], BF16, "ExternalOutput")

    ctxsfx = "_s1"
    from contextlib import ExitStack
    with ExitStack() as stk:
        if ctx is not None:
            def sb(name, shape, dt):
                if name not in ctx["bufs"]:
                    ctx["bufs"][name] = ctx["stk"].enter_context(nc.sbuf_tensor(name + ctxsfx, shape, dt))
                return ctx["bufs"][name]
        else:
            sb = lambda name, shape, dt: stk.enter_context(nc.sbuf_tensor(name, shape, dt))
        xq = sb("xq", [128, NKC * xq_cols], BF16)
        wb0 = sb("wb0", [128, DM], BF16)
        wb1 = sb("wb1", [128, DM], BF16)
        wteb = sb("wteb", [128, NKC * 10], BF16)
        cosb = sb("cosb", [128, kvW], BF16)
        ssinb = sb("ssinb", [128, kvW], BF16)
        am = sb("am", [128, kvW], BF16)
        bm = sb("bm", [128, kvW], BF16)
        sq_scr = sb("sq", [128, 1024], BF16)
        rowbuf = sb("rows", [1, 6 * 512], F32)
        ones_col = sb("onec", [128, 1], BF16)
        ones_row = sb("oner", [1, 128], F32)
        ropet = sb("ropet", [128, kvW], BF16)
        st0 = sb("st0", [128, kvW + 16], BF16)
        st1 = sb("st1", [128, kvW + 16], BF16)
        sr0 = sb("sr0", [128, kvW + 16], BF16)
        sr1 = sb("sr1", [128, kvW + 16], BF16)
        if ctx is not None:
            ppool = ctx["pf32"]
            p = ctx["p"]
        else:
            ppool = [stk.enter_context(nc.psum_tensor(f"pp{i}", [128, 512], F32)) for i in range(8)]
            block = stk.enter_context(nc.Block())
            p = Prog(nc)
            p.make_sems()
        pool = PsumPool(p, ppool)

        p.emit("vector", lambda v: v.memset(ones_col[:, :], 1.0))
        c_ones = p.emit("vector", lambda v: v.memset(ones_row[:, :], 1.0), inc=True)

        xtoks = []
        for kc in range(NKC):
            tk = p.emit("gpsimd", lambda g, kc=kc: g.dma_start(
                out=xq[:, kc * xq_cols: kc * xq_cols + kvW], in_=xfm[kc * 128:(kc + 1) * 128, 0:kvW]),
                inc=True, dma="xin")
            xtoks.append(tk)
        t_wte = p.emit("gpsimd", lambda g: g.dma_start(out=wteb[:, :], in_=wte), inc=True, dma="xin")
        t_cos = p.emit("gpsimd", lambda g: g.dma_start(out=cosb[:, 0:kvW], in_=cosd[:, 0:kvW]), inc=True, dma="xin")
        t_ssin = p.emit("gpsimd", lambda g: g.dma_start(out=ssinb[:, 0:kvW], in_=ssind[:, 0:kvW]), inc=True, dma="xin")
        xin_all = p.chan_total("xin")
        t_wte = t_cos = t_ssin = xin_all
        wte_toks = []
        for kc in range(NKC):
            tk = p.emit("vector", lambda v, kc=kc: v.tensor_copy(
                xq[:, kc * xq_cols + kvW: kc * xq_cols + kvW + 10],
                wteb[:, kc * 10:(kc + 1) * 10]), waits=[t_wte], inc=(kc == NKC - 1))
            wte_toks.append(tk)

        ln_toks = emit_ln(p, pool, xq, xq_cols, kvW, am, bm, sq_scr, rowbuf,
                          ones_col, ones_row, xin_all, ones_tok=c_ones)
        ln_done = ln_toks[-1]
        wte_done = wte_toks[-1]

        out_toks = []
        for kc in range(NKC):
            tk = p.emit("sync", lambda s, kc=kc: s.dma_start(
                out=xn_out[kc][:, 0:qW], in_=xq[:, kc * xq_cols + q0: kc * xq_cols + q0 + qW]),
                waits=[ln_toks[kc]], inc=True, dma="xout")
            out_toks.append(tk)

        wbufs = [wb0, wb1]
        stages = [(st0, sr0), (st1, sr1)]
        wload_tok = {}
        mm_last = {}
        raw_free = [None, None]
        rope_free = [None, None]
        for oc in (range(96) if qkv_enable else []):
            kind = "q" if oc < 32 else ("k" if oc < 64 else "v")
            head = oc % 32
            wbuf = wbufs[oc % 2]
            wload_tok[oc] = p.emit("gpsimd", lambda g, oc=oc, wbuf=wbuf: g.dma_start(
                out=wbuf[:, :], in_=wqkv[oc]), waits=[mm_last.get(oc - 2)], inc=True, dma="w%d" % (oc % 2))

            if kind == "q":
                col0, width = q0, qW
            else:
                col0, width = 0, kvW + 10
            tl = tiles_of(width)
            if len(tl) >= 2 and tl[-1][1] + tl[-2][1] <= 512:
                t0a, w0a = tl[-2]
                tl = tl[:-2] + [(t0a, w0a + tl[-1][1])]
            banks = pool.acquire(len(tl))
            stg_raw, stg_rope = stages[oc % 2]
            sfree = raw_free[oc % 2]
            last = None
            evac_done = None
            for ti, (t0, tw) in enumerate(tl):
                bi, ps, wtok = banks[ti]
                for kc in range(NKC):
                    rhs = xq[:, kc * xq_cols + col0 + t0: kc * xq_cols + col0 + t0 + tw]
                    w_first = [wload_tok[oc], ln_done, wte_done, wtok] if kc == 0 else ()
                    last = p.emit("tensor", lambda t, ps=ps, wbuf=wbuf, kc=kc, rhs=rhs, tw=tw:
                                  t.matmul(ps[:, 0:tw], wbuf[:, kc * 128:(kc + 1) * 128], rhs,
                                           start=(kc == 0), stop=(kc == NKC - 1)),
                                  waits=w_first, inc=(kc == NKC - 1))
                etok = p.emit("scalar", lambda s, ps=ps, t0=t0, tw=tw, stg_raw=stg_raw:
                              s.copy(stg_raw[:, t0:t0 + tw], ps[:, 0:tw]),
                              waits=[last, sfree], inc=True)
                pool.release(bi, etok)
                evac_done = etok
            mm_last[oc] = last

            if kind == "v":
                dtok = p.emit("sync", lambda s, head=head, stg_raw=stg_raw, width=width: s.dma_start(
                    out=v_out[head][:, 0:width], in_=stg_raw[:, 0:width]), waits=[evac_done], inc=True, dma="st%d" % (oc % 2))
                raw_free[oc % 2] = dtok
                out_toks.append(dtok)
            else:
                wW = qW if kind == "q" else kvW
                cs0 = q0 if kind == "q" else 0
                coss = cosb[:, cs0:cs0 + wW]
                ssins = ssinb[:, cs0:cs0 + wW]
                rfree = rope_free[oc % 2]
                p.emit("vector", lambda v, sr=stg_rope, st=stg_raw, coss=coss, wW=wW:
                       v.tensor_mul(sr[:, 0:wW], st[:, 0:wW], coss),
                       waits=[evac_done, t_cos, t_ssin, rfree])
                p.emit("vector", lambda v, st=stg_raw, ssins=ssins, wW=wW:
                       v.tensor_mul(ropet[0:64, 0:wW], st[64:128, 0:wW], ssins[64:128, 0:wW]))
                p.emit("vector", lambda v, st=stg_raw, ssins=ssins, wW=wW:
                       v.tensor_mul(ropet[64:128, 0:wW], st[0:64, 0:wW], ssins[0:64, 0:wW]))
                if kind == "k":
                    p.emit("vector", lambda v, sr=stg_rope, st=stg_raw:
                           v.tensor_copy(sr[:, kvW:kvW + 10], st[:, kvW:kvW + 10]))
                rtok = p.emit("vector", lambda v, sr=stg_rope, wW=wW:
                              v.tensor_add(sr[:, 0:wW], sr[:, 0:wW], ropet[:, 0:wW]), inc=True)
                raw_free[oc % 2] = rtok
                dwidth = qW if kind == "q" else kvW + 10
                dst = q_out[head] if kind == "q" else k_out[head]
                dtok = p.emit("sync", lambda s, dst=dst, sr=stg_rope, dwidth=dwidth: s.dma_start(
                    out=dst[:, 0:dwidth], in_=sr[:, 0:dwidth]), waits=[rtok], inc=True, dma="st%d" % (oc % 2))
                rope_free[oc % 2] = dtok
                out_toks.append(dtok)

        if ctx is None:
            p.finalize(block, final_waits=[p.chan_total("xout"), p.chan_total("st0"), p.chan_total("st1")])
    return nc


def build_s2(parity, ctx=None):
    P = PARAMS[parity]
    kvW, q0, qW = P["kvW"], P["q0"], P["qW"]
    KT = kvW // 128
    if ctx is not None:
        nc = ctx["nc"]
        _D = lambda name, shape, dt, kind: ctx["d"][name]
    else:
        nc = bass.Bass("TRN2", target_bir_lowering=False)
        _D = lambda name, shape, dt, kind: nc.dram_tensor(name, shape, dt, kind=kind).ap()

    q_in = _D("q_out", [32, 128, qW], BF16, "ExternalInput")
    k_in = _D("k_out", [32, 128, kvW + 10], BF16, "ExternalInput")
    v_in = _D("v_out", [32, 128, kvW + 10], BF16, "ExternalInput")
    gat = _D("gating", [1, 1], F32, "ExternalInput")
    ident = _D("ident", [128, 128], BF16, "ExternalInput")
    y_out = _D("y_out", [32, 128, qW], BF16, "ExternalOutput")

    ctxsfx = "_s2"
    from contextlib import ExitStack
    with ExitStack() as stk:
        if ctx is not None:
            def sb(name, shape, dt):
                if name not in ctx["bufs"]:
                    ctx["bufs"][name] = ctx["stk"].enter_context(nc.sbuf_tensor(name + ctxsfx, shape, dt))
                return ctx["bufs"][name]
        else:
            sb = lambda name, shape, dt: stk.enter_context(nc.sbuf_tensor(name, shape, dt))
        kb = [sb("kb0", [128, kvW + 10], BF16), sb("kb1", [128, kvW + 10], BF16)]
        vb = [sb("vb0", [128, kvW + 10], BF16), sb("vb1", [128, kvW + 10], BF16)]
        qb = [sb("qb0", [128, 1280], BF16), sb("qb1", [128, 1280], BF16)]
        vtm = [sb("vtm0", [128, kvW], BF16), sb("vtm1", [128, kvW], BF16)]
        avtm = [sb("avtm0", [10, 128], BF16), sb("avtm1", [10, 128], BF16)]
        expt = sb("expt", [128, 16 * 512 * 2], BF16)
        expa = sb("expa", [10, 1024], BF16)
        rowbuf = sb("rows", [1, 3072], F32)
        smv = sb("smv", [10, 1024], F32)
        rmv = sb("rmv", [128, 1024], BF16)
        ystage = [sb("yst0", [128, 1280], BF16), sb("yst1", [128, 1280], BF16)]
        onec = sb("onec", [128, 1], BF16)
        oner = sb("oner", [1, 128], F32)
        gatb = sb("gatb", [1, 1], F32)
        idb = sb("idb", [128, 128], BF16)
        if ctx is not None:
            pool_rot = ctx["pf32"][0:2]
            pool_y = ctx["pf32"][2:4]
            pool_sm = ctx["pf32"][4:6]
            ptrb = ctx["pbf"]
            p = ctx["p"]
        else:
            pool_rot = [stk.enter_context(nc.psum_tensor(f"pr{i}", [128, 512], F32)) for i in range(2)]
            pool_y = [stk.enter_context(nc.psum_tensor(f"py{i}", [128, 512], F32)) for i in range(2)]
            pool_sm = [stk.enter_context(nc.psum_tensor(f"pm{i}", [128, 512], F32)) for i in range(2)]
            ptrb = [stk.enter_context(nc.psum_tensor(f"pt{i}", [128, 1024], BF16)) for i in range(2)]
            block = stk.enter_context(nc.Block())
            p = Prog(nc)
            p.make_sems()
        rot = PsumPool(p, pool_rot)

        p.emit("vector", lambda v: v.memset(onec[:, :], 1.0))
        c_ones = p.emit("vector", lambda v: v.memset(oner[:, :], 1.0), inc=True)
        p.emit("gpsimd", lambda g: g.dma_start(out=gatb[:, :], in_=gat), inc=True, dma="xin")
        p.emit("gpsimd", lambda g: g.dma_start(out=idb[:, :], in_=ident), inc=True, dma="xin")
        xin_all = p.chan_total("xin")

        SCALE_M = 1.0 / float(np.sqrt(128.0))
        SCALE_A = 1.0 / 64.0
        head_done = {}
        tr_tok = [None, None]
        avp_tok = [None, None]
        ybank_tok = [None, None]
        sm_tok = [None, None]
        ystage_free = [None, None]
        pv_tok = [None, None]
        g_tok = [None, None]
        rb_tok = [None, None]
        for h in range(32):
            par = h % 2
            hd2 = head_done.get(h - 2)
            ld_k = p.emit("sync", lambda s, h=h, b=kb[par]: s.dma_start(out=b[:, 0:kvW + 10], in_=k_in[h][:, 0:kvW + 10]),
                          waits=[hd2], inc=True, dma="kld%d" % par)
            ld_v = p.emit("sync", lambda s, h=h, b=vb[par]: s.dma_start(out=b[:, 0:kvW + 10], in_=v_in[h][:, 0:kvW + 10]),
                          waits=[hd2], inc=True, dma="vld%d" % par)
            ld_q = p.emit("sync", lambda s, h=h, b=qb[par]: s.dma_start(out=b[:, 0:qW], in_=q_in[h][:, 0:qW]),
                          waits=[hd2], inc=True, dma="qld%d" % par)
            for kt in range(KT):
                bk = kt % 2
                ttok = p.emit("tensor", lambda t, b=vb[par], kt=kt, o=ptrb[bk][:, 0:128]:
                              t.transpose(o, b[:, kt * 128:(kt + 1) * 128], idb[:, :]),
                              waits=[ld_v, xin_all, tr_tok[bk], hd2], inc=True)
                tr_tok[bk] = p.emit("vector", lambda v, o=vtm[par], kt=kt, i=ptrb[bk][:, 0:128]:
                                    v.tensor_copy(o[:, kt * 128:(kt + 1) * 128], i), waits=[ttok], inc=True)
            bk = KT % 2
            ttok = p.emit("tensor", lambda t, b=vb[par], o=ptrb[bk][0:10, 0:128]:
                          t.transpose(o, b[:, kvW:kvW + 10], idb[:, :]),
                          waits=[ld_v, tr_tok[bk], avp_tok[bk]], inc=True)
            av_ev = p.emit("vector", lambda v, o=avtm[par], i=ptrb[bk][0:10, 0:128]:
                           v.tensor_copy(o[:, :], i), waits=[ttok], inc=True)
            avp_tok[bk] = av_ev
            vt_done = av_ev

            last_pva = None
            yev = None
            for (qc0, qcW) in tiles_of(qW):
                qa0 = q0 + qc0
                qcp = (qc0 // 512) % 2
                kt_hi = (qa0 + qcW - 1) // 128 + 1
                smb = pool_sm[qcp]
                ybank = pool_y[qcp]
                EB = qcp * 16 * 512
                AB = qcp * 512
                RB = qcp * 1536
                sum_last = None
                for kt in range(kt_hi):
                    bi, ps, wtok = rot.acquire(1)[0]
                    mm = p.emit("tensor", lambda t, ps=ps, b=kb[par], kt=kt, q=qb[par], qc0=qc0, qcW=qcW:
                                t.matmul(ps[:, 0:qcW], b[:, kt * 128:(kt + 1) * 128], q[:, qc0:qc0 + qcW],
                                         start=True, stop=True),
                                waits=[ld_k, ld_q, wtok], inc=True)
                    etok = p.emit("scalar", lambda s, ps=ps, kt=kt, qcW=qcW:
                                  s.activation(expt[:, EB + kt * 512: EB + kt * 512 + qcW], ps[:, 0:qcW], AF.Exp,
                                               scale=SCALE_M),
                                  waits=[mm, pv_tok[qcp], g_tok[qcp]], inc=True)
                    rot.release(bi, etok)
                    if kt * 128 + 127 >= qa0:
                        etok = p.emit("gpsimd", lambda g, kt=kt, qcW=qcW, qa0=qa0:
                                      g.affine_select(expt[:, EB + kt * 512: EB + kt * 512 + qcW],
                                                      expt[:, EB + kt * 512: EB + kt * 512 + qcW],
                                                      [[1, qcW]], ALU.is_ge, 0.0,
                                                      base=qa0 - kt * 128, channel_multiplier=-1),
                                      waits=[etok], inc=True)
                        g_tok[qcp] = etok
                    sum_last = p.emit("tensor", lambda t, ps=smb, kt=kt, qcW=qcW, first=(kt == 0), last=(kt == kt_hi - 1):
                                      t.matmul(ps[0:1, 0:qcW], onec[:, 0:1], expt[:, EB + kt * 512: EB + kt * 512 + qcW],
                                               start=first, stop=last),
                                      waits=[etok, c_ones, sm_tok[qcp] if kt == 0 else None], inc=(kt == kt_hi - 1))
                    p.emit("tensor", lambda t, ps=ybank, kt=kt, qcW=qcW, par=par, first=(kt == 0):
                           t.matmul(ps[:, 0:qcW], vtm[par][:, kt * 128:(kt + 1) * 128],
                                    expt[:, EB + kt * 512: EB + kt * 512 + qcW], start=first, stop=False),
                           waits=[vt_done, etok, ybank_tok[qcp] if kt == 0 else None])
                bi, psa, wtok = rot.acquire(1)[0]
                mma = p.emit("tensor", lambda t, ps=psa, b=kb[par], q=qb[par], qc0=qc0, qcW=qcW:
                             t.matmul(ps[0:10, 0:qcW], b[:, kvW:kvW + 10], q[:, qc0:qc0 + qcW],
                                      start=True, stop=True), waits=[wtok], inc=True)
                eta = p.emit("scalar", lambda s, ps=psa, qcW=qcW:
                             s.activation(expa[0:10, AB:AB + qcW], ps[0:10, 0:qcW], AF.Exp, scale=SCALE_A),
                             waits=[mma, pv_tok[qcp]], inc=True)
                rot.release(bi, eta)
                bi2, psas, wtok2 = rot.acquire(1)[0]
                asum = p.emit("tensor", lambda t, ps=psas, qcW=qcW:
                              t.matmul(ps[0:1, 0:qcW], onec[0:10, 0:1], expa[0:10, AB:AB + qcW],
                                       start=True, stop=True), waits=[eta, wtok2, c_ones], inc=True)
                r_as = p.emit("vector", lambda v, i=psas, qcW=qcW: v.reciprocal(rowbuf[0:1, RB:RB + qcW], i[0:1, 0:qcW]),
                              waits=[asum, rb_tok[qcp]], inc=True)
                rot.release(bi2, r_as)
                p.emit("vector", lambda v, i=smb, qcW=qcW:
                       v.tensor_mul(rowbuf[0:1, RB + 512:RB + 512 + qcW], i[0:1, 0:qcW], rowbuf[0:1, RB:RB + qcW]),
                       waits=[sum_last, xin_all])
                rS = p.emit("vector", lambda v, qcW=qcW:
                            v.tensor_scalar_mul(rowbuf[0:1, RB + 512:RB + 512 + qcW], rowbuf[0:1, RB + 512:RB + 512 + qcW], gatb[0:1, 0:1]),
                            inc=True)
                rR = p.emit("vector", lambda v, i=smb, qcW=qcW:
                            v.reciprocal(rowbuf[0:1, RB + 1024:RB + 1024 + qcW], i[0:1, 0:qcW]),
                            waits=[sum_last], inc=True)
                sm_tok[qcp] = rR
                bi3, psm, wtok3 = rot.acquire(1)[0]
                mS = p.emit("tensor", lambda t, ps=psm, qcW=qcW:
                            t.matmul(ps[0:10, 0:qcW], oner[0:1, 0:10], rowbuf[0:1, RB + 512:RB + 512 + qcW],
                                     start=True, stop=True), waits=[rS, wtok3], inc=True)
                eS = p.emit("vector", lambda v, i=psm, qcW=qcW: v.tensor_copy(smv[0:10, AB:AB + qcW], i[0:10, 0:qcW]),
                            waits=[mS], inc=True)
                rot.release(bi3, eS)
                sc = p.emit("vector", lambda v, qcW=qcW:
                            v.tensor_mul(expa[0:10, AB:AB + qcW], expa[0:10, AB:AB + qcW], smv[0:10, AB:AB + qcW]), inc=True)
                last_pva = p.emit("tensor", lambda t, ps=ybank, par=par, qcW=qcW:
                                  t.matmul(ps[:, 0:qcW], avtm[par][:, :], expa[0:10, AB:AB + qcW],
                                           start=False, stop=True), waits=[av_ev, sc], inc=True)
                bi4, psr, wtok4 = rot.acquire(1)[0]
                mR = p.emit("tensor", lambda t, ps=psr, qcW=qcW:
                            t.matmul(ps[:, 0:qcW], oner[0:1, 0:128], rowbuf[0:1, RB + 1024:RB + 1024 + qcW],
                                     start=True, stop=True), waits=[rR, wtok4], inc=True)
                eR = p.emit("vector", lambda v, i=psr, qcW=qcW: v.tensor_copy(rmv[:, AB:AB + qcW], i[:, 0:qcW]),
                            waits=[mR], inc=True)
                rot.release(bi4, eR)
                rb_tok[qcp] = mR
                yev = p.emit("vector", lambda v, ps=ybank, par=par, qc0=qc0, qcW=qcW:
                             v.tensor_mul(ystage[par][:, qc0:qc0 + qcW], ps[:, 0:qcW], rmv[:, AB:AB + qcW]),
                             waits=[last_pva, eR, ystage_free[par]], inc=True)
                ybank_tok[qcp] = yev
                pv_tok[qcp] = last_pva
            dtok = p.emit("sync", lambda s, h=h, par=par: s.dma_start(out=y_out[h][:, 0:qW], in_=ystage[par][:, 0:qW]),
                          waits=[yev], inc=True, dma="yo%d" % par)
            ystage_free[par] = dtok
            head_done[h] = last_pva
        if ctx is None:
            p.finalize(block, final_waits=[p.chan_total("yo0"), p.chan_total("yo1")])
    return nc


def build_s3(parity, ctx=None):
    P = PARAMS[parity]
    q0, qW = P["q0"], P["qW"]
    if ctx is not None:
        nc = ctx["nc"]
        _D = lambda name, shape, dt, kind: ctx["d"][name]
    else:
        nc = bass.Bass("TRN2", target_bir_lowering=False)
        _D = lambda name, shape, dt, kind: nc.dram_tensor(name, shape, dt, kind=kind).ap()

    y_in = _D("y_out", [32, 128, qW], BF16, "ExternalInput")
    xn_in = _D("xn_own", [NKC, 128, qW], BF16, "ExternalInput")
    wp_in = _D("w_proj_img", [32, 128, DM], F32, "ExternalInput")
    qw_in = _D("qw_img", [32, 128, DM], I32, "ExternalInput")
    sr_in = _D("s_rep", [32, 128, DM], F32, "ExternalInput")
    z16_in = _D("z16", [16, DM], F32, "ExternalInput")
    s16_in = _D("s16", [16, DM], F32, "ExternalInput")
    gim_in = _D("gimg", [128, NKC * 16], F32, "ExternalInput")
    b0_in = _D("bias0", [128, 32], F32, "ExternalInput")
    xq_in = _D("xfm_q", [DM, qW], F32, "ExternalInput")
    x1_out = _D("x1", [NKC, 128, qW], F32, "ExternalOutput")

    passes = [(0, 640), (640, qW - 640)]

    ctxsfx = "_s3"
    from contextlib import ExitStack
    with ExitStack() as stk:
        if ctx is not None:
            def sb(name, shape, dt):
                if name not in ctx["bufs"]:
                    ctx["bufs"][name] = ctx["stk"].enter_context(nc.sbuf_tensor(name + ctxsfx, shape, dt))
                return ctx["bufs"][name]
        else:
            sb = lambda name, shape, dt: stk.enter_context(nc.sbuf_tensor(name, shape, dt))
        pW = max(w for (_, w) in passes)
        ysb = sb("ysb", [128, NKC * pW], BF16)
        xnsb = sb("xnsb", [128, NKC * pW], BF16)
        wpb = [sb("wpb0", [128, DM], BF16), sb("wpb1", [128, DM], BF16)]
        qwb = [sb("qwb0", [128, DM], F16), sb("qwb1", [128, DM], F16)]
        srb = [sb("srb0", [128, DM], F16), sb("srb1", [128, DM], F16)]
        wdq = [sb("wdq0", [128, DM], BF16), sb("wdq1", [128, DM], BF16)]
        z16b = sb("z16b", [16, DM], BF16)
        s16b = sb("s16b", [16, DM], BF16)
        zsneg = sb("zsneg", [16, DM], BF16)
        gimb = sb("gimb", [128, NKC * 16], BF16)
        gsum = sb("gsum", [16, 1280], BF16)
        b0b = sb("b0b", [128, 32], F32)
        xts = [sb(f"xt{i}", [128, 512], F32) for i in range(4)]
        x1s = [sb(f"x1s{i}", [128, 512], F32) for i in range(4)]
        if ctx is not None:
            ppool = ctx["pf32"]
            p = ctx["p"]
        else:
            ppool = [stk.enter_context(nc.psum_tensor(f"pp{i}", [128, 512], F32)) for i in range(8)]
            block = stk.enter_context(nc.Block())
            p = Prog(nc)
            p.make_sems()
        pool = PsumPool(p, ppool)

        p.emit("gpsimd", lambda g: g.dma_start(out=z16b[:, :], in_=z16_in), inc=True, dma="tab")
        p.emit("gpsimd", lambda g: g.dma_start(out=s16b[:, :], in_=s16_in), inc=True, dma="tab")
        p.emit("gpsimd", lambda g: g.dma_start(out=gimb[:, :], in_=gim_in), inc=True, dma="tab")
        p.emit("gpsimd", lambda g: g.dma_start(out=b0b[:, :], in_=b0_in), inc=True, dma="tab")
        tab_all = p.chan_total("tab")
        p.emit("vector", lambda v: v.tensor_mul(zsneg[:, :], z16b[:, :], s16b[:, :]), waits=[tab_all])
        zs_tok = p.emit("vector", lambda v: v.tensor_scalar_mul(zsneg[:, :], zsneg[:, :], -1.0), inc=True)

        wl_tok = {}
        mm_last = {}
        dq_tok = {}
        xt_tok = [None] * 4
        x1_tok = [None] * 4
        slot = 0
        for pi, (p0, pw) in enumerate(passes):
            ych = "yin%d" % pi
            for kc in range(NKC):
                p.emit("sync", lambda s, kc=kc, p0=p0, pw=pw: s.dma_start(
                    out=ysb[:, kc * pW: kc * pW + pw], in_=y_in[kc][:, p0:p0 + pw]),
                    waits=[mm_last.get(("pass_end", pi - 1))], inc=True, dma=ych)
                p.emit("sync", lambda s, kc=kc, p0=p0, pw=pw: s.dma_start(
                    out=xnsb[:, kc * pW: kc * pW + pw], in_=xn_in[kc][:, p0:p0 + pw]),
                    inc=True, dma=ych)
            act_all = p.chan_total(ych)

            gs_toks = []
            for (t0, tw) in tiles_of(pw):
                bi, ps, wtok = pool.acquire(1)[0]
                last = None
                for kc in range(NKC):
                    last = p.emit("tensor", lambda t, ps=ps, kc=kc, t0=t0, tw=tw:
                                  t.matmul(ps[0:16, 0:tw], gimb[:, kc * 16:(kc + 1) * 16],
                                           xnsb[:, kc * pW + t0: kc * pW + t0 + tw],
                                           start=(kc == 0), stop=(kc == NKC - 1)),
                                  waits=[act_all, tab_all, wtok] if kc == 0 else (), inc=(kc == NKC - 1))
                ev = p.emit("vector", lambda v, ps=ps, t0=t0, tw=tw, p0=p0:
                            v.tensor_copy(gsum[0:16, p0 + t0: p0 + t0 + tw], ps[0:16, 0:tw]),
                            waits=[last], inc=True)
                pool.release(bi, ev)
                gs_toks.append(ev)
            gs_done = gs_toks[-1]

            for oci in range(32):
                oc = oci
                gi = pi * 32 + oci
                wb = wpb[gi % 2]
                qb_ = qwb[gi % 2]
                sr_ = srb[gi % 2]
                dq_ = wdq[gi % 2]
                prev = mm_last.get(gi - 2)
                wl = p.emit("gpsimd", lambda g, oc=oc, wb=wb: g.dma_start(out=wb[:, :], in_=wp_in[oc]),
                            waits=[prev], inc=True, dma="wp%d" % (gi % 2))
                ql = p.emit("gpsimd", lambda g, oc=oc, qb_=qb_: g.dma_start(out=qb_[:, :], in_=qw_in[oc]),
                            waits=[prev], inc=True, dma="qw%d" % (gi % 2))
                sl = p.emit("gpsimd", lambda g, oc=oc, sr_=sr_: g.dma_start(out=sr_[:, :], in_=sr_in[oc]),
                            waits=[prev], inc=True, dma="sr%d" % (gi % 2))
                deq_eng = "vector" if oci % 2 == 0 else "gpsimd"
                dq_tok[gi] = p.emit(deq_eng, lambda e, qb_=qb_, sr_=sr_, dq_=dq_:
                                    e.tensor_mul(dq_[:, :], qb_[:, :], sr_[:, :]),
                                    waits=[ql, sl, mm_last.get(gi - 2)], inc=True)
                last = None
                for (t0, tw) in tiles_of(pw):
                    bi, ps, wtok = pool.acquire(1)[0]
                    sl_x = slot % 4
                    xt = p.emit("sync", lambda s, oc=oc, t0=t0, tw=tw, b=xts[sl_x], p0=p0:
                                s.dma_start(out=b[:, 0:tw], in_=xq_in[oc * 128:(oc + 1) * 128, p0 + t0:p0 + t0 + tw]),
                                waits=[xt_tok[sl_x]], inc=True, dma="xt%d" % sl_x)
                    last = p.emit("tensor", lambda t, ps=ps, oc=oc, t0=t0, tw=tw, p0=p0:
                                  t.matmul(ps[:, 0:tw], zsneg[:, oc * 128:(oc + 1) * 128],
                                           gsum[0:16, p0 + t0: p0 + t0 + tw], start=True, stop=False),
                                  waits=[zs_tok, gs_done, wtok], inc=False)
                    for kc in range(NKC):
                        p.emit("tensor", lambda t, ps=ps, kc=kc, t0=t0, tw=tw, dq_=dq_:
                               t.matmul(ps[:, 0:tw], dq_[:, kc * 128:(kc + 1) * 128],
                                        xnsb[:, kc * pW + t0: kc * pW + t0 + tw], start=False, stop=False),
                               waits=[dq_tok[gi]] if kc == 0 else ())
                    for kc in range(NKC):
                        last = p.emit("tensor", lambda t, ps=ps, kc=kc, t0=t0, tw=tw, wb=wb:
                                      t.matmul(ps[:, 0:tw], wb[:, kc * 128:(kc + 1) * 128],
                                               ysb[:, kc * pW + t0: kc * pW + t0 + tw],
                                               start=False, stop=(kc == NKC - 1)),
                                      waits=[wl] if kc == 0 else (), inc=(kc == NKC - 1))
                    ev = p.emit("vector", lambda v, ps=ps, oc=oc, tw=tw, b=xts[sl_x], o=x1s[sl_x]:
                                v.scalar_tensor_tensor(o[:, 0:tw], ps[:, 0:tw], b0b[:, oc:oc + 1],
                                                       b[:, 0:tw], op0=ALU.add, op1=ALU.add),
                                waits=[last, xt, x1_tok[sl_x]], inc=True)
                    pool.release(bi, ev)
                    xt_tok[sl_x] = ev
                    dt = p.emit("sync", lambda s, oc=oc, t0=t0, tw=tw, o=x1s[sl_x], p0=p0:
                                s.dma_start(out=x1_out[oc][:, p0 + t0: p0 + t0 + tw], in_=o[:, 0:tw]),
                                waits=[ev], inc=True, dma="x1o%d" % sl_x)
                    x1_tok[sl_x] = dt
                    slot += 1
                mm_last[gi] = last
            mm_last[("pass_end", pi)] = mm_last[pi * 32 + 31]
        if ctx is None:
            p.finalize(block, final_waits=[p.chan_total("x1o%d" % i) for i in range(4)])
    return nc


def build_s4a(parity, ctx=None):
    P = PARAMS[parity]
    q0, qW, own0, ownW = P["q0"], P["qW"], P["own0"], P["ownW"]
    stride = ownW + 2
    lo = 1 if own0 == 0 else 0
    ln_cnt = ownW + 1
    x1_lo = own0 - q0 - 1 + lo
    if ctx is not None:
        nc = ctx["nc"]
        _D = lambda name, shape, dt, kind: ctx["d"][name]
    else:
        nc = bass.Bass("TRN2", target_bir_lowering=False)
        _D = lambda name, shape, dt, kind: nc.dram_tensor(name, shape, dt, kind=kind).ap()

    x1_in = _D("x1", [NKC, 128, qW], F32, "ExternalInput")
    cw_in = _D("conv_img", [24, 128, 3 * DM], F32, "ExternalInput")
    cb_in = _D("conv_b", [128, 24], F32, "ExternalInput")
    qkv1_out = _D("qkv1", [24, 128, ownW], BF16, "ExternalOutput")
    xn1_out = _D("xn1_own", [NKC, 128, ownW], BF16, "ExternalOutput")

    ctxsfx = "_s4a"
    from contextlib import ExitStack
    with ExitStack() as stk:
        if ctx is not None:
            def sb(name, shape, dt):
                if name not in ctx["bufs"]:
                    ctx["bufs"][name] = ctx["stk"].enter_context(nc.sbuf_tensor(name + ctxsfx, shape, dt))
                return ctx["bufs"][name]
        else:
            sb = lambda name, shape, dt: stk.enter_context(nc.sbuf_tensor(name, shape, dt))
        xn1 = sb("xn1", [128, NKC * 1154], BF16)
        cwb = [sb("cwb0", [128, 3 * DM], BF16), sb("cwb1", [128, 3 * DM], BF16)]
        cbb = sb("cbb", [128, 24], F32)
        am = sb("am", [128, 1153], BF16)
        bm = sb("bm", [128, 1153], BF16)
        sq_scr = sb("sq", [128, 1024], BF16)
        rowbuf = sb("rows", [1, 6 * 512], F32)
        onec = sb("onec", [128, 1], BF16)
        oner = sb("oner", [1, 128], F32)
        qst = [sb(f"qst{i}", [128, 512], BF16) for i in range(4)]
        if ctx is not None:
            ppool = ctx["pf32"]
            p = ctx["p"]
        else:
            ppool = [stk.enter_context(nc.psum_tensor(f"pp{i}", [128, 512], F32)) for i in range(8)]
            block = stk.enter_context(nc.Block())
            p = Prog(nc)
            p.make_sems()
        pool = PsumPool(p, ppool)

        p.emit("vector", lambda v: v.memset(onec[:, :], 1.0))
        c_ones = p.emit("vector", lambda v: v.memset(oner[:, :], 1.0), inc=True)
        padc = 0 if own0 == 0 else ownW + 1
        zp = []
        for kc in range(NKC):
            zp.append(p.emit("vector", lambda v, kc=kc: v.memset(
                xn1[:, kc * stride + padc: kc * stride + padc + 1], 0.0), inc=(kc == NKC - 1)))
        pad_tok = zp[-1]
        p.emit("gpsimd", lambda g: g.dma_start(out=cbb[:, :], in_=cb_in), inc=True, dma="tab")
        tab_all = p.chan_total("tab")
        for kc in range(NKC):
            p.emit("gpsimd", lambda g, kc=kc: g.dma_start(
                out=xn1[:, kc * stride + lo: kc * stride + lo + ln_cnt],
                in_=x1_in[kc][:, x1_lo: x1_lo + ln_cnt]), inc=True, dma="xin")
        xin_all = p.chan_total("xin")

        ln_toks = emit_ln(p, pool, xn1, stride, ln_cnt, am, bm, sq_scr, rowbuf,
                          onec, oner, xin_all, col0=lo, ones_tok=c_ones)
        ln_done = ln_toks[-1]

        for kc in range(NKC):
            p.emit("sync", lambda s, kc=kc: s.dma_start(
                out=xn1_out[kc][:, 0:ownW], in_=xn1[:, kc * stride + 1: kc * stride + 1 + ownW]),
                waits=[ln_toks[kc]], inc=True, dma="xno")

        wl_tok = {}
        mm_last = {}
        st_tok = [None] * 4
        slot = 0
        for oc in range(24):
            wb = cwb[oc % 2]
            wl = p.emit("gpsimd", lambda g, oc=oc, wb=wb: g.dma_start(out=wb[:, :], in_=cw_in[oc]),
                        waits=[mm_last.get(oc - 2)], inc=True, dma="cw%d" % (oc % 2))
            last = None
            for (t0, tw) in tiles_of(ownW):
                bi, ps, wtok = pool.acquire(1)[0]
                for w in range(3):
                    for kc in range(NKC):
                        first = (w == 0 and kc == 0)
                        last = p.emit("tensor", lambda t, ps=ps, w=w, kc=kc, t0=t0, tw=tw, wb=wb:
                                      t.matmul(ps[:, 0:tw], wb[:, (w * NKC + kc) * 128:(w * NKC + kc + 1) * 128],
                                               xn1[:, kc * stride + t0 + w: kc * stride + t0 + w + tw],
                                               start=(w == 0 and kc == 0), stop=(w == 2 and kc == NKC - 1)),
                                      waits=[wl, ln_done, pad_tok, wtok] if first else (),
                                      inc=(w == 2 and kc == NKC - 1))
                sl = slot % 4
                ev = p.emit("scalar", lambda s, ps=ps, oc=oc, tw=tw, o=qst[sl]:
                            s.add(o[:, 0:tw], ps[:, 0:tw], cbb[:, oc:oc + 1]),
                            waits=[last, tab_all, st_tok[sl]], inc=True)
                pool.release(bi, ev)
                dt = p.emit("sync", lambda s, oc=oc, t0=t0, tw=tw, o=qst[sl]:
                            s.dma_start(out=qkv1_out[oc][:, t0:t0 + tw], in_=o[:, 0:tw]),
                            waits=[ev], inc=True, dma="qo%d" % sl)
                st_tok[sl] = dt
                slot += 1
            mm_last[oc] = last
        if ctx is None:
            p.finalize(block, final_waits=[p.chan_total("xno")] + [p.chan_total("qo%d" % i) for i in range(4)])
    return nc


def build_s4b(parity, stop_after=None, skip=(), ctx=None):
    P = PARAMS[parity]
    ownW = P["ownW"]
    NB = ownW // 128
    if ctx is not None:
        nc = ctx["nc"]
        _D = lambda name, shape, dt, kind: ctx["d"][name]
    else:
        nc = bass.Bass("TRN2", target_bir_lowering=False)
        _D = lambda name, shape, dt, kind: nc.dram_tensor(name, shape, dt, kind=kind).ap()

    qkv1_in = _D("qkv1", [24, 128, ownW], BF16, "ExternalInput")
    wq_in = _D("wq_img", [8, 128, 1024], F32, "ExternalInput")
    wk_in = _D("wk_img", [8, 128, 1024], F32, "ExternalInput")
    wv_in = _D("wv_img", [8, 128, 1024], F32, "ExternalInput")
    wo_in = _D("wo_img", [8, 128, 1024], F32, "ExternalInput")
    bq_in = _D("bq", [128, 8], F32, "ExternalInput")
    bk_in = _D("bk", [128, 8], F32, "ExternalInput")
    bv_in = _D("bv_col", [128, 8], F32, "ExternalInput")
    bo_in = _D("bo", [128, 8], F32, "ExternalInput")
    id_in = _D("ident", [128, 128], BF16, "ExternalInput")
    o2_out = _D("o2", [8, 128, ownW], BF16, "ExternalOutput")

    ctxsfx = "_s4b"
    from contextlib import ExitStack
    with ExitStack() as stk:
        if ctx is not None:
            def sb(name, shape, dt):
                if name not in ctx["bufs"]:
                    ctx["bufs"][name] = ctx["stk"].enter_context(nc.sbuf_tensor(name + ctxsfx, shape, dt))
                return ctx["bufs"][name]
        else:
            sb = lambda name, shape, dt: stk.enter_context(nc.sbuf_tensor(name, shape, dt))
        qkv1 = sb("qkv1b", [128, 24 * 1152], BF16)
        whb = [sb("whb0", [128, 1024], BF16), sb("whb1", [128, 1024], BF16)]
        wob = sb("wob", [128, 8 * 1024], BF16)
        qh = sb("qh", [128, 8 * 1152], BF16)
        kh = sb("kh", [128, 8 * 1152], BF16)
        vh = sb("vh", [128, 8 * 1152], BF16)
        ofm = sb("ofm", [128, 8 * 1152], BF16)
        vtmb = sb("vtmb", [128, 1024], BF16)
        expq = sb("expq", [128, 512], BF16)
        r4 = sb("r4", [1, 512], F32)
        rmq = sb("rmq", [64, 512], BF16)
        wobias = sb("wobias", [128, 8], F32)
        bqb = sb("bqb", [128, 8], F32)
        bkb = sb("bkb", [128, 8], F32)
        bvb = sb("bvb", [128, 8], BF16)
        bob = sb("bob", [128, 8], F32)
        idb = sb("idb", [128, 128], BF16)
        onec = sb("onec", [128, 1], BF16)
        oner = sb("oner", [1, 128], F32)
        ost = [sb(f"ost{i}", [128, 512], BF16) for i in range(4)]
        if ctx is not None:
            pool6 = ctx["pf32"][0:6]
            ptrb = ctx["pbf"]
            p = ctx["p"]
        else:
            pool6 = [stk.enter_context(nc.psum_tensor(f"pp{i}", [128, 512], F32)) for i in range(6)]
            ptrb = [stk.enter_context(nc.psum_tensor(f"pt{i}", [128, 1024], BF16)) for i in range(2)]
            block = stk.enter_context(nc.Block())
            p = Prog(nc)
            p.make_sems()
        pool = PsumPool(p, pool6)

        p.emit("vector", lambda v: v.memset(onec[:, :], 1.0))
        c_ones = p.emit("vector", lambda v: v.memset(oner[:, :], 1.0), inc=True)
        for (dst, src) in [(bqb, bq_in), (bkb, bk_in), (bvb, bv_in), (bob, bo_in), (idb, id_in)]:
            p.emit("gpsimd", lambda g, dst=dst, src=src: g.dma_start(out=dst[:, :], in_=src),
                   inc=True, dma="tab")
        for oc in range(8):
            p.emit("gpsimd", lambda g, oc=oc: g.dma_start(out=wob[:, oc * 1024:(oc + 1) * 1024], in_=wo_in[oc]),
                   inc=True, dma="tab")
        tab_all = p.chan_total("tab")
        for ch in range(24):
            p.emit("sync", lambda s, ch=ch: s.dma_start(out=qkv1[:, ch * ownW:(ch + 1) * ownW], in_=qkv1_in[ch][:, 0:ownW]),
                   inc=True, dma="qin")
        qin_all = p.chan_total("qin")

        wb_toks = []
        for oc in range(8):
            bi, ps, wtok = pool.acquire(1)[0]
            last = None
            for kc in range(8):
                last = p.emit("tensor", lambda t, ps=ps, oc=oc, kc=kc:
                              t.matmul(ps[:, 0:1], wob[:, oc * 1024 + kc * 128: oc * 1024 + (kc + 1) * 128],
                                       bvb[:, kc:kc + 1], start=(kc == 0), stop=(kc == 7)),
                              waits=[tab_all, wtok] if kc == 0 else (), inc=(kc == 7))
            ev = p.emit("vector", lambda v, ps=ps, oc=oc: v.tensor_copy(wobias[:, oc:oc + 1], ps[:, 0:1]),
                        waits=[last], inc=True)
            pool.release(bi, ev)
            wb_toks.append(ev)
        wob_tok = p.emit("vector", lambda v: v.tensor_add(wobias[:, :], wobias[:, :], bob[:, :]),
                         waits=[tab_all], inc=True)

        st_tok = [None] * 4
        proj_done_s = None
        proj_done_v = None
        gi = 0
        wl_prev = {}
        for (kind, img, dstbuf, bias) in [(0, wq_in, qh, bqb), (1, wk_in, kh, bkb), (2, wv_in, vh, None)]:
            for oc in range(8):
                wb = whb[gi % 2]
                wl = p.emit("gpsimd", lambda g, img=img, oc=oc, wb=wb: g.dma_start(out=wb[:, :], in_=img[oc]),
                            waits=[wl_prev.get(gi - 2)], inc=True, dma="wh%d" % (gi % 2))
                last = None
                for (t0, tw) in tiles_of(ownW):
                    bi, ps, wtok = pool.acquire(1)[0]
                    for kc in range(8):
                        last = p.emit("tensor", lambda t, ps=ps, kc=kc, t0=t0, tw=tw, wb=wb, kind=kind:
                                      t.matmul(ps[:, 0:tw], wb[:, kc * 128:(kc + 1) * 128],
                                               qkv1[:, (kind * 8 + kc) * ownW + t0:(kind * 8 + kc) * ownW + t0 + tw],
                                               start=(kc == 0), stop=(kc == 7)),
                                      waits=[wl, qin_all, wtok] if kc == 0 else (), inc=(kc == 7))
                    for hh in range(2):
                        h = 2 * oc + hh
                        rb = 64 * (h // 8)
                        cb = (h % 8) * ownW
                        if bias is not None:
                            ev = p.emit("scalar", lambda s, ps=ps, hh=hh, oc=oc, t0=t0, tw=tw, dstbuf=dstbuf,
                                        bias=bias, rb=rb, cb=cb:
                                        s.add(dstbuf[rb:rb + 64, cb + t0: cb + t0 + tw],
                                              ps[64 * hh:64 * hh + 64, 0:tw], bias[64 * hh:64 * hh + 64, oc:oc + 1]),
                                        waits=[last, tab_all], inc=True)
                            proj_done_s = ev
                        else:
                            ev = p.emit("vector", lambda v, ps=ps, hh=hh, oc=oc, t0=t0, tw=tw, dstbuf=dstbuf,
                                        rb=rb, cb=cb:
                                        v.tensor_copy(dstbuf[rb:rb + 64, cb + t0: cb + t0 + tw],
                                                      ps[64 * hh:64 * hh + 64, 0:tw]),
                                        waits=[last], inc=True)
                            proj_done_v = ev
                    pool.release(bi, ev)
                wl_prev[gi] = last
                gi += 1
        if stop_after == "proj" and ctx is None:
            dbg = p.emit("sync", lambda s: s.dma_start(out=o2_out[0], in_=qh[:, 0:ownW]), waits=[proj_done_s, proj_done_v], inc=True, dma="oo0")
            p.finalize(block, final_waits=[p.chan_total("oo0")])
            return nc
        tr_tok = [None, None]
        prev_quad_t = None
        prev_quad_g = None
        prev_b_pv = None
        prev_rml = None
        for b in range(NB):
            vt_evs = []
            for h in range(16):
                bk_ = h % 2
                rb = 64 * (h // 8)
                cb = (h % 8) * ownW
                ttok = p.emit("tensor", lambda t, h=h, b=b, rb=rb, cb=cb, o=ptrb[bk_][:, 0:64]:
                              t.transpose(o, vh[rb:rb + 64, cb + b * 128: cb + b * 128 + 128], idb[rb:rb + 64, rb:rb + 64]),
                              waits=[proj_done_v, tab_all, tr_tok[bk_], prev_b_pv], inc=True)
                ev = p.emit("vector", lambda v, h=h, i=ptrb[bk_][:, 0:64]:
                            v.tensor_copy(vtmb[:, h * 64:(h + 1) * 64], i), waits=[ttok], inc=True)
                tr_tok[bk_] = ev
                vt_evs.append(ev)
            vt_done = vt_evs[-1]
            if stop_after == "trans" and b == 0 and ctx is None:
                dbg = p.emit("sync", lambda s: s.dma_start(out=o2_out[0][:, 0:1024], in_=vtmb[:, :]), waits=[vt_done], inc=True, dma="oo0")
                p.finalize(block, final_waits=[p.chan_total("oo0")])
                return nc
            for qd in range(4):
                bi, psc, wtok = pool.acquire(1)[0]
                mms = None
                rbq = 64 * (qd // 2)
                for j in range(4):
                    h = 4 * qd + j
                    cb = (h % 8) * ownW
                    mms = p.emit("tensor", lambda t, ps=psc, j=j, cb=cb, rbq=rbq, b=b:
                                 t.matmul(ps[:, j * 128:(j + 1) * 128],
                                          kh[rbq:rbq + 64, cb + b * 128: cb + b * 128 + 128],
                                          qh[rbq:rbq + 64, cb + b * 128: cb + b * 128 + 128],
                                          start=True, stop=True),
                                 waits=[proj_done_s, proj_done_v, wtok] if j == 0 else (), inc=(j == 3))
                etok = p.emit("scalar", lambda s, ps=psc: s.activation(expq[:, :], ps[:, 0:512], AF.Exp, scale=0.125),
                              waits=[mms, prev_quad_t, prev_quad_g], inc=True)
                rot_rel_sc = etok
                if "mask" not in skip:
                    mtok = p.emit("gpsimd", lambda g: g.affine_select(expq[:, :], expq[:, :],
                                                                      [[0, 4], [1, 128]], ALU.is_ge, 0.0,
                                                                      base=0, channel_multiplier=-1),
                                  waits=[etok], inc=True)
                    prev_quad_g = mtok
                else:
                    mtok = etok
                bi2, pss, wtok2 = pool.acquire(1)[0]
                smm = p.emit("tensor", lambda t, ps=pss: t.matmul(ps[0:1, 0:512], onec[:, 0:1], expq[:, :],
                                                                  start=True, stop=True),
                             waits=[mtok, wtok2, c_ones], inc=True)
                rtok = p.emit("vector", lambda v, ps=pss: v.reciprocal(r4[0:1, :], ps[0:1, 0:512]),
                              waits=[smm, prev_rml], inc=True)
                pool.release(bi2, rtok)
                bi3, pso, wtok3 = pool.acquire(1)[0]
                pvl = None
                for j in range(4):
                    h = 4 * qd + j
                    pvl = p.emit("tensor", lambda t, ps=pso, j=j, h=h:
                                 t.matmul(ps[0:64, j * 128:(j + 1) * 128],
                                          vtmb[:, h * 64: h * 64 + 64],
                                          expq[:, j * 128:(j + 1) * 128], start=True, stop=True),
                                 waits=[vt_done, mtok, wtok3] if j == 0 else (), inc=(j == 3))
                pool.release(bi, etok)
                bi4, psr, wtok4 = pool.acquire(1)[0]
                rml = None
                for j in ([] if "rm" in skip else range(4)):
                    rml = p.emit("tensor", lambda t, ps=psr, j=j:
                                 t.matmul(ps[0:64, j * 128:(j + 1) * 128],
                                          oner[0:1, 0:64], r4[0:1, j * 128:(j + 1) * 128],
                                          start=True, stop=True),
                                 waits=[rtok, wtok4] if j == 0 else (), inc=(j == 3))
                if "rm" in skip:
                    rml = rtok
                    erm = p.emit("vector", lambda v: v.memset(rmq[:, :], 1.0), waits=[rtok], inc=True)
                else:
                    erm = p.emit("vector", lambda v, ps=psr: v.tensor_copy(rmq[0:64, :], ps[0:64, 0:512]), waits=[rml], inc=True)
                prev_rml = rml
                pool.release(bi4, erm)
                ev2 = None
                for j in range(4):
                    h = 4 * qd + j
                    rb = 64 * (h % 2)
                    cb = (h // 2) * ownW
                    ev2 = p.emit("vector", lambda v, ps=pso, j=j, rb=rb, cb=cb, b=b:
                                 v.tensor_mul(ofm[rb:rb + 64, cb + b * 128: cb + b * 128 + 128],
                                              ps[0:64, j * 128:(j + 1) * 128], rmq[0:64, j * 128:(j + 1) * 128]),
                                 waits=[pvl, erm], inc=(j == 3))
                pool.release(bi3, ev2)
                prev_quad_t = pvl
                prev_b_pv = pvl
                ofm_tok = ev2
        if stop_after == "attn" and ctx is None:
            dbg = p.emit("sync", lambda s: s.dma_start(out=o2_out[0], in_=ofm[:, 0:ownW]), waits=[ofm_tok], inc=True, dma="oo0")
            p.finalize(block, final_waits=[p.chan_total("oo0")])
            return nc
        st_tok2 = [None] * 4
        slot = 0
        out_chans = []
        for oc in range(8):
            last = None
            for (t0, tw) in tiles_of(ownW):
                bi, ps, wtok = pool.acquire(1)[0]
                for kc in range(8):
                    last = p.emit("tensor", lambda t, ps=ps, oc=oc, kc=kc, t0=t0, tw=tw:
                                  t.matmul(ps[:, 0:tw], wob[:, oc * 1024 + kc * 128: oc * 1024 + (kc + 1) * 128],
                                           ofm[:, kc * ownW + t0: kc * ownW + t0 + tw],
                                           start=(kc == 0), stop=(kc == 7)),
                                  waits=[ofm_tok, wob_tok, wtok] if kc == 0 else (), inc=(kc == 7))
                sl = slot % 4
                ev = p.emit("scalar", lambda s, ps=ps, oc=oc, tw=tw, o=ost[sl]:
                            s.add(o[:, 0:tw], ps[:, 0:tw], wobias[:, oc:oc + 1]),
                            waits=[last, st_tok2[sl], wob_tok], inc=True)
                pool.release(bi, ev)
                dt = p.emit("sync", lambda s, oc=oc, t0=t0, tw=tw, o=ost[sl]:
                            s.dma_start(out=o2_out[oc][:, t0:t0 + tw], in_=o[:, 0:tw]),
                            waits=[ev], inc=True, dma="oo%d" % sl)
                st_tok2[sl] = dt
                slot += 1
        if ctx is None:
            p.finalize(block, final_waits=[p.chan_total("oo%d" % i) for i in range(4)])
    return nc


SORT8_NET = [(0, 1), (2, 3), (4, 5), (6, 7),
             (0, 2), (1, 3), (4, 6), (5, 7),
             (1, 2), (5, 6), (0, 4), (3, 7),
             (1, 5), (2, 6), (1, 4), (3, 6),
             (2, 4), (3, 5), (3, 4)]


def build_s5(parity, ctx=None):
    P = PARAMS[parity]
    q0, qW, own0, ownW = P["q0"], P["qW"], P["own0"], P["ownW"]
    NT = ownW // 128
    x1_own = own0 - q0
    if ctx is not None:
        nc = ctx["nc"]
        _D = lambda name, shape, dt, kind: ctx["d"][name]
    else:
        nc = bass.Bass("TRN2", target_bir_lowering=False)
        _D = lambda name, shape, dt, kind: nc.dram_tensor(name, shape, dt, kind=kind).ap()

    x1_in = _D("x1", [NKC, 128, qW], F32, "ExternalInput")
    xn_in = _D("xn1_own", [NKC, 128, ownW], BF16, "ExternalInput")
    o2_in = _D("o2", [8, 128, ownW], BF16, "ExternalInput")
    wb_in = _D("wbase_img", [32, 128, DM], F32, "ExternalInput")
    pj_in = _D("projw_img", [32, 128, 1024], F32, "ExternalInput")
    wr_in = _D("wr_img", [128, NKC * 8], F32, "ExternalInput")
    br_in = _D("br", [8, 1], F32, "ExternalInput")
    A_in = _D("A_img", [128, DM], F32, "ExternalInput")
    B_in = _D("B_sb", [128, DM], F32, "ExternalInput")
    rep_in = _D("rep", [8, 128], F32, "ExternalInput")
    bb_in = _D("bbase", [128, 32], F32, "ExternalInput")
    pb_in = _D("projb", [128, 32], F32, "ExternalInput")
    idf_in = _D("identf", [128, 128], F32, "ExternalInput")
    out_d = _D("outp", [NKC, 128, ownW], F32, "ExternalOutput")

    ctxsfx = "_s5"
    from contextlib import ExitStack
    with ExitStack() as stk:
        if ctx is not None:
            def sb(name, shape, dt):
                if name not in ctx["bufs"]:
                    ctx["bufs"][name] = ctx["stk"].enter_context(nc.sbuf_tensor(name + ctxsfx, shape, dt))
                return ctx["bufs"][name]
        else:
            sb = lambda name, shape, dt: stk.enter_context(nc.sbuf_tensor(name, shape, dt))
        xnb = sb("xnb", [128, NKC * 1152], BF16)
        o2b = sb("o2b", [128, 8 * 1152], BF16)
        wbb = [sb("wbb0", [128, DM], BF16), sb("wbb1", [128, DM], BF16)]
        pjb = [sb("pjb0", [128, 1024], BF16), sb("pjb1", [128, 1024], BF16)]
        wrb = sb("wrb", [128, NKC * 8], BF16)
        brb = sb("brb", [8, 1], F32)
        ab = sb("ab", [128, DM], BF16)
        bb = sb("bb", [128, DM], BF16)
        repb = sb("repb", [8, 128], BF16)
        btot = sb("btot", [128, 32], F32)
        pbb = sb("pbb", [128, 32], F32)
        idf = sb("idf", [128, 128], F32)
        logits = sb("logits", [8, 2048], F32)
        ztm = sb("ztm", [128, 72], F32)
        zs = sb("zs", [128, 72], F32)
        zc = sb("zc", [128, 72], F32)
        tmp = sb("tmpv", [128, 9], F32)
        tau = sb("tau", [128, 9], F32)
        wsp = sb("wsp", [128, 72], F32)
        wem = sb("wem", [8, 1152], BF16)
        wrep = sb("wrep", [128, 1152], BF16)
        hsb = sb("hsb", [128, 1152], BF16)
        xts = [sb(f"xt{i}", [128, 512], F32) for i in range(4)]
        x1s = [sb(f"x1s{i}", [128, 512], F32) for i in range(4)]
        if ctx is not None:
            ppool = ctx["pf32"]
            p = ctx["p"]
        else:
            ppool = [stk.enter_context(nc.psum_tensor(f"pp{i}", [128, 512], F32)) for i in range(8)]
            block = stk.enter_context(nc.Block())
            p = Prog(nc)
            p.make_sems()
        pool = PsumPool(p, ppool)

        for (dst, src) in [(wrb, wr_in), (brb, br_in), (ab, A_in), (bb, B_in), (repb, rep_in),
                           (btot, bb_in), (pbb, pb_in), (idf, idf_in)]:
            p.emit("gpsimd", lambda g, dst=dst, src=src: g.dma_start(out=dst[:, :], in_=src),
                   inc=True, dma="tab")
        tab_all = p.chan_total("tab")
        bt_tok = p.emit("vector", lambda v: v.tensor_add(btot[:, :], btot[:, :], pbb[:, :]),
                        waits=[tab_all], inc=True)
        for kc in range(NKC):
            p.emit("sync", lambda s, kc=kc: s.dma_start(out=xnb[:, kc * ownW:(kc + 1) * ownW], in_=xn_in[kc][:, 0:ownW]),
                   inc=True, dma="xin")
        for ch in range(8):
            p.emit("sync", lambda s, ch=ch: s.dma_start(out=o2b[:, ch * ownW:(ch + 1) * ownW], in_=o2_in[ch][:, 0:ownW]),
                   inc=True, dma="xin")
        xin_all = p.chan_total("xin")

        lg_toks = []
        for (t0, tw) in tiles_of(ownW):
            bi, ps, wtok = pool.acquire(1)[0]
            last = None
            for kc in range(NKC):
                last = p.emit("tensor", lambda t, ps=ps, kc=kc, t0=t0, tw=tw:
                              t.matmul(ps[0:8, 0:tw], wrb[:, kc * 8:(kc + 1) * 8],
                                       xnb[:, kc * ownW + t0: kc * ownW + t0 + tw],
                                       start=(kc == 0), stop=(kc == NKC - 1)),
                              waits=[tab_all, xin_all, wtok] if kc == 0 else (), inc=(kc == NKC - 1))
            ev = p.emit("scalar", lambda s, ps=ps, t0=t0, tw=tw:
                        s.add(logits[0:8, t0:t0 + tw], ps[0:8, 0:tw], brb[0:8, 0:1]),
                        waits=[last], inc=True)
            pool.release(bi, ev)
            lg_toks.append(ev)
        lg_done = lg_toks[-1]

        tr_evs = []
        for tt in range(NT):
            bi, ps, wtok = pool.acquire(1)[0]
            ttok = p.emit("tensor", lambda t, ps=ps, tt=tt:
                          t.transpose(ps[0:128, 0:8], logits[0:8, tt * 128:(tt + 1) * 128], idf[0:8, 0:8]),
                          waits=[lg_done, tab_all, wtok], inc=True)
            ev = p.emit("vector", lambda v, ps=ps, tt=tt:
                        v.tensor_copy(ztm[:, tt * 8:(tt + 1) * 8], ps[0:128, 0:8]), waits=[ttok], inc=True)
            pool.release(bi, ev)
            tr_evs.append(ev)
        def esl(buf, e):
            return bass.AP(buf[:, :].tensor, buf[:, :].offset + e, [[buf[:, :].ap[0][0], 128], [8, NT]])
        cp = p.emit("vector", lambda v: v.tensor_copy(zs[:, :], ztm[:, :]), waits=[tr_evs[-1]])
        for (i, j) in SORT8_NET:
            p.emit("vector", lambda v, i=i, j=j: v.tensor_tensor(tmp[:, 0:NT], esl(zs, i), esl(zs, j), ALU.min))
            p.emit("vector", lambda v, i=i, j=j: v.tensor_tensor(esl(zs, i), esl(zs, i), esl(zs, j), ALU.max))
            p.emit("vector", lambda v, j=j: v.tensor_copy(esl(zs, j), tmp[:, 0:NT]))
        p.emit("vector", lambda v: v.tensor_copy(esl(zc, 0), esl(zs, 0)))
        for k in range(1, 8):
            p.emit("vector", lambda v, k=k: v.tensor_add(esl(zc, k), esl(zc, k - 1), esl(zs, k)))
        for k in range(8):
            p.emit("vector", lambda v, k=k: v.tensor_scalar(esl(wsp, k), esl(zc, k),
                                                            1.0 / (k + 1), -1.0 / (k + 1),
                                                            op0=ALU.mult, op1=ALU.add))
        p.emit("vector", lambda v: v.tensor_tensor(tau[:, 0:NT], esl(wsp, 0), esl(wsp, 1), ALU.max))
        for k in range(2, 8):
            p.emit("vector", lambda v, k=k: v.tensor_tensor(tau[:, 0:NT], tau[:, 0:NT], esl(wsp, k), ALU.max))
        for e in range(8):
            p.emit("vector", lambda v, e=e: v.tensor_tensor(esl(wsp, e), esl(ztm, e), tau[:, 0:NT], ALU.subtract))
        sp_done = p.emit("vector", lambda v: v.tensor_scalar_max(wsp[:, :], wsp[:, :], 0.0), inc=True)

        wt_evs = []
        for tt in range(NT):
            bi, ps, wtok = pool.acquire(1)[0]
            ttok = p.emit("tensor", lambda t, ps=ps, tt=tt:
                          t.transpose(ps[0:8, 0:128], wsp[:, tt * 8:(tt + 1) * 8], idf[:, :]),
                          waits=[sp_done, wtok], inc=True)
            ev = p.emit("vector", lambda v, ps=ps, tt=tt:
                        v.tensor_copy(wem[0:8, tt * 128:(tt + 1) * 128], ps[0:8, 0:128]), waits=[ttok], inc=True)
            pool.release(bi, ev)
            wt_evs.append(ev)
        wem_done = wt_evs[-1]

        h_toks = []
        for (t0, tw) in tiles_of(ownW):
            bi, ps, wtok = pool.acquire(1)[0]
            mr = p.emit("tensor", lambda t, ps=ps, t0=t0, tw=tw:
                        t.matmul(ps[:, 0:tw], repb[0:8, :], wem[0:8, t0:t0 + tw], start=True, stop=True),
                        waits=[wem_done, tab_all, wtok], inc=True)
            er = p.emit("vector", lambda v, ps=ps, t0=t0, tw=tw:
                        v.tensor_copy(wrep[:, t0:t0 + tw], ps[:, 0:tw]), waits=[mr], inc=True)
            pool.release(bi, er)
            bi2, ps2, wtok2 = pool.acquire(1)[0]
            last = None
            for kc in range(NKC):
                last = p.emit("tensor", lambda t, ps=ps2, kc=kc, t0=t0, tw=tw:
                              t.matmul(ps[:, 0:tw], ab[:, kc * 128:(kc + 1) * 128],
                                       xnb[:, kc * ownW + t0: kc * ownW + t0 + tw],
                                       start=(kc == 0), stop=(kc == NKC - 1)),
                              waits=[tab_all, wtok2] if kc == 0 else (), inc=(kc == NKC - 1))
            eh = p.emit("vector", lambda v, ps=ps2, t0=t0, tw=tw:
                        v.tensor_copy(hsb[:, t0:t0 + tw], ps[:, 0:tw]), waits=[last], inc=True)
            pool.release(bi2, eh)
            ht = p.emit("vector", lambda v, t0=t0, tw=tw:
                        v.tensor_mul(hsb[:, t0:t0 + tw], hsb[:, t0:t0 + tw], wrep[:, t0:t0 + tw]),
                        waits=[er], inc=True)
            h_toks.append(ht)
        h_done = h_toks[-1]

        mm_last = {}
        xt_tok = [None] * 4
        x1_tok = [None] * 4
        slot = 0
        for oc in range(32):
            wb = wbb[oc % 2]
            pj = pjb[oc % 2]
            prev = mm_last.get(oc - 2)
            wl = p.emit("gpsimd", lambda g, oc=oc, wb=wb: g.dma_start(out=wb[:, :], in_=wb_in[oc]),
                        waits=[prev], inc=True, dma="wb%d" % (oc % 2))
            pl = p.emit("gpsimd", lambda g, oc=oc, pj=pj: g.dma_start(out=pj[:, :], in_=pj_in[oc]),
                        waits=[prev], inc=True, dma="pj%d" % (oc % 2))
            last = None
            for (t0, tw) in tiles_of(ownW):
                bi, ps, wtok = pool.acquire(1)[0]
                sl_x = slot % 4
                xt = p.emit("sync", lambda s, oc=oc, t0=t0, tw=tw, b=xts[sl_x]:
                            s.dma_start(out=b[:, 0:tw], in_=x1_in[oc][:, x1_own + t0: x1_own + t0 + tw]),
                            waits=[xt_tok[sl_x]], inc=True, dma="xt%d" % sl_x)
                for kc in range(NKC):
                    last = p.emit("tensor", lambda t, ps=ps, kc=kc, t0=t0, tw=tw, wb=wb:
                                  t.matmul(ps[:, 0:tw], wb[:, kc * 128:(kc + 1) * 128],
                                           xnb[:, kc * ownW + t0: kc * ownW + t0 + tw],
                                           start=(kc == 0), stop=False),
                                  waits=[wl, wtok] if kc == 0 else ())
                p.emit("tensor", lambda t, ps=ps, oc=oc, t0=t0, tw=tw:
                       t.matmul(ps[:, 0:tw], bb[:, oc * 128:(oc + 1) * 128],
                                hsb[:, t0:t0 + tw], start=False, stop=False),
                       waits=[h_done, tab_all])
                for kc in range(8):
                    last = p.emit("tensor", lambda t, ps=ps, kc=kc, t0=t0, tw=tw, pj=pj:
                                  t.matmul(ps[:, 0:tw], pj[:, kc * 128:(kc + 1) * 128],
                                           o2b[:, kc * ownW + t0: kc * ownW + t0 + tw],
                                           start=False, stop=(kc == 7)),
                                  waits=[pl] if kc == 0 else (), inc=(kc == 7))
                ev = p.emit("vector", lambda v, ps=ps, oc=oc, tw=tw, b=xts[sl_x], o=x1s[sl_x]:
                            v.scalar_tensor_tensor(o[:, 0:tw], ps[:, 0:tw], btot[:, oc:oc + 1],
                                                   b[:, 0:tw], op0=ALU.add, op1=ALU.add),
                            waits=[last, xt, x1_tok[sl_x], bt_tok], inc=True)
                pool.release(bi, ev)
                xt_tok[sl_x] = ev
                dt = p.emit("sync", lambda s, oc=oc, t0=t0, tw=tw, o=x1s[sl_x]:
                            s.dma_start(out=out_d[oc][:, t0:t0 + tw], in_=o[:, 0:tw]),
                            waits=[ev], inc=True, dma="oo%d" % sl_x)
                x1_tok[sl_x] = dt
                slot += 1
            mm_last[oc] = last
        if ctx is None:
            p.finalize(block, final_waits=[p.chan_total("oo%d" % i) for i in range(4)])
    return nc



def kernel_unfused(**inputs):
    g, cores = host_prep(inputs)
    ident_bf = np.eye(128, dtype=np.float32).astype(BF)
    identf = np.eye(128, dtype=np.float32)
    res = {c: {} for c in range(8)}

    def launch(builder, par, make_in):
        ids = [c for c in range(8) if c % 2 == par]
        nc = builder(par)
        in_maps = [make_in(c) for c in ids]
        r = run_bass_kernel_spmd(nc, in_maps, core_ids=ids)
        for i, c in enumerate(ids):
            res[c].update(r.results[i])

    for par in (0, 1):
        launch(build_s1, par, lambda c: dict(
            xfm_kv=cores[c]["xfm_kv"], wqkv_img=g["wqkv_img"], wte_img=g["wte_img"],
            cos=cores[c]["cos"], ssin=cores[c]["ssin"]))
    for par in (0, 1):
        launch(build_s2, par, lambda c: dict(
            q_out=res[c]["q_out"], k_out=res[c]["k_out"], v_out=res[c]["v_out"],
            gating=g["gating"], ident=ident_bf))
    for par in (0, 1):
        launch(build_s3, par, lambda c: dict(
            y_out=res[c]["y_out"], xn_own=res[c]["xn_own"], w_proj_img=g["w_proj_img"],
            qw_img=g["qw_img"], s_rep=g["s_rep"], z16=g["z16"], s16=g["s16"],
            gimg=g["gimg"], bias0=g["bias0"], xfm_q=cores[c]["xfm_q"]))
    for par in (0, 1):
        launch(build_s4a, par, lambda c: dict(
            x1=res[c]["x1"], conv_img=g["conv_img"], conv_b=g["conv_b"]))
    for par in (0, 1):
        launch(build_s4b, par, lambda c: dict(
            qkv1=res[c]["qkv1"], wq_img=g["wq_img"], wk_img=g["wk_img"], wv_img=g["wv_img"],
            wo_img=g["wo_img"], bq=g["bq"], bk=g["bk"], bv_col=g["bv_col"], bo=g["bo"],
            ident=ident_bf))
    for par in (0, 1):
        launch(build_s5, par, lambda c: dict(
            x1=res[c]["x1"], xn1_own=res[c]["xn1_own"], o2=res[c]["o2"],
            wbase_img=g["wbase_img"], projw_img=g["projw_img"], wr_img=g["wr_img"], br=g["br"],
            A_img=g["A_img"], B_sb=g["B_sb"], rep=g["rep"], bbase=g["bbase"], projb=g["projb"],
            identf=identf))

    out = np.empty((B_SEQ, T_SEQ, DM), np.float32)
    for c in range(8):
        P = PARAMS[c % 2]
        seq = c // 2
        o = np.asarray(res[c]["outp"], np.float32).reshape(DM, P["ownW"])
        out[seq, P["own0"]:P["own0"] + P["ownW"], :] = o.T
    return out



def _emit_ops(eng, ename, p):
    sems = p.sems
    for (fn, wl, tok) in p.ops[ename]:
        for (sk, v) in wl:
            eng.wait_ge(sems[sk], v)
        ins = fn(eng)
        if tok is not None:
            ins.then_inc(sems[tok[0]], 16 if tok[0].startswith("dma_") else 1)
    for k, v in p.counts.items():
        if v > 0:
            eng.wait_ge(sems[k], v)


def build_fused():
    nc = bass.Bass("TRN2", target_bir_lowering=False)
    d = {}

    def DI(name, shape, dt):
        d[name] = nc.dram_tensor(name, shape, dt, kind="ExternalInput").ap()

    def DN(name, shape, dt):
        d[name] = nc.dram_tensor(name, shape, dt).ap()

    def DO(name, shape, dt):
        d[name] = nc.dram_tensor(name, shape, dt, kind="ExternalOutput").ap()

    DI("par", [1, 1], I32)
    DI("xfm_kv", [DM, 2048], F32)
    DI("wqkv_img", [96, 128, DM], F32)
    DI("wte_img", [128, NKC * 10], F32)
    DI("cos", [128, 2048], F32)
    DI("ssin", [128, 2048], F32)
    DI("gating", [1, 1], F32)
    DI("ident", [128, 128], BF16)
    DI("identf", [128, 128], F32)
    DI("w_proj_img", [32, 128, DM], F32)
    DI("qw_img", [32, 128, DM], I32)
    DI("s_rep", [32, 128, DM], F32)
    DI("z16", [16, DM], F32)
    DI("s16", [16, DM], F32)
    DI("gimg", [128, NKC * 16], F32)
    DI("bias0", [128, 32], F32)
    DI("xfm_q", [DM, 1280], F32)
    DI("conv_img", [24, 128, 3 * DM], F32)
    DI("conv_b", [128, 24], F32)
    DI("wq_img", [8, 128, 1024], F32)
    DI("wk_img", [8, 128, 1024], F32)
    DI("wv_img", [8, 128, 1024], F32)
    DI("wo_img", [8, 128, 1024], F32)
    DI("bq", [128, 8], F32)
    DI("bk", [128, 8], F32)
    DI("bv_col", [128, 8], F32)
    DI("bo", [128, 8], F32)
    DI("wbase_img", [32, 128, DM], F32)
    DI("projw_img", [32, 128, 1024], F32)
    DI("wr_img", [128, NKC * 8], F32)
    DI("br", [8, 1], F32)
    DI("A_img", [128, DM], F32)
    DI("B_sb", [128, DM], F32)
    DI("rep", [8, 128], F32)
    DI("bbase", [128, 32], F32)
    DI("projb", [128, 32], F32)
    DN("q_out", [32, 128, 1280], BF16)
    DN("k_out", [32, 128, 2058], BF16)
    DN("v_out", [32, 128, 2058], BF16)
    DN("xn_own", [NKC, 128, 1280], BF16)
    DN("y_out", [32, 128, 1280], BF16)
    DN("x1", [NKC, 128, 1280], F32)
    DN("qkv1", [24, 128, 1152], BF16)
    DN("xn1_own", [NKC, 128, 1152], BF16)
    DN("o2", [8, 128, 1152], BF16)
    DO("outp", [NKC, 128, 1152], F32)

    from contextlib import ExitStack
    with ExitStack() as master:
        parb = master.enter_context(nc.sbuf_tensor("parb", [1, 1], I32))
        par_sem = master.enter_context(nc.semaphore("parsem"))
        pf32 = [master.enter_context(nc.psum_tensor(f"gf{i}", [128, 512], F32)) for i in range(6)]
        pbf = [master.enter_context(nc.psum_tensor(f"gb{i}", [128, 1024], BF16)) for i in range(2)]
        progs = {0: Prog(nc, tag="f"), 1: None}
        progs[1] = Prog(nc, tag="f", shared_sems=progs[0].sems)
        progs[0].make_sems()
        progs[1].make_sems()
        stages = [(build_s1, [1, 0]), (build_s2, [1, 0]), (build_s3, [1, 0]),
                  (build_s4a, [0, 1]), (build_s4b, [0, 1]), (build_s5, [0, 1])]
        for (fn, order) in stages:
            with ExitStack() as stage_stk:
                bufs = {}
                for par in order:
                    ctx = dict(nc=nc, d=d, bufs=bufs, stk=stage_stk, pf32=pf32, pbf=pbf, p=progs[par])
                    fn(par, ctx=ctx)
                    progs[par].barrier()
        block = master.enter_context(nc.Block())
        for ename in ENGINES:
            def mk(ename):
                def body(eng):
                    if ename == "sync":
                        eng.dma_start(out=parb[:, :], in_=d["par"]).then_inc(par_sem, 16)
                    eng.wait_ge(par_sem, 16)
                    with eng.register("preg_" + ename) as preg:
                        eng.reg_load(preg, parb[0:1, 0:1])
                        with eng.If_eq(preg, 0):
                            _emit_ops(eng, ename, progs[0])
                        with eng.Else():
                            _emit_ops(eng, ename, progs[1])
                return body
            getattr(block, ename)(mk(ename))
    return nc


def kernel(**inputs):
    g, cores = host_prep(inputs)
    ident_bf = np.ascontiguousarray(np.eye(128, dtype=np.float32).astype(BF))
    identf = np.eye(128, dtype=np.float32)
    nc = build_fused()
    in_maps = []
    for c in range(8):
        P = PARAMS[c % 2]
        dcr = cores[c]
        xkv = np.zeros((DM, 2048), np.float32)
        xkv[:, :P["kvW"]] = dcr["xfm_kv"]
        xq = np.zeros((DM, 1280), np.float32)
        xq[:, :P["qW"]] = dcr["xfm_q"]
        in_maps.append(dict(
            par=np.array([[c % 2]], np.int32), xfm_kv=xkv, xfm_q=xq,
            cos=g["cos_fm"], ssin=g["ssin_fm"],
            wqkv_img=g["wqkv_img"], wte_img=g["wte_img"], gating=g["gating"],
            ident=ident_bf, identf=identf,
            w_proj_img=g["w_proj_img"], qw_img=g["qw_img"], s_rep=g["s_rep"],
            z16=g["z16"], s16=g["s16"], gimg=g["gimg"], bias0=g["bias0"],
            conv_img=g["conv_img"], conv_b=g["conv_b"],
            wq_img=g["wq_img"], wk_img=g["wk_img"], wv_img=g["wv_img"], wo_img=g["wo_img"],
            bq=g["bq"], bk=g["bk"], bv_col=g["bv_col"], bo=g["bo"],
            wbase_img=g["wbase_img"], projw_img=g["projw_img"], wr_img=g["wr_img"], br=g["br"],
            A_img=g["A_img"], B_sb=g["B_sb"], rep=g["rep"], bbase=g["bbase"], projb=g["projb"]))
    res = run_bass_kernel_spmd(nc, in_maps, core_ids=list(range(8)))
    out = np.empty((B_SEQ, T_SEQ, DM), np.float32)
    for c in range(8):
        P = PARAMS[c % 2]
        seq = c // 2
        o = np.asarray(res.results[c]["outp"], np.float32)[:, :, :P["ownW"]].reshape(DM, P["ownW"]) if False else \
            np.asarray(res.results[c]["outp"], np.float32)[:, :, :P["ownW"]]
        o = o.reshape(NKC * 128, P["ownW"])
        out[seq, P["own0"]:P["own0"] + P["ownW"], :] = o.T
    return out
